# version 12
# baseline (speedup 1.0000x reference)
import numpy as np
import ml_dtypes
import concourse.bass as bass
import concourse.mybir as mybir
from concourse.bass_utils import run_bass_kernel_spmd
from concourse.bass_core_rust import IndirectOffsetOnAxis

bfloat16 = ml_dtypes.bfloat16
F32 = mybir.dt.float32
BF16 = mybir.dt.bfloat16
U32 = mybir.dt.uint32
I32 = mybir.dt.int32
AF = mybir.ActivationFunctionType
ALU = mybir.AluOpType
AXX = mybir.AxisListType.X

B, S, H = 2, 4096, 4096
O_Q, O_K = 4096, 1024
WR = 1536
INTER = 11008
F2 = 22016
NT = 8
TPC = 1024
TB = 4096
GU_NSLOT = 22
DN_NSLOT = 11
GU_FS = GU_NSLOT * 128
GU_STARTS = [0, 22, 44, 66, 88, 109, 130, 151]
GU_COUNTS = [22, 22, 22, 22, 21, 21, 21, 21]
DN_STARTS = [0, 11, 22, 33, 44, 55, 66, 76]
DN_COUNTS = [11, 11, 11, 11, 11, 11, 10, 10]
EXP_SCALE = float(1.0 / np.sqrt(128.0))
EXP_BIAS = -12.0
LN_EPS = 1e-5
INV_H = 1.0 / 4096.0


def _gu_block_rs(g):
    for r in range(8):
        if GU_STARTS[r] <= g < GU_STARTS[r] + GU_COUNTS[r]:
            return r, g - GU_STARTS[r]
    raise AssertionError(g)


SB = {}


def _sb(name, off, shape, dt):
    SB[name] = (off, shape, dt)
    nb = {F32: 4, BF16: 2, U32: 4, I32: 4}[dt]
    return off + ((shape[1] * nb + 31) // 32) * 32


o = 0
o = _sb("c_f32", o, [128, 16], F32)
o = _sb("c_shift", o, [128, 1], U32)
o = _sb("c_id", o, [128, 128], BF16)
o = _sb("c_ones_col", o, [128, 1], BF16)
o = _sb("c_ones_row", o, [1, 128], BF16)
o = _sb("c_masks", o, [128, 2048], BF16)
o = _sb("c_iota32", o, [128, 32], F32)
o = _sb("c_iota86", o, [128, 86], F32)
o = _sb("gidx_gu", o, [128, 32], I32)
o = _sb("gidx_dn", o, [128, 86], I32)
o = _sb("gidx_dnmy", o, [128, DN_NSLOT], I32)
o = _sb("gidx_gu_f", o, [128, 32], F32)
o = _sb("gidx_dn_f", o, [128, 86], F32)
o = _sb("ind32", o, [128, 1024], BF16)
o = _sb("ind_d", o, [128, 172], BF16)
o = _sb("sums", o, [128, 64], F32)
o = _sb("sumsq", o, [128, 64], F32)
o = _sb("redt", o, [128, 4], F32)
o = _sb("rstd", o, [128, 1], F32)
o = _sb("lnb", o, [128, 1], F32)
o = _sb("rden", o, [1, 1024], BF16)
o = _sb("bcb", o, [128, 512], F32)
o = _sb("S_gu", o, [32, 1024], BF16)
o = _sb("S_dn", o, [86, 1024], BF16)
assert o <= 19456, o
_sb("kT", 19456, [128, 8192], BF16)
_sb("v", 35840, [128, 8192], BF16)
_sb("gu0", 19456, [128, 4096], BF16)
_sb("gu1", 27648, [128, 4096], BF16)
_sb("gu2", 35840, [128, 4096], BF16)
_sb("gu3", 44032, [128, 4096], BF16)
_sb("r3tt", 19456, [128, 4096], BF16)
_sb("outb", 27648, [128, 4096], F32)
_sb("ld32_0", 52224, [128, 4096], F32)
_sb("ld32_1", 68608, [128, 4096], F32)
_sb("castb_0", 84992, [128, 4096], BF16)
_sb("castb_1", 93184, [128, 4096], BF16)
_sb("tstage_0", 101376, [128, 4096], BF16)
_sb("tstage_1", 109568, [128, 4096], BF16)
_sb("drep", 117760, [128, 4096], U32)
_sb("dgath", 134144, [128, 4096], F32)
_sb("dnib", 150528, [128, 4096], U32)
_sb("dqout", 166912, [128, 4096], BF16)
_sb("xt_0", 52224, [128, 16384], BF16)
_sb("xt_1", 84992, [128, 16384], BF16)
_sb("wq_0", 175104, [128, 4096], BF16)
_sb("wq_1", 183296, [128, 4096], BF16)
_sb("vw", 191488, [128, 8192], BF16)
_sb("qstage_0", 207872, [128, 512], BF16)
_sb("qstage_1", 208896, [128, 512], BF16)
_sb("atst_0", 139264, [128, 4096], BF16)
_sb("atst_1", 147456, [128, 4096], BF16)
_sb("qt_0", 117760, [128, 4096], BF16)
_sb("qt_1", 125952, [128, 4096], BF16)
_sb("probs_0", 134144, [128, 512], BF16)
_sb("probs_1", 135168, [128, 512], BF16)
_sb("probs_2", 136192, [128, 512], BF16)
_sb("bcb2", 137216, [128, 512], F32)
_sb("at3_0", 54272, [128, 8192], BF16)
_sb("at3_1", 70656, [128, 8192], BF16)
_sb("wd3_0", 179200, [128, 4096], BF16)
_sb("wd3_1", 187392, [128, 4096], BF16)
_sb("dstage_0", 195584, [128, 512], BF16)
_sb("dstage_1", 197632, [128, 512], BF16)
_sb("x2T", 87040, [128, 32768], BF16)
_sb("r_tt", 152576, [128, 4096], BF16)
_sb("rstile_0", 160768, [128, 512], BF16)
_sb("rstile_1", 162816, [128, 512], BF16)
_sb("xres_0", 164864, [128, 512], F32)
_sb("xres_1", 166912, [128, 512], F32)
_sb("x2sb", 168960, [128, 4096], BF16)
_sb("sq_junk", 177152, [128, 512], F32)
_sb("znegg", 152576, [32, 22016], BF16)
_sb("znegd", 196608, [86, 4096], BF16)
_sb("zqz", 70656, [86, 256], U32)
_sb("zpl", 54272, [86, 2048], U32)
_sb("zsc", 62464, [86, 2048], F32)
_sb("hT4_0", 204800, [128, 1024], BF16)
_sb("hT4_1", 206848, [128, 1024], BF16)
_sb("g_sb", 208896, [128, 512], BF16)
_sb("u_sb", 209920, [128, 512], BF16)
_sb("wd5", 19456, [128, 44032], BF16)
_sb("hT5_0", 107520, [128, 11008], BF16)
_sb("hT5_1", 129536, [128, 11008], BF16)
_sb("x2t_0", 152576, [128, 512], BF16)
_sb("x2t_1", 153600, [128, 512], BF16)
_sb("r3st_0", 154624, [128, 512], BF16)
_sb("r3st_1", 155648, [128, 512], BF16)

SBUF_BASE = 16384
for name, (off, shape, dt) in SB.items():
    nb = {F32: 4, BF16: 2, U32: 4, I32: 4}[dt]
    assert SBUF_BASE + off + shape[1] * nb <= 229376, (name, off + shape[1] * nb)

SEM_NAMES = (
    ["gc", "gdq", "gdw", "cc", "dzn", "dr3",
     "sq", "spp", "sx2", "sr3", "sout", "dxT",
     "vp", "vdq", "ap", "tp", "vs", "as_"]
    + [f"{s}{p}" for s in ("dx", "dxt", "dwq", "dqt", "dwd", "drs", "dgu",
                           "dwdn", "dht", "dx2", "sxw", "sww", "sht",
                           "sat", "dat") for p in (0, 1)]
)


class Emitter:
    def __init__(self, eng_name, E, T, sems):
        self.eng = eng_name
        self.E = E
        self.T = T
        self.sems = sems
        self.cnt = {k: 0 for k in sems}

    def evsync(self, eng, fn, waits=(), inc=None):
        ssem = {"vector": "vs", "scalar": "as_"}[eng]
        self.cnt[ssem] += 1
        v = self.cnt[ssem]
        if inc is not None:
            self.cnt[inc[0]] += inc[1]
        if eng != self.eng:
            return
        E = self.E
        for s, val in waits:
            if val > 0:
                E.wait_ge(self.sems[s], val)
        ins = fn(E)
        ins.then_inc(self.sems[ssem], 1)
        E.wait_ge(self.sems[ssem], v)
        if inc is not None:
            E.sem_inc(self.sems[inc[0]], inc[1])

    def ev(self, eng, fn=None, waits=(), inc=None):
        if inc is not None:
            self.cnt[inc[0]] += inc[1]
        if eng != self.eng:
            return
        E = self.E
        for s, v in waits:
            if v > 0:
                E.wait_ge(self.sems[s], v)
        if fn is not None:
            ins = fn(E)
            if inc is not None and ins is not None:
                ins.then_inc(self.sems[inc[0]], inc[1])
        elif inc is not None:
            E.sem_inc(self.sems[inc[0]], inc[1])


def build_schedule(em: Emitter):
    T = em.T
    sb = T["sb"]
    ps = T["ps"]
    C = em.cnt

    const_loads = [
        ("c_f32", "in_cf32"), ("c_shift", "in_cshift"), ("c_id", "in_cid"),
        ("c_ones_col", "in_cones_col"), ("c_ones_row", "in_cones_row"),
        ("c_masks", "in_cmasks"), ("c_iota32", "in_ciota32"), ("c_iota86", "in_ciota86"),
        ("gidx_gu", "in_gidx_gu"), ("gidx_dn", "in_gidx_dn"), ("gidx_dnmy", "in_gidx_dnmy"),
    ]
    for name, key in const_loads:
        em.ev("gpsimd", lambda E, name=name, key=key: E.dma_start(
            out=sb[name][:, :], in_=T[key]), inc=("gc", 16))
    GC = C["gc"]
    em.ev("scalar", None, waits=[("gc", GC)])
    em.ev("vector", None, waits=[("gc", GC)])

    def transpose_pipeline(n_tiles, width, src_fn, store_fn, store_sem):
        cast_v = PIPE["cast_v"]
        tile_t = PIPE["tile_t"]
        evac_v = PIPE["evac_v"]
        store = PIPE["store"]
        ngr = width // 1024
        for i_ in range(n_tiles):
            i = PIPE["n"]
            PIPE["n"] += 1
            ldb = f"ld32_{i % 2}"
            cbb = f"castb_{i % 2}"
            stg = f"tstage_{i % 2}"
            w = [("vp", cast_v[i - 2])] if i - 2 in cast_v else []
            dxs = f"dx{i % 2}"
            em.ev("sync", lambda E, i_=i_, ldb=ldb: E.dma_start(
                out=sb[ldb][:, 0:width], in_=src_fn(i_)), waits=w, inc=(dxs, 16))
            dxn = C[dxs]
            w = [(dxs, dxn)]
            if i - 2 in tile_t:
                w.append(("tp", tile_t[i - 2]))
            em.ev("vector", lambda E, ldb=ldb, cbb=cbb: E.tensor_copy(
                sb[cbb][:, 0:width], sb[ldb][:, 0:width]), waits=w, inc=("vp", 1))
            cast_v[i] = C["vp"]
            for g in range(ngr):
                gg = PIPE["grp"]
                PIPE["grp"] += 1
                tb = f"T{gg % 2}"
                w = [("vp", cast_v[i]), ("gc", GC)]
                if gg - 2 in evac_v:
                    w.append(("vp", evac_v[gg - 2]))
                for k in range(8):
                    ht = 8 * g + k
                    em.ev("tensor", lambda E, cbb=cbb, ht=ht, k=k, tb=tb: E.transpose(
                        ps[tb][:, 128 * k:128 * (k + 1)],
                        sb[cbb][:, 128 * ht:128 * (ht + 1)], sb["c_id"][:, :]),
                        waits=w if k == 0 else [],
                        inc=("tp", 1) if k == 7 else None)
                w = [("tp", C["tp"])]
                if i - 2 in store:
                    w.append(store[i - 2])
                em.ev("vector", lambda E, tb=tb, stg=stg, g=g: E.tensor_copy(
                    sb[stg][:, 1024 * g:1024 * (g + 1)], ps[tb][:, :]),
                    waits=w, inc=("vp", 1))
                evac_v[gg] = C["vp"]
            tile_t[i] = C["tp"]
            ssem = f"{store_sem}{i % 2}"
            store_fn(i_, stg, evac_v[PIPE["grp"] - 1], ssem)
            store[i] = (ssem, C[ssem])

    PIPE = {"n": 0, "cast_v": {}, "tile_t": {}, "evac_v": {}, "store": {}, "grp": 0}
    xcast_v = {}
    xst = {}
    for i in range(8):
        ldb = f"ld32_{i % 2}"
        cbb = f"castb_{i % 2}"
        w = [("vp", xcast_v[i - 2])] if i - 2 in xcast_v else []
        dxs = f"dx{i % 2}"
        em.ev("sync", lambda E, i=i, ldb=ldb: E.dma_start(
            out=sb[ldb][:, :], in_=T["in_xseq"][128 * i:128 * (i + 1), :]), waits=w, inc=(dxs, 16))
        dxn = C[dxs]
        w = [(dxs, dxn)]
        if i - 2 in xst:
            w.append(xst[i - 2])
        em.ev("vector", lambda E, ldb=ldb, cbb=cbb: E.tensor_copy(
            sb[cbb][:, :], sb[ldb][:, :]), waits=w, inc=("vp", 1))
        xcast_v[i] = C["vp"]
        ssem = f"sxw{i % 2}"
        em.ev("scalar", lambda E, i=i, cbb=cbb: E.dma_start(
            out=T["xbf_in"][128 * i:128 * (i + 1), :], in_=sb[cbb][:, :]),
            waits=[("vp", xcast_v[i])], inc=(ssem, 16))
        xst[i] = (ssem, C[ssem])
    em.ev("gpsimd", lambda E: E.collective_compute(
        "AllGather", ALU.bypass, replica_groups=[[0, 1, 2, 3], [4, 5, 6, 7]],
        ins=[T["xbf_in"].opt()], outs=[T["xbf_all"].opt()]),
        waits=[("sxw0", C["sxw0"]), ("sxw1", C["sxw1"])], inc=("cc", 1))
    PIPE["cast_v"][-2] = xcast_v[6]
    PIPE["cast_v"][-1] = xcast_v[7]
    em.ev("vector", None, waits=[xst[6], xst[7]])

    def wq_store(i, stg, vv, ssem):
        em.ev("scalar", lambda E, i=i, stg=stg: E.dma_start(
            out=T["wqT_priv"][128 * i:128 * (i + 1), :], in_=sb[stg][:, :]),
            waits=[("vp", vv)], inc=(ssem, 16))

    transpose_pipeline(12, 4096, lambda i: T["in_wqkv"][128 * i:128 * (i + 1), :], wq_store, "sww")
    SWW_WQ = [("sww0", C["sww0"]), ("sww1", C["sww1"])]

    def wd_store(i, stg, vv, ssem):
        em.ev("scalar", lambda E, i=i, stg=stg: E.dma_start(
            out=T["wdT_priv"].rearrange("(at p) c -> p at c", at=8)[:, :, 128 * i:128 * (i + 1)],
            in_=sb[stg][:, 0:1024].rearrange("p (at w) -> p at w", at=8)),
            waits=[("vp", vv)], inc=(ssem, 16))

    transpose_pipeline(32, 1024, lambda i: T["in_wdense"][128 * i:128 * (i + 1), :], wd_store, "sww")
    SWW_ALL = [("sww0", C["sww0"]), ("sww1", C["sww1"])]

    em.evsync("vector", lambda E: E.tensor_copy(sb["gidx_gu_f"][:, :], sb["gidx_gu"][:, :]))
    em.evsync("vector", lambda E: E.tensor_copy(sb["gidx_dn_f"][:, :], sb["gidx_dn"][:, :]))
    for ht in range(32):
        em.ev("vector", lambda E, ht=ht: E.tensor_scalar(
            out=sb["ind32"][:, 32 * ht:32 * (ht + 1)], in0=sb["c_iota32"][:, :],
            scalar1=sb["gidx_gu_f"][:, ht:ht + 1], scalar2=None, op0=ALU.is_equal))
    em.ev("vector", None, inc=("vp", 1))
    V_IND32 = C["vp"]

    dq_vec_jobs = []
    n_dq = 0
    for kind in ("gu", "dn"):
        n_tiles = 32 if kind == "gu" else DN_NSLOT
        w_ = GU_FS if kind == "gu" else 4096
        for ti in range(n_tiles):
            gw = [("gc", GC)]
            if n_dq >= 1:
                gw.append(("vdq", n_dq))
            if kind == "gu":
                src = T["in_qwgu"][16 * ti:16 * (ti + 1), :].rearrange(
                    "r (o n) -> r o n", o=1).broadcast_to([16, 8, w_])
                tbl, idx = T["in_scgu"], sb["gidx_gu"][:, ti:ti + 1]
            else:
                src = T["in_qwdn"][16 * ti:16 * (ti + 1), :].rearrange(
                    "r (o n) -> r o n", o=1).broadcast_to([16, 8, w_])
                tbl, idx = T["in_scdn"], sb["gidx_dnmy"][:, ti:ti + 1]
            em.ev("gpsimd", lambda E, src=src, w_=w_: E.dma_start(
                out=sb["drep"][:, 0:w_], in_=src), waits=gw, inc=("gdq", 16))
            em.ev("gpsimd", lambda E, tbl=tbl, idx=idx, w_=w_: E.indirect_dma_start(
                out=sb["dgath"][:, 0:w_], out_offset=None, in_=tbl,
                in_offset=IndirectOffsetOnAxis(ap=idx, axis=0)), inc=("gdq", 16))
            gdqn = C["gdq"]
            vw_ = [("gdq", gdqn)]
            if n_dq >= 1:
                vw_.append(("gdw", 16 * n_dq))
            def dq_math(vw_=vw_, w_=w_):
                em.ev("vector", lambda E, w_=w_: E.tensor_scalar(
                    out=sb["dnib"][:, 0:w_], in0=sb["drep"][:, 0:w_],
                    scalar1=sb["c_shift"][:, :], scalar2=15,
                    op0=ALU.logical_shift_right, op1=ALU.bitwise_and), waits=vw_)
                em.ev("vector", lambda E, w_=w_: E.tensor_mul(
                    sb["dqout"][:, 0:w_], sb["dnib"][:, 0:w_], sb["dgath"][:, 0:w_]),
                    inc=("vdq", 1))
            dq_vec_jobs.append(dq_math)
            n_dq += 1
            if kind == "gu":
                em.ev("gpsimd", lambda E, ti=ti: E.dma_start(
                    out=T["gu_in"].rearrange("(s p) c -> p s c", s=GU_NSLOT)[:, :, 128 * ti:128 * (ti + 1)],
                    in_=sb["dqout"][:, 0:GU_FS].rearrange("p (s f) -> p s f", s=GU_NSLOT)),
                    waits=[("vdq", n_dq)], inc=("gdw", 16))
            else:
                em.ev("gpsimd", lambda E, ti=ti: E.dma_start(
                    out=T["dn_in"][128 * ti:128 * (ti + 1), :], in_=sb["dqout"][:, 0:4096]),
                    waits=[("vdq", n_dq)], inc=("gdw", 16))
    GDW_ALL = C["gdw"]
    em.ev("gpsimd", lambda E: E.collective_compute(
        "AllGather", ALU.bypass, replica_groups=[list(range(8))],
        ins=[T["gu_in"].opt()], outs=[T["gu_all"].opt()]),
        waits=[("gdw", GDW_ALL)], inc=("cc", 1))
    em.ev("gpsimd", lambda E: E.collective_compute(
        "AllGather", ALU.bypass, replica_groups=[list(range(8))],
        ins=[T["dn_in"].opt()], outs=[T["dn_all"].opt()]),
        inc=("cc", 1))

    for vb in range(2):
        em.ev("sync", lambda E, vb=vb: E.dma_start(
            out=sb["vw"].ap().rearrange("p (ht v) -> p ht v", v=256)[:, :, 128 * vb:128 * (vb + 1)],
            in_=T["wqT_priv"][128 * (10 + vb):128 * (11 + vb), :].rearrange("p (ht o) -> p ht o", ht=32)),
            waits=list(SWW_WQ), inc=(f"dwq{vb}", 16))
    VW_DONE = [("dwq0", C["dwq0"]), ("dwq1", C["dwq1"])]
    mm_done = {}
    ev_done = {}
    for tc in range(8):
        rkl = tc // 2
        colo = 512 * (tc % 2)
        xtb = f"xt_{tc % 2}"
        w = [("cc", 1)]
        if (tc - 2, 13) in mm_done:
            w.append(("tp", mm_done[(tc - 2, 13)]))
        dxts = f"dxt{tc % 2}"
        for ht in range(32):
            em.ev("sync", lambda E, tc=tc, ht=ht, xtb=xtb: E.dma_start_transpose(
                sb[xtb][:, 512 * ht:512 * (ht + 1)],
                T["xbf_all"][512 * tc:512 * (tc + 1), 128 * ht:128 * (ht + 1)]),
                waits=w if ht == 0 else [], inc=(dxts, 16))
        dxtn = (dxts, C[dxts])
        for ot in range(10):
            wqb = f"wq_{ot % 2}"
            w = list(SWW_WQ)
            prev = (tc, ot - 2) if ot >= 2 else ((tc - 1, 8 + ot) if tc >= 1 else None)
            if prev in mm_done:
                w.append(("tp", mm_done[prev]))
            dwqs = f"dwq{ot % 2}"
            em.ev("scalar", lambda E, ot=ot, wqb=wqb: E.dma_start(
                out=sb[wqb][:, :], in_=T["wqT_priv"][128 * ot:128 * (ot + 1), :]),
                waits=w, inc=(dwqs, 16))
            dwqn = (dwqs, C[dwqs])
            fb = f"F{ot % 2}"
            bw = [dxtn, dwqn]
            prev = (tc, ot - 2) if ot >= 2 else ((tc - 1, 8 + ot) if tc >= 1 else None)
            if prev in ev_done:
                bw.append(("vp", ev_done[prev]))
            for k in range(32):
                em.ev("tensor", lambda E, fb=fb, wqb=wqb, xtb=xtb, k=k: E.matmul(
                    ps[fb][:, :], sb[wqb][:, 128 * k:128 * (k + 1)],
                    sb[xtb][:, 512 * k:512 * (k + 1)], start=(k == 0), stop=(k == 31)),
                    waits=bw if k == 0 else [],
                    inc=("tp", 1) if k == 31 else None)
            mm_done[(tc, ot)] = C["tp"]
            if ot < 8:
                qsb = f"qstage_{ot % 2}"
                em.ev("vector", lambda E, fb=fb, qsb=qsb: E.tensor_copy(sb[qsb][:, :], ps[fb][:, :]),
                      waits=[("tp", mm_done[(tc, ot)]), ("sq", C["sq"])], inc=("vp", 1))
                ev_done[(tc, ot)] = C["vp"]
                em.ev("scalar", lambda E, qsb=qsb, ot=ot, tc=tc: E.dma_start(
                    out=T["qT_priv"][128 * ot:128 * (ot + 1), 512 * tc:512 * (tc + 1)],
                    in_=sb[qsb][:, :]),
                    waits=[("vp", ev_done[(tc, ot)])], inc=("sq", 16))
            else:
                u = ot - 8
                em.ev("vector", lambda E, fb=fb, u=u, tc=tc: E.tensor_copy(
                    sb["kT"][:, 4096 * u + 512 * tc:4096 * u + 512 * (tc + 1)], ps[fb][:, :]),
                    waits=[("tp", mm_done[(tc, ot)])], inc=("vp", 1))
                ev_done[(tc, ot)] = C["vp"]
        for vt in range(4):
            fb = f"F{2 + vt % 2}"
            bw = [dxtn] + VW_DONE
            prev = (tc, 10 + vt - 2) if vt >= 2 else ((tc - 1, 12 + vt) if tc >= 1 else None)
            if prev in ev_done:
                bw.append(("vp", ev_done[prev]))
            for k in range(32):
                em.ev("tensor", lambda E, fb=fb, xtb=xtb, vt=vt, k=k: E.matmul(
                    ps[fb][:, 0:256], sb[xtb][:, 512 * k + 128 * vt:512 * k + 128 * (vt + 1)],
                    sb["vw"][:, 256 * k:256 * (k + 1)], start=(k == 0), stop=(k == 31)),
                    waits=bw if k == 0 else [],
                    inc=("tp", 1) if k == 31 else None)
            mm_done[(tc, 10 + vt)] = C["tp"]
            ttb = 4 * tc + vt
            em.ev("vector", lambda E, fb=fb, ttb=ttb: E.tensor_copy(
                sb["v"][:, 256 * ttb:256 * (ttb + 1)], ps[fb][:, 0:256]),
                waits=[("tp", mm_done[(tc, 10 + vt)])], inc=("vp", 1))
            ev_done[(tc, 10 + vt)] = C["vp"]
        for _ in range(6):
            if dq_vec_jobs:
                dq_vec_jobs.pop(0)()
    while dq_vec_jobs:
        dq_vec_jobs.pop(0)()
    V_P1_END = C["vp"]
    SQ_ALL = C["sq"]

    strip_t = {}
    qt_loaded = {}
    expn = 0
    tail_tt = {}
    tail_rec = {}
    tail_cp = {}
    sc_pw = {}

    TL = []
    n = -1
    for u in range(2):
        for qh in range(4):
            sidx = 4 * u + qh
            for sqc in range(8):
                n += 1
                ntk = 4 * (sqc + 1)
                for tk in range(ntk):
                    TL.append(dict(n=n, u=u, sidx=sidx, sqc=sqc, tk=tk, ntk=ntk,
                                   par=n % 2, fa="F3" if n % 2 == 0 else "F4",
                                   dslc=(0, 1) if n % 2 == 0 else (32, 33)))

    def ensure_strip(sidx):
        if sidx in qt_loaded:
            return
        qtb = f"qt_{sidx % 2}"
        w = [("sq", SQ_ALL), ("gdw", GDW_ALL)]
        if sidx - 2 in strip_t:
            w.append(("tp", strip_t[sidx - 2]))
        dqts = f"dqt{sidx % 2}"
        em.ev("sync", lambda E, sidx=sidx, qtb=qtb: E.dma_start(
            out=sb[qtb][:, :], in_=T["qT_priv"][128 * sidx:128 * (sidx + 1), :]),
            waits=w, inc=(dqts, 16))
        qt_loaded[sidx] = (dqts, C[dqts])

    def emit_sc(gi):
        nonlocal expn
        t = TL[gi]
        ensure_strip(t["sidx"])
        fb = f"F{gi % 2}"
        pb = f"probs_{gi % 3}"
        qtb = f"qt_{t['sidx'] % 2}"
        sw = [qt_loaded[t["sidx"]], ("vp", V_P1_END)]
        if expn - 1 > 0:
            sw.append(("ap", expn - 1))
        em.ev("tensor", lambda E, fb=fb, t=t, qtb=qtb: E.matmul(
            ps[fb][:, :], sb["kT"][:, 4096 * t["u"] + 128 * t["tk"]:4096 * t["u"] + 128 * (t["tk"] + 1)],
            sb[qtb][:, 512 * t["sqc"]:512 * (t["sqc"] + 1)], start=True, stop=True),
            waits=sw, inc=("tp", 1))
        tsc = C["tp"]
        em.ev("scalar", lambda E, fb=fb, pb=pb: E.activation(
            out=sb[pb][:, :], in_=ps[fb][:, :], func=AF.Exp,
            bias=EXP_BIAS, scale=EXP_SCALE),
            waits=[("tp", tsc)], inc=("ap", 1))
        expn = C["ap"]
        if t["tk"] >= t["ntk"] - 4:
            dpos = t["tk"] - (t["ntk"] - 4)
            em.ev("vector", lambda E, pb=pb, dpos=dpos: E.tensor_mul(
                sb[pb][:, :], sb[pb][:, :], sb["c_masks"][:, 512 * dpos:512 * (dpos + 1)]),
                waits=[("ap", expn)], inc=("vp", 1))
            sc_pw[gi] = ("vp", C["vp"])
        else:
            sc_pw[gi] = ("ap", expn)

    def emit_avden(gi):
        t = TL[gi]
        n = t["n"]
        pb = f"probs_{gi % 3}"
        f_ = t["tk"] == 0
        l_ = t["tk"] == t["ntk"] - 1
        avw = [sc_pw.pop(gi)]
        if f_ and n - 2 in tail_tt:
            avw.append(("vp", tail_tt[n - 2]))
        em.ev("tensor", lambda E, t=t, pb=pb, f_=f_, l_=l_: E.matmul(
            ps[t["fa"]][:, :], sb["v"][:, 256 * t["tk"] + 128 * t["u"]:256 * t["tk"] + 128 * (t["u"] + 1)],
            sb[pb][:, :], start=f_, stop=l_),
            waits=avw, inc=("tp", 1))
        dw = []
        if f_ and n - 2 in tail_rec:
            dw.append(("vp", tail_rec[n - 2]))
        em.ev("tensor", lambda E, t=t, pb=pb, f_=f_, l_=l_: E.matmul(
            ps["F5"][t["dslc"][0]:t["dslc"][1], :], sb["c_ones_col"][:, :], sb[pb][:, :],
            start=f_, stop=l_,
            tile_position=(0, 32) if t["par"] == 1 else None),
            waits=dw, inc=("tp", 1))
        if not l_:
            return
        tden = C["tp"]
        par = t["par"]
        sidx = t["sidx"]
        sqc = t["sqc"]
        dslc = t["dslc"]
        fa = t["fa"]

        def _recip(E, dslc=dslc, par=par):
            with E.bass.allow_low_precision(reason="bf16 softmax denom"):
                return E.reciprocal(sb["rden"][0:1, 512 * par:512 * (par + 1)],
                                    ps["F5"][dslc[0]:dslc[1], :])
        em.evsync("vector", _recip, waits=[("tp", tden)], inc=("vp", 1))
        tail_rec[n] = C["vp"]
        bw = []
        if n - 1 in tail_cp:
            bw.append(("ap", tail_cp[n - 1]))
        em.ev("tensor", lambda E, par=par: E.matmul(
            ps["F2"][:, :], sb["c_ones_row"][:, :],
            sb["rden"][0:1, 512 * par:512 * (par + 1)], start=True, stop=True),
            waits=[("vp", tail_rec[n])] + bw, inc=("tp", 1))
        tbc = C["tp"]
        bcb = "bcb" if par == 0 else "bcb2"
        em.ev("scalar", lambda E, bcb=bcb: E.copy(sb[bcb][:, :], ps["F2"][:, :]),
              waits=[("tp", tbc)], inc=("ap", 1))
        tail_cp[n] = C["ap"]
        ats = f"atst_{sidx % 2}"
        sats = f"sat{sidx % 2}"
        ttw = [("ap", tail_cp[n])]
        if sqc == 0 and C[sats] > 0:
            ttw.append((sats, C[sats]))
        em.ev("vector", lambda E, ats=ats, sqc=sqc, fa=fa, bcb=bcb: E.tensor_mul(
            sb[ats][:, 512 * sqc:512 * (sqc + 1)], ps[fa][:, :], sb[bcb][:, :]),
            waits=ttw, inc=("vp", 1))
        tail_tt[n] = C["vp"]
        if sqc == 7:
            em.ev("scalar", lambda E, sidx=sidx, ats=ats: E.dma_start(
                out=T["attnT_priv"][128 * sidx:128 * (sidx + 1), :], in_=sb[ats][:, :]),
                waits=[("vp", tail_tt[n])], inc=(sats, 16))
            strip_t[sidx] = C["tp"]

    NTILES = len(TL)
    emit_sc(0)
    emit_sc(1)
    for gi in range(NTILES):
        if gi + 2 < NTILES:
            emit_sc(gi + 2)
        emit_avden(gi)
    V_P2_END = C["vp"]
    TP2_END = C["tp"]
    SAT_ALL = [("sat0", C["sat0"]), ("sat1", C["sat1"])]
    for nm in ("kT", "v", "attnT"):
        if nm in T.get("dbg", {}):
            em.ev("sync", lambda E, nm=nm: E.dma_start(out=T["dbg"][nm], in_=sb[nm][:, :]),
                  waits=[("vp", V_P2_END)], inc=("dr3", 16))
    if any(nm in T.get("dbg", {}) for nm in ("kT", "v", "attnT")):
        em.ev("sync", None, waits=[("dr3", C["dr3"])])

    tile_t3 = {}
    KEY3 = {}
    spp_g = {}
    nps = -1
    for g in range(4):
        datg = f"dat{g % 2}"
        aw = list(SAT_ALL)
        if g == 0:
            aw.append(("vp", V_P2_END))
        if g - 2 >= 0:
            aw.append(("vp", tile_t3[(g - 2, 7, 7)]))
        em.ev("sync", lambda E, g=g: E.dma_start(
            out=sb[f"at3_{g % 2}"].ap().rearrange("p (s c) -> p s c", s=8),
            in_=T["attnT_priv"].rearrange("(s p) c -> p s c", s=8)[:, :, 1024 * g:1024 * (g + 1)]),
            waits=aw, inc=(datg, 16))
        datn = (datg, C[datg])
        for hc in range(8):
            wdb = f"wd3_{hc % 2}"
            w = list(SWW_ALL)
            if g == 0 and hc == 0:
                w.append(("vp", V_P2_END))
            prev = (g, hc - 2) if hc >= 2 else ((g - 1, hc + 6) if g >= 1 else None)
            if prev is not None and (prev[0], prev[1], 7) in tile_t3:
                w.append(("vp", tile_t3[(prev[0], prev[1], 7)]))
            dwds = f"dwd{(8 * g + hc) % 2}"
            em.ev("sync", lambda E, hc=hc, wdb=wdb: E.dma_start(
                out=sb[wdb].ap().rearrange("p (at w) -> p at w", at=8),
                in_=T["wdT_priv"].rearrange("(at p) c -> p at c", at=8)[:, :, 512 * hc:512 * (hc + 1)]),
                waits=w, inc=(dwds, 16))
            dwdn_ = (dwds, C[dwds])
            for tl in range(8):
                nps += 1
                tt_b = 8 * g + tl
                fb = f"F{nps % 2}"
                bw = [dwdn_, datn]
                if nps >= 2:
                    bw.append(("vp", tile_t3[KEY3[nps - 2]]))
                KEY3[nps] = (g, hc, tl)
                for j in range(8):
                    em.ev("tensor", lambda E, fb=fb, j=j, tl=tl, wdb=wdb, g=g: E.matmul(
                        ps[fb][:, :], sb[f"at3_{g % 2}"][:, 1024 * j + 128 * tl:1024 * j + 128 * (tl + 1)],
                        sb[wdb][:, 512 * j:512 * (j + 1)], start=(j == 0), stop=(j == 7)),
                        waits=bw if j == 0 else [],
                        inc=("tp", 1) if j == 7 else None)
                tpn = C["tp"]
                dsb = f"dstage_{nps % 2}"
                em.ev("vector", lambda E, fb=fb, dsb=dsb: E.tensor_copy(sb[dsb][:, :], ps[fb][:, :]),
                      waits=[("tp", tpn), ("spp", C["spp"])], inc=("vp", 1))
                tile_t3[(g, hc, tl)] = C["vp"]
                em.ev("scalar", lambda E, dsb=dsb, tt_b=tt_b, hc=hc: E.dma_start(
                    out=T["partial_priv"][128 * tt_b:128 * (tt_b + 1), 512 * hc:512 * (hc + 1)],
                    in_=sb[dsb][:, :]),
                    waits=[("vp", tile_t3[(g, hc, tl)])], inc=("spp", 16))
        spp_g[g] = C["spp"]
        em.ev("gpsimd", lambda E, g=g: E.collective_compute(
            "ReduceScatter", ALU.add, replica_groups=[[0, 1, 2, 3], [4, 5, 6, 7]],
            ins=[T["partial_priv"][1024 * g:1024 * (g + 1), :].opt()],
            outs=[T["rs_out"][256 * g:256 * (g + 1), :].opt()]),
            waits=[("spp", spp_g[g])], inc=("cc", 1))
    TP3_END = C["tp"]

    ap_x2 = {}
    sx2_n = {}
    for tt in range(NT):
        g = tt // 2
        for hc in range(8):
            rsb = f"rstile_{hc % 2}"
            xrb = f"xres_{hc % 2}"
            w = [("cc", 4 + g)]
            if hc >= 2 or tt > 0:
                w.append(("vp", C["vp"]))
            drss = f"drs{hc % 2}"
            em.ev("sync", lambda E, tt=tt, hc=hc, rsb=rsb: E.dma_start(
                out=sb[rsb][:, :], in_=T["rs_out"][128 * tt:128 * (tt + 1), 512 * hc:512 * (hc + 1)]),
                waits=w, inc=(drss, 16))
            em.ev("sync", lambda E, tt=tt, hc=hc, xrb=xrb: E.dma_start(
                out=sb[xrb][:, :], in_=T["in_x"][128 * tt:128 * (tt + 1), 512 * hc:512 * (hc + 1)]),
                inc=(drss, 16))
            aw = [(drss, C[drss])]
            if hc == 0 and tt - 1 in ap_x2:
                aw.append(("ap", ap_x2[tt - 1]))
            em.ev("vector", lambda E, rsb=rsb, xrb=xrb, hc=hc: E.tensor_add(
                sb["r_tt"][:, 512 * hc:512 * (hc + 1)], sb[rsb][:, :], sb[xrb][:, :]), waits=aw)
            col = 8 * tt + hc
            em.evsync("vector", lambda E, hc=hc, col=col: E.reduce_sum(
                sb["sums"][:, col:col + 1], sb["r_tt"][:, 512 * hc:512 * (hc + 1)], axis=AXX),
                inc=("vp", 1))
            vn = C["vp"]
            em.evsync("scalar", lambda E, hc=hc, col=col: E.activation(
                out=sb["sq_junk"][:, :], in_=sb["r_tt"][:, 512 * hc:512 * (hc + 1)],
                func=AF.Square, accum_out=sb["sumsq"][:, col:col + 1]),
                waits=[("vp", vn)], inc=("ap", 1))
        apsq = C["ap"]
        em.evsync("vector", lambda E, tt=tt: E.reduce_sum(
            sb["redt"][:, 0:1], sb["sums"][:, 8 * tt:8 * (tt + 1)], axis=AXX),
            waits=[("ap", apsq)])
        em.evsync("vector", lambda E, tt=tt: E.reduce_sum(
            sb["redt"][:, 1:2], sb["sumsq"][:, 8 * tt:8 * (tt + 1)], axis=AXX))
        em.evsync("vector", lambda E: E.tensor_scalar_mul(sb["redt"][:, 2:3], sb["redt"][:, 0:1], INV_H))
        em.evsync("vector", lambda E: E.tensor_scalar_mul(sb["redt"][:, 1:2], sb["redt"][:, 1:2], INV_H))
        em.evsync("vector", lambda E: E.tensor_mul(sb["redt"][:, 0:1], sb["redt"][:, 2:3], sb["redt"][:, 2:3]))
        em.evsync("vector", lambda E: E.tensor_sub(sb["redt"][:, 1:2], sb["redt"][:, 1:2], sb["redt"][:, 0:1]),
              inc=("vp", 1))
        vvar = C["vp"]
        em.evsync("scalar", lambda E: E.activation(out=sb["redt"][:, 3:4], in_=sb["redt"][:, 1:2],
                                                   func=AF.Ln, bias=LN_EPS), waits=[("vp", vvar)])
        em.evsync("scalar", lambda E: E.activation(out=sb["rstd"][:, :], in_=sb["redt"][:, 3:4],
                                                   func=AF.Exp, scale=-0.5), inc=("ap", 1))
        aprs = C["ap"]
        em.evsync("vector", lambda E: E.tensor_scalar(
            out=sb["lnb"][:, :], in0=sb["redt"][:, 2:3], scalar1=sb["rstd"][:, :],
            scalar2=-1.0, op0=ALU.mult, op1=ALU.mult),
            waits=[("ap", aprs)], inc=("vp", 1))
        vlnb = C["vp"]
        aw = [("vp", vlnb)]
        if tt - 1 in sx2_n:
            aw.append(("sx2", sx2_n[tt - 1]))
        em.ev("scalar", lambda E: E.activation(
            out=sb["x2sb"][:, :], in_=sb["r_tt"][:, :], func=AF.Identity,
            scale=sb["rstd"][:, :], bias=sb["lnb"][:, :]), waits=aw, inc=("ap", 1))
        ap_x2[tt] = C["ap"]
        em.ev("scalar", lambda E, tt=tt: E.dma_start(
            out=T["x2_priv"][128 * tt:128 * (tt + 1), :], in_=sb["x2sb"][:, :]),
            waits=[("ap", ap_x2[tt])], inc=("sx2", 16))
        sx2_n[tt] = C["sx2"]
        for ht in range(32):
            em.ev("scalar", lambda E, ht=ht, tt=tt: E.dma_start_transpose(
                sb["x2T"][:, 1024 * ht + 128 * tt:1024 * ht + 128 * (tt + 1)],
                T["x2_priv"][128 * tt:128 * (tt + 1), 128 * ht:128 * (ht + 1)]),
                waits=[("sx2", sx2_n[tt])] if ht == 0 else [], inc=("dxT", 16))
    V_X2T = C["vp"]
    DXT_ALL = C["dxT"]
    SX2_ALL = C["sx2"]
    if "x2e" in T.get("dbg", {}):
        em.ev("sync", lambda E: E.dma_start(out=T["dbg"]["x2e"], in_=T["x2_priv"]),
              waits=[("sx2", SX2_ALL)], inc=("dr3", 16))
        em.ev("sync", None, waits=[("dr3", C["dr3"])])
    if "lnst" in T.get("dbg", {}):
        em.ev("sync", lambda E: E.dma_start(out=T["dbg"]["lnst"][:, 0:64], in_=sb["sums"][:, :]),
              waits=[("ap", C["ap"]), ("vp", C["vp"])], inc=("dr3", 16))
        em.ev("sync", lambda E: E.dma_start(out=T["dbg"]["lnst"][:, 64:128], in_=sb["sumsq"][:, :]),
              inc=("dr3", 16))
        em.ev("sync", lambda E: E.dma_start(out=T["dbg"]["lnst"][:, 128:132], in_=sb["redt"][:, :]),
              inc=("dr3", 16))
        def _d1(E):
            with E.bass.allow_non_contiguous_dma(reason="dbg"):
                return E.dma_start(out=T["dbg"]["lnst"][:, 132:133], in_=sb["rstd"][:, :])
        def _d2(E):
            with E.bass.allow_non_contiguous_dma(reason="dbg"):
                return E.dma_start(out=T["dbg"]["lnst"][:, 133:134], in_=sb["lnb"][:, :])
        em.ev("sync", _d1, inc=("dr3", 16))
        em.ev("sync", _d2, inc=("dr3", 16))
        em.ev("sync", None, waits=[("dr3", C["dr3"])])

    for tc2 in range(2):
        fb = "F4" if tc2 == 0 else "F5"
        for ht in range(32):
            em.ev("tensor", lambda E, fb=fb, ht=ht, tc2=tc2: E.matmul(
                ps[fb][0:32, :], sb["ind32"][:, 32 * ht:32 * (ht + 1)],
                sb["x2T"][:, 1024 * ht + 512 * tc2:1024 * ht + 512 * (tc2 + 1)],
                start=(ht == 0), stop=(ht == 31)),
                waits=[("dxT", DXT_ALL), ("vp", V_IND32)] if ht == 0 else [],
                inc=("tp", 1) if ht == 31 else None)
        tsg = C["tp"]
        em.ev("vector", lambda E, fb=fb, tc2=tc2: E.tensor_copy(
            sb["S_gu"][:, 512 * tc2:512 * (tc2 + 1)], ps[fb][0:32, :]),
            waits=[("tp", tsg)], inc=("vp", 1))
    V_SGU = C["vp"]
    ZG = [("ap", ap_x2[NT - 1]), ("sx2", C["sx2"]), ("tp", TP3_END)]
    for k in range(16):
        em.ev("sync", lambda E, k=k: E.dma_start(
            out=sb["zqz"][0:32, 0:172], in_=T["in_qzgu"][:, 172 * k:172 * (k + 1)]),
            waits=ZG + [("vp", C["vp"])], inc=("dzn", 16))
        em.ev("sync", lambda E, k=k: E.dma_start(
            out=sb["zsc"][0:32, 0:1376], in_=T["in_scgu_full"][:, 1376 * k:1376 * (k + 1)]),
            inc=("dzn", 16))
        dznn = C["dzn"]
        for j in range(8):
            fj = lambda E, j=j: E.tensor_scalar(
                out=sb["zpl"][0:32, 172 * j:172 * (j + 1)], in0=sb["zqz"][0:32, 0:172],
                scalar1=float(4 * j), scalar2=15,
                op0=ALU.logical_shift_right, op1=ALU.bitwise_and)
            if j == 7:
                em.evsync("vector", fj)
            else:
                em.ev("vector", fj, waits=(ZG + [("dzn", dznn)]) if (j == 0 and k == 0) else ([("dzn", dznn)] if j == 0 else []))
        em.evsync("vector", lambda E, k=k: E.scalar_tensor_tensor(
            out=sb["znegg"][0:32, 1376 * k:1376 * (k + 1)].rearrange("g (c j) -> g c j", j=8),
            in0=sb["zpl"][0:32, 0:1376].rearrange("g (j c) -> g c j", j=8),
            scalar=1.0, in1=sb["zsc"][0:32, 0:1376].rearrange("g (c j) -> g c j", j=8),
            op0=ALU.add, op1=ALU.mult))
        em.evsync("vector", lambda E, k=k: E.tensor_scalar_mul(
            sb["znegg"][0:32, 1376 * k:1376 * (k + 1)], sb["znegg"][0:32, 1376 * k:1376 * (k + 1)], -1.0),
            inc=("vp", 1))
    for half in range(2):
        em.ev("sync", lambda E, half=half: E.dma_start(
            out=sb["zqz"][0:86, 0:256], in_=T["in_qzdn"][:, 256 * half:256 * (half + 1)]),
            waits=ZG + [("vp", C["vp"])], inc=("dzn", 16))
        em.ev("sync", lambda E, half=half: E.dma_start(
            out=sb["zsc"][0:86, 0:2048], in_=T["in_scdn"][:, 2048 * half:2048 * (half + 1)]),
            inc=("dzn", 16))
        dznn = C["dzn"]
        for j in range(8):
            fj = lambda E, j=j: E.tensor_scalar(
                out=sb["zpl"][0:86, 256 * j:256 * (j + 1)], in0=sb["zqz"][0:86, 0:256],
                scalar1=float(4 * j), scalar2=15,
                op0=ALU.logical_shift_right, op1=ALU.bitwise_and)
            if j == 7:
                em.evsync("vector", fj)
            else:
                em.ev("vector", fj, waits=[("dzn", dznn)] if j == 0 else [])
        em.evsync("vector", lambda E, half=half: E.scalar_tensor_tensor(
            out=sb["znegd"][0:86, 2048 * half:2048 * (half + 1)].rearrange("g (c j) -> g c j", j=8),
            in0=sb["zpl"][0:86, 0:2048].rearrange("g (j c) -> g c j", j=8),
            scalar=1.0, in1=sb["zsc"][0:86, 0:2048].rearrange("g (c j) -> g c j", j=8),
            op0=ALU.add, op1=ALU.mult))
        em.evsync("vector", lambda E, half=half: E.tensor_scalar_mul(
            sb["znegd"][0:86, 2048 * half:2048 * (half + 1)],
            sb["znegd"][0:86, 2048 * half:2048 * (half + 1)], -1.0),
            inc=("vp", 1))
    V_ZNEG = C["vp"]

    blk_t = {}
    hT_v = {}
    sil_a = {}
    sht_n = {}
    ind_v = {}
    for i in range(86):
        gr, gs = _gu_block_rs(i)
        ur, us = _gu_block_rs(86 + i)
        grow = 128 * (GU_NSLOT * gr + gs)
        urow = 128 * (GU_NSLOT * ur + us)
        ga = f"gu{2 * (i % 2)}"
        ub = f"gu{2 * (i % 2) + 1}"
        w = [("cc", 2)]
        if i == 0:
            w.append(("tp", TP2_END))
            w.append(("sq", SQ_ALL))
        if i - 2 in blk_t:
            w.append(("tp", blk_t[i - 2]))
        dgus = f"dgu{i % 2}"
        em.ev("sync", lambda E, grow=grow, ga=ga: E.dma_start(
            out=sb[ga][:, :], in_=T["gu_all"][grow:grow + 128, :]), waits=w, inc=(dgus, 16))
        em.ev("sync", lambda E, urow=urow, ub=ub: E.dma_start(
            out=sb[ub][:, :], in_=T["gu_all"][urow:urow + 128, :]), inc=(dgus, 16))
        dgun = (dgus, C[dgus])
        iw = [("vp", V_ZNEG)]
        if i - 2 in blk_t:
            iw.append(("tp", blk_t[i - 2]))
        em.ev("vector", lambda E, i=i: E.tensor_scalar(
            out=sb["ind_d"][:, 86 * (i % 2):86 * (i % 2) + 86], in0=sb["c_iota86"][:, :],
            scalar1=sb["gidx_dn_f"][:, i:i + 1], scalar2=None, op0=ALU.is_equal),
            waits=iw, inc=("vp", 1))
        ind_v[i] = C["vp"]
        for tc2 in range(2):
            fg = f"F{i % 2}"
            fu = f"F{2 + i % 2}"
            bw = [dgun, ("vp", V_ZNEG), ("dxT", DXT_ALL)]
            if (i - 2, tc2) in hT_v:
                bw.append(("vp", hT_v[(i - 2, tc2)]))
            if (i - 2, tc2) in sil_a:
                bw.append(("ap", sil_a[(i - 2, tc2)]))
            for k in range(32):
                em.ev("tensor", lambda E, fg=fg, ga=ga, k=k, tc2=tc2: E.matmul(
                    ps[fg][:, :], sb[ga][:, 128 * k:128 * (k + 1)],
                    sb["x2T"][:, 1024 * k + 512 * tc2:1024 * k + 512 * (tc2 + 1)],
                    start=(k == 0), stop=False),
                    waits=bw if k == 0 else [])
            em.ev("tensor", lambda E, fg=fg, i=i, tc2=tc2: E.matmul(
                ps[fg][:, :], sb["znegg"][0:32, 128 * i:128 * (i + 1)],
                sb["S_gu"][:, 512 * tc2:512 * (tc2 + 1)], start=False, stop=True),
                waits=[("vp", V_SGU)], inc=("tp", 1))
            for k in range(32):
                em.ev("tensor", lambda E, fu=fu, ub=ub, k=k, tc2=tc2: E.matmul(
                    ps[fu][:, :], sb[ub][:, 128 * k:128 * (k + 1)],
                    sb["x2T"][:, 1024 * k + 512 * tc2:1024 * k + 512 * (tc2 + 1)],
                    start=(k == 0), stop=False))
            em.ev("tensor", lambda E, fu=fu, i=i, tc2=tc2: E.matmul(
                ps[fu][:, :], sb["znegg"][0:32, 128 * (86 + i):128 * (87 + i)],
                sb["S_gu"][:, 512 * tc2:512 * (tc2 + 1)], start=False, stop=True),
                inc=("tp", 1))
            tgu = C["tp"]
            sw = [("tp", tgu)]
            if i == 0 and tc2 == 0:
                sw.append(("sq", SQ_ALL))
            em.ev("scalar", lambda E, fg=fg: E.activation(
                out=sb["g_sb"][:, :], in_=ps[fg][:, :], func=AF.Silu),
                waits=sw, inc=("ap", 1))
            sil_a[(i, tc2)] = C["ap"]
            em.ev("vector", lambda E, fu=fu: E.tensor_copy(sb["u_sb"][:, :], ps[fu][:, :]),
                  waits=[("tp", tgu)] + ([("sq", SQ_ALL)] if i == 0 and tc2 == 0 else []))
            hb = f"hT4_{i % 2}"
            hw = [("ap", sil_a[(i, tc2)])]
            if i - 2 in sht_n:
                hw.append(sht_n[i - 2])
            em.ev("vector", lambda E, hb=hb, tc2=tc2: E.tensor_mul(
                sb[hb][:, 512 * tc2:512 * (tc2 + 1)], sb["g_sb"][:, :], sb["u_sb"][:, :]),
                waits=hw, inc=("vp", 1))
            hT_v[(i, tc2)] = C["vp"]
            fsd = "F4" if tc2 == 0 else "F5"
            em.ev("tensor", lambda E, fsd=fsd, i=i, hb=hb, tc2=tc2: E.matmul(
                ps[fsd][0:86, :], sb["ind_d"][:, 86 * (i % 2):86 * (i % 2) + 86],
                sb[hb][:, 512 * tc2:512 * (tc2 + 1)], start=(i == 0), stop=(i == 85)),
                waits=[("vp", hT_v[(i, tc2)]), ("vp", ind_v[i])] +
                      ([("vp", V_SGU)] if i == 0 else []),
                inc=("tp", 1))
        blk_t[i] = C["tp"]
        hb = f"hT4_{i % 2}"
        shts = f"sht{i % 2}"
        em.ev("scalar", lambda E, i=i, hb=hb: E.dma_start(
            out=T["hT_priv"].rearrange("(tt p) c -> p tt c", tt=8)[:, :, 128 * i:128 * (i + 1)],
            in_=sb[hb].ap().rearrange("p (tt t) -> p tt t", tt=8)),
            waits=[("vp", hT_v[(i, 1)])], inc=(shts, 16))
        sht_n[i] = (shts, C[shts])
    TP4_END = C["tp"]
    SHT_ALL = [("sht0", C["sht0"]), ("sht1", C["sht1"])]
    for tc2 in range(2):
        fsd = "F4" if tc2 == 0 else "F5"
        em.ev("vector", lambda E, fsd=fsd, tc2=tc2: E.tensor_copy(
            sb["S_dn"][:, 512 * tc2:512 * (tc2 + 1)], ps[fsd][0:86, :]),
            waits=[("tp", TP4_END)], inc=("vp", 1))
    V_SDN = C["vp"]

    ch5_t = {}
    tile_t5 = []
    ev5 = {}
    v_p5 = C["vp"]
    add_v = {}
    for hc in range(8):
        w = [("cc", 3), ("vp", V_SDN)] + list(SHT_ALL)
        if hc - 1 in ch5_t:
            w.append(("tp", ch5_t[hc - 1]))
        dwdns = f"dwdn{hc % 2}"
        em.ev("sync", lambda E, hc=hc: E.dma_start(
            out=sb["wd5"][:, 0:76 * 512].rearrange("p (g w) -> p g w", g=76),
            in_=T["dn_all"].rearrange("(g p) c -> p g c", g=88)[:, 0:76, 512 * hc:512 * (hc + 1)]),
            waits=w, inc=(dwdns, 16))
        em.ev("sync", lambda E, hc=hc: E.dma_start(
            out=sb["wd5"][:, 76 * 512:86 * 512].rearrange("p (g w) -> p g w", g=10),
            in_=T["dn_all"].rearrange("(g p) c -> p g c", g=88)[:, 77:87, 512 * hc:512 * (hc + 1)]),
            inc=(dwdns, 16))
        dwdnn = (dwdns, C[dwdns])
        for tt in range(NT):
            n = 8 * hc + tt
            h5 = f"hT5_{tt % 2}"
            w = list(SHT_ALL)
            if n >= 2:
                w.append(("tp", tile_t5[n - 2]))
            dhts = f"dht{n % 2}"
            em.ev("sync", lambda E, tt=tt, h5=h5: E.dma_start(
                out=sb[h5][:, :], in_=T["hT_priv"][128 * tt:128 * (tt + 1), :]),
                waits=w, inc=(dhts, 16))
            x2b = f"x2t_{tt % 2}"
            dx2s = f"dx2{n % 2}"
            xw = [("sx2", SX2_ALL), ("vp", V_SDN)]
            if n - 2 in add_v:
                xw.append(("vp", add_v[n - 2]))
            em.ev("sync", lambda E, tt=tt, hc=hc, x2b=x2b: E.dma_start(
                out=sb[x2b][:, :], in_=T["x2_priv"][128 * tt:128 * (tt + 1), 512 * hc:512 * (hc + 1)]),
                waits=xw, inc=(dx2s, 16))
            dhtn = (dhts, C[dhts])
            dx2n = (dx2s, C[dx2s])
            fb = f"F{n % 2}"
            bw = [dwdnn, dhtn]
            if n - 2 in ev5:
                bw.append(("vp", ev5[n - 2]))
            for g in range(86):
                em.ev("tensor", lambda E, fb=fb, g=g, h5=h5: E.matmul(
                    ps[fb][:, :], sb[h5][:, 128 * g:128 * (g + 1)],
                    sb["wd5"][:, 512 * g:512 * (g + 1)], start=(g == 0), stop=False),
                    waits=bw if g == 0 else [])
            em.ev("tensor", lambda E, fb=fb, tt=tt, hc=hc: E.matmul(
                ps[fb][:, :], sb["S_dn"][:, 128 * tt:128 * (tt + 1)],
                sb["znegd"][0:86, 512 * hc:512 * (hc + 1)], start=False, stop=True),
                inc=("tp", 1))
            tile_t5.append(C["tp"])
            r3b = f"r3st_{tt % 2}"
            em.ev("vector", lambda E, fb=fb, r3b=r3b, x2b=x2b: E.tensor_add(
                sb[r3b][:, :], ps[fb][:, :], sb[x2b][:, :]),
                waits=[("tp", tile_t5[-1]), dx2n, ("sr3", C["sr3"])])
            add_v[n] = C["vp"] + 1
            col = 8 * tt + hc
            em.evsync("vector", lambda E, r3b=r3b, col=col: E.reduce_sum(
                sb["sums"][:, col:col + 1], sb[r3b][:, :], axis=AXX), inc=("vp", 1))
            v_p5 = C["vp"]
            ev5[n] = C["vp"]
            em.evsync("scalar", lambda E, r3b=r3b, col=col: E.activation(
                out=sb["sq_junk"][:, :], in_=sb[r3b][:, :], func=AF.Square,
                accum_out=sb["sumsq"][:, col:col + 1]),
                waits=[("vp", v_p5)], inc=("ap", 1))
            em.ev("scalar", lambda E, r3b=r3b, tt=tt, hc=hc: E.dma_start(
                out=T["r3_priv"][128 * tt:128 * (tt + 1), 512 * hc:512 * (hc + 1)],
                in_=sb[r3b][:, :]),
                inc=("sr3", 16))
        ch5_t[hc] = C["tp"]
    SR3_ALL = C["sr3"]
    TP5_END = C["tp"]

    sout_n = {}
    apout_n = {}
    for tt in range(NT):
        w = [("sr3", SR3_ALL)]
        if tt == 0:
            w.append(("tp", TP5_END))
        if tt - 1 in apout_n:
            w.append(("ap", apout_n[tt - 1]))
        em.ev("sync", lambda E, tt=tt: E.dma_start(
            out=sb["r3tt"][:, :], in_=T["r3_priv"][128 * tt:128 * (tt + 1), :]),
            waits=w, inc=("dr3", 16))
        dr3n = C["dr3"]
        em.evsync("vector", lambda E, tt=tt: E.reduce_sum(
            sb["redt"][:, 0:1], sb["sums"][:, 8 * tt:8 * (tt + 1)], axis=AXX),
            waits=[("dr3", dr3n), ("ap", C["ap"])])
        em.evsync("vector", lambda E, tt=tt: E.reduce_sum(
            sb["redt"][:, 1:2], sb["sumsq"][:, 8 * tt:8 * (tt + 1)], axis=AXX))
        em.evsync("vector", lambda E: E.tensor_scalar_mul(sb["redt"][:, 2:3], sb["redt"][:, 0:1], INV_H))
        em.evsync("vector", lambda E: E.tensor_scalar_mul(sb["redt"][:, 1:2], sb["redt"][:, 1:2], INV_H))
        em.evsync("vector", lambda E: E.tensor_mul(sb["redt"][:, 0:1], sb["redt"][:, 2:3], sb["redt"][:, 2:3]))
        em.evsync("vector", lambda E: E.tensor_sub(sb["redt"][:, 1:2], sb["redt"][:, 1:2], sb["redt"][:, 0:1]),
              inc=("vp", 1))
        vvar = C["vp"]
        em.evsync("scalar", lambda E: E.activation(out=sb["redt"][:, 3:4], in_=sb["redt"][:, 1:2],
                                                   func=AF.Ln, bias=LN_EPS), waits=[("vp", vvar)])
        em.evsync("scalar", lambda E: E.activation(out=sb["rstd"][:, :], in_=sb["redt"][:, 3:4],
                                                   func=AF.Exp, scale=-0.5), inc=("ap", 1))
        aprs = C["ap"]
        em.evsync("vector", lambda E: E.tensor_scalar(
            out=sb["lnb"][:, :], in0=sb["redt"][:, 2:3], scalar1=sb["rstd"][:, :],
            scalar2=-1.0, op0=ALU.mult, op1=ALU.mult),
            waits=[("ap", aprs)], inc=("vp", 1))
        vlnb = C["vp"]
        aw = [("vp", vlnb)]
        if tt - 1 in sout_n:
            aw.append(("sout", sout_n[tt - 1]))
        em.ev("scalar", lambda E: E.activation(
            out=sb["outb"][:, :], in_=sb["r3tt"][:, :], func=AF.Identity,
            scale=sb["rstd"][:, :], bias=sb["lnb"][:, :]), waits=aw, inc=("ap", 1))
        apout = C["ap"]
        apout_n[tt] = apout
        em.ev("scalar", lambda E, tt=tt: E.dma_start(
            out=T["out"][128 * tt:128 * (tt + 1), :], in_=sb["outb"][:, :]),
            waits=[("ap", apout)], inc=("sout", 16))
        sout_n[tt] = C["sout"]
    em.ev("sync", None, waits=[("sout", C["sout"])])
    for nm, dst in T.get("dbg", {}).items():
        if nm == "at3":
            em.ev("sync", lambda E, dst=dst: E.dma_start(out=dst, in_=sb["at3_1"][:, :]),
                  inc=("dr3", 16))
            continue
        if nm == "wd3c":
            em.ev("sync", lambda E, dst=dst: E.dma_start(out=dst, in_=sb["wd3_1"][:, :]),
                  inc=("dr3", 16))
            continue
        if nm in ("kT", "v", "attnT", "lnst", "x2e"):
            continue
        srcteno = {"gu_in": T["gu_in"], "qT": T["qT_priv"], "pp": T["partial_priv"],
                   "wdT": T["wdT_priv"],
                   "rs": T["rs_out"], "x2": T["x2_priv"], "hT": T["hT_priv"],
                   "r3": T["r3_priv"], "dn_in": T["dn_in"], "atp": T["attnT_priv"]}[nm]
        em.ev("sync", lambda E, dst=dst, srcteno=srcteno: E.dma_start(out=dst, in_=srcteno),
              inc=("dr3", 16))
    if T.get("dbg"):
        em.ev("sync", None, waits=[("dr3", C["dr3"])])


def build_nc(debug=()):
    nc = bass.Bass("TRN2", target_bir_lowering=False)
    T = {}
    T["in_x"] = nc.dram_tensor("x_my", [TPC, H], F32, kind="ExternalInput").ap()
    T["in_xseq"] = nc.dram_tensor("x_seq", [TPC, H], F32, kind="ExternalInput").ap()
    T["in_wqkv"] = nc.dram_tensor("wqkv_my", [WR, H], F32, kind="ExternalInput").ap()
    T["in_wdense"] = nc.dram_tensor("wdense_my", [H, 1024], F32, kind="ExternalInput").ap()
    T["in_qwgu"] = nc.dram_tensor("qwgu_my", [512, GU_FS], U32, kind="ExternalInput").ap()
    T["in_scgu"] = nc.dram_tensor("scgu_my", [32, GU_FS], F32, kind="ExternalInput").ap()
    T["in_qwdn"] = nc.dram_tensor("qwdn_my", [16 * DN_NSLOT, H], U32, kind="ExternalInput").ap()
    T["in_scdn"] = nc.dram_tensor("scdn_full", [86, H], F32, kind="ExternalInput").ap()
    T["in_qzgu"] = nc.dram_tensor("qzgu_full", [32, 2752], U32, kind="ExternalInput").ap()
    T["in_scgu_full"] = nc.dram_tensor("scgu_full", [32, F2], F32, kind="ExternalInput").ap()
    T["in_qzdn"] = nc.dram_tensor("qzdn_full", [86, 512], U32, kind="ExternalInput").ap()
    T["in_cf32"] = nc.dram_tensor("c_f32", [128, 16], F32, kind="ExternalInput").ap()
    T["in_cshift"] = nc.dram_tensor("c_shift", [128, 1], U32, kind="ExternalInput").ap()
    T["in_cid"] = nc.dram_tensor("c_id", [128, 128], BF16, kind="ExternalInput").ap()
    T["in_cones_col"] = nc.dram_tensor("c_ones_col", [128, 1], BF16, kind="ExternalInput").ap()
    T["in_cones_row"] = nc.dram_tensor("c_ones_row", [1, 128], BF16, kind="ExternalInput").ap()
    T["in_cmasks"] = nc.dram_tensor("c_masks", [128, 2048], BF16, kind="ExternalInput").ap()
    T["in_ciota32"] = nc.dram_tensor("c_iota32", [128, 32], F32, kind="ExternalInput").ap()
    T["in_ciota86"] = nc.dram_tensor("c_iota86", [128, 86], F32, kind="ExternalInput").ap()
    T["in_gidx_gu"] = nc.dram_tensor("gidx_gu", [128, 32], I32, kind="ExternalInput").ap()
    T["in_gidx_dn"] = nc.dram_tensor("gidx_dn", [128, 86], I32, kind="ExternalInput").ap()
    T["in_gidx_dnmy"] = nc.dram_tensor("gidx_dnmy", [128, DN_NSLOT], I32, kind="ExternalInput").ap()
    T["out"] = nc.dram_tensor("out", [TPC, H], F32, kind="ExternalOutput").ap()
    T["xbf_in"] = nc.dram_tensor("xbf_in", [TPC, H], BF16, kind="Internal").ap()
    T["xbf_all"] = nc.dram_tensor("xbf_all", [TB, H], BF16, kind="Internal").ap()
    T["wqT_priv"] = nc.dram_tensor("wqT_priv", [12 * 128, 4096], BF16, kind="Internal").ap()
    T["wdT_priv"] = nc.dram_tensor("wdT_priv", [8 * 128, 4096], BF16, kind="Internal").ap()
    T["qT_priv"] = nc.dram_tensor("qT_priv", [1024, TB], BF16, kind="Internal").ap()
    T["partial_priv"] = nc.dram_tensor("partial_priv", [TB, H], BF16, kind="Internal").ap()
    T["rs_out"] = nc.dram_tensor("rs_out", [TPC, H], BF16, kind="Internal").ap()
    T["attnT_priv"] = nc.dram_tensor("attnT_priv", [1024, TB], BF16, kind="Internal").ap()
    T["gu_in"] = nc.dram_tensor("gu_in", [GU_NSLOT * 128, 4096], BF16, kind="Internal").ap()
    T["gu_all"] = nc.dram_tensor("gu_all", [8 * GU_NSLOT * 128, 4096], BF16, kind="Internal").ap()
    T["dn_in"] = nc.dram_tensor("dn_in", [DN_NSLOT * 128, 4096], BF16, kind="Internal").ap()
    T["dn_all"] = nc.dram_tensor("dn_all", [8 * DN_NSLOT * 128, 4096], BF16, kind="Internal").ap()
    T["x2_priv"] = nc.dram_tensor("x2_priv", [TPC, H], BF16, kind="Internal").ap()
    T["hT_priv"] = nc.dram_tensor("hT_priv", [8 * 128, INTER], BF16, kind="Internal").ap()
    T["r3_priv"] = nc.dram_tensor("r3_priv", [TPC, H], BF16, kind="Internal").ap()
    _dbg_shapes = {"gu_in": ([GU_NSLOT * 128, 4096], BF16), "qT": ([1024, TB], BF16),
                   "pp": ([TB, H], BF16), "rs": ([TPC, H], BF16), "x2": ([TPC, H], BF16),
                   "hT": ([8 * 128, INTER], BF16), "r3": ([TPC, H], BF16),
                   "xTin": ([128, 32768], BF16), "dn_in": ([DN_NSLOT * 128, 4096], BF16),
                   "kT": ([128, 8192], BF16), "v": ([128, 8192], BF16),
                   "atp": ([1024, TB], BF16), "at3": ([128, 8192], BF16),
                   "wd3c": ([128, 4096], BF16),
                   "attnT": ([128, 32768], BF16), "wdT": ([8 * 128, 4096], BF16),
                   "lnst": ([128, 134], F32), "x2e": ([TPC, H], BF16)}
    T["dbg"] = {nm: nc.dram_tensor("dbg_" + nm, *_dbg_shapes[nm], kind="ExternalOutput").ap()
                for nm in debug}

    sbt = {}
    for name, (off, shape, dt) in SB.items():
        sbt[name] = nc.alloc_sbuf_tensor_at(name, shape, dt, offset=SBUF_BASE + off)
    T["sb"] = sbt
    pst = {}
    psum_guards = []
    for i in range(6):
        g = nc.psum_tensor(f"F{i}", [128, 512], F32)
        pst[f"F{i}"] = g.__enter__()
        psum_guards.append(g)
    for i in range(2):
        g = nc.psum_tensor(f"T{i}", [128, 1024], BF16)
        pst[f"T{i}"] = g.__enter__()
        psum_guards.append(g)
    T["ps"] = pst

    from contextlib import ExitStack
    stack = ExitStack()
    sems = {}
    for s in SEM_NAMES:
        sems[s] = stack.enter_context(nc.semaphore(s))
    blk = stack.enter_context(nc.Block())

    for ci, v in enumerate([EXP_BIAS, EXP_SCALE, LN_EPS, -0.5, INV_H, -1.0, 1.0, 0.0]):
        nc.const_aps.register(F32, float(v), sbt["c_f32"][:, ci:ci + 1])

    def mk(eng_name):
        def body(E):
            em = Emitter(eng_name, E, T, sems)
            build_schedule(em)
        return body

    blk.sync(mk("sync"))
    blk.scalar(mk("scalar"))
    blk.vector(mk("vector"))
    blk.tensor(mk("tensor"))
    blk.gpsimd(mk("gpsimd"))
    stack.close()
    return nc


def _make_in_maps(inputs):
    x = np.ascontiguousarray(np.asarray(inputs["L0_attn_q"], dtype=np.float32)).reshape(B * S, H)
    wqkv = np.asarray(inputs["L0_attn_w_qkv"], dtype=np.float32)
    wdense = np.asarray(inputs["L0_attn_w_dense"], dtype=np.float32)
    qwgu = np.asarray(inputs["L0_ffn_qweight_gu"], dtype=np.uint32)
    qzgu = np.asarray(inputs["L0_ffn_qzeros_gu"], dtype=np.uint32)
    scgu = np.asarray(inputs["L0_ffn_scales_gu"], dtype=np.float32)
    gigu = np.asarray(inputs["L0_ffn_g_idx_gu"], dtype=np.int32)
    qwdn = np.asarray(inputs["L0_ffn_qweight_down"], dtype=np.uint32)
    qzdn = np.asarray(inputs["L0_ffn_qzeros_down"], dtype=np.uint32)
    scdn = np.asarray(inputs["L0_ffn_scales_down"], dtype=np.float32)
    gidn = np.asarray(inputs["L0_ffn_g_idx_down"], dtype=np.int32)

    cf32 = np.tile(np.array([[EXP_BIAS, EXP_SCALE, LN_EPS, -0.5, INV_H, -1.0, 1.0, 0.0,
                              0, 0, 0, 0, 0, 0, 0, 0]], np.float32), (128, 1))
    cshift = (4 * (np.arange(128) % 8)).astype(np.uint32).reshape(128, 1)
    cid = np.eye(128).astype(bfloat16)
    cones_col = np.ones((128, 1), dtype=bfloat16)
    cones_row = np.ones((1, 128), dtype=bfloat16)
    masks = np.zeros((128, 2048), dtype=bfloat16)
    p = np.arange(128)[:, None]
    j = np.arange(512)[None, :]
    for d in range(4):
        masks[:, 512 * d:512 * (d + 1)] = (j >= p + 128 * d).astype(bfloat16)
    ciota32 = np.tile(np.arange(32, dtype=np.float32)[None, :], (128, 1))
    ciota86 = np.tile(np.arange(86, dtype=np.float32)[None, :], (128, 1))
    gidx_gu_in = np.ascontiguousarray(gigu.reshape(32, 128).T).astype(np.int32)
    gidx_dn_in = np.ascontiguousarray(gidn.reshape(86, 128).T).astype(np.int32)

    in_maps = []
    for c in range(8):
        kh0 = 2 * (c % 4)
        q0 = 512 * kh0
        k0 = O_Q + 128 * kh0
        v0 = O_Q + O_K + 128 * kh0
        wq_my = np.ascontiguousarray(np.concatenate(
            [wqkv[q0:q0 + 1024], wqkv[k0:k0 + 256], wqkv[v0:v0 + 256]], axis=0))
        wd_my = np.ascontiguousarray(wdense[:, q0:q0 + 1024])
        blocks = list(range(GU_STARTS[c], GU_STARTS[c] + GU_COUNTS[c]))
        while len(blocks) < GU_NSLOT:
            blocks.append(blocks[0])
        qwgu_my = np.ascontiguousarray(np.concatenate(
            [qwgu[:, 128 * g:128 * (g + 1)] for g in blocks], axis=1))
        scgu_my = np.ascontiguousarray(np.concatenate(
            [scgu[:, 128 * g:128 * (g + 1)] for g in blocks], axis=1))
        dslots = list(range(DN_STARTS[c], DN_STARTS[c] + DN_COUNTS[c]))
        while len(dslots) < DN_NSLOT:
            dslots.append(dslots[0])
        qwdn_my = np.ascontiguousarray(np.concatenate(
            [qwdn[16 * g:16 * (g + 1), :] for g in dslots], axis=0))
        gidx_dnmy = np.ascontiguousarray(gidx_dn_in[:, dslots])
        tok = np.concatenate([np.arange(4096 * (c // 4) + 1024 * g + 256 * (c % 4),
                                        4096 * (c // 4) + 1024 * g + 256 * (c % 4) + 256)
                              for g in range(4)])
        in_maps.append({
            "x_my": np.ascontiguousarray(x[tok]),
            "x_seq": np.ascontiguousarray(x[TPC * c:TPC * (c + 1)]),
            "wqkv_my": wq_my,
            "wdense_my": wd_my,
            "qwgu_my": qwgu_my,
            "scgu_my": scgu_my,
            "qwdn_my": qwdn_my,
            "scdn_full": np.ascontiguousarray(scdn),
            "qzgu_full": np.ascontiguousarray(qzgu),
            "scgu_full": np.ascontiguousarray(scgu),
            "qzdn_full": np.ascontiguousarray(qzdn),
            "c_f32": cf32, "c_shift": cshift, "c_id": cid,
            "c_ones_col": cones_col, "c_ones_row": cones_row, "c_masks": masks,
            "c_iota32": ciota32, "c_iota86": ciota86,
            "gidx_gu": gidx_gu_in, "gidx_dn": gidx_dn_in, "gidx_dnmy": gidx_dnmy,
        })
    return in_maps


def kernel(**inputs):
    in_maps = _make_in_maps(inputs)
    nc = build_nc()
    res = run_bass_kernel_spmd(nc, in_maps, core_ids=list(range(8)))
    out = np.empty((B * S, H), np.float32)
    for c in range(8):
        tok = np.concatenate([np.arange(4096 * (c // 4) + 1024 * g + 256 * (c % 4),
                                        4096 * (c // 4) + 1024 * g + 256 * (c % 4) + 256)
                              for g in range(4)])
        out[tok] = np.asarray(res.results[c]["out"])
    return np.ascontiguousarray(out.reshape(B, S, H))


# version 13
# speedup vs baseline: 1.0545x; 1.0545x over previous
import numpy as np
import ml_dtypes
import concourse.bass as bass
import concourse.mybir as mybir
from concourse.bass_utils import run_bass_kernel_spmd
from concourse.bass_core_rust import IndirectOffsetOnAxis

bfloat16 = ml_dtypes.bfloat16
F32 = mybir.dt.float32
BF16 = mybir.dt.bfloat16
U32 = mybir.dt.uint32
I32 = mybir.dt.int32
AF = mybir.ActivationFunctionType
ALU = mybir.AluOpType
AXX = mybir.AxisListType.X

B, S, H = 2, 4096, 4096
O_Q, O_K = 4096, 1024
WR = 1536
INTER = 11008
F2 = 22016
NT = 8
TPC = 1024
TB = 4096
GU_NSLOT = 22
DN_NSLOT = 11
GU_FS = GU_NSLOT * 128
GU_STARTS = [0, 22, 44, 66, 88, 109, 130, 151]
GU_COUNTS = [22, 22, 22, 22, 21, 21, 21, 21]
DN_STARTS = [0, 11, 22, 33, 44, 55, 66, 76]
DN_COUNTS = [11, 11, 11, 11, 11, 11, 10, 10]
EXP_SCALE = float(1.0 / np.sqrt(128.0))
EXP_BIAS = -12.0
LN_EPS = 1e-5
INV_H = 1.0 / 4096.0


def _gu_block_rs(g):
    for r in range(8):
        if GU_STARTS[r] <= g < GU_STARTS[r] + GU_COUNTS[r]:
            return r, g - GU_STARTS[r]
    raise AssertionError(g)


SB = {}


def _sb(name, off, shape, dt):
    SB[name] = (off, shape, dt)
    nb = {F32: 4, BF16: 2, U32: 4, I32: 4}[dt]
    return off + ((shape[1] * nb + 31) // 32) * 32


o = 0
o = _sb("c_f32", o, [128, 16], F32)
o = _sb("c_shift", o, [128, 1], U32)
o = _sb("c_id", o, [128, 128], BF16)
o = _sb("c_ones_col", o, [128, 1], BF16)
o = _sb("c_ones_row", o, [1, 128], BF16)
o = _sb("c_masks", o, [128, 2048], BF16)
o = _sb("c_iota32", o, [128, 32], F32)
o = _sb("c_iota86", o, [128, 86], F32)
o = _sb("gidx_gu", o, [128, 32], I32)
o = _sb("gidx_dn", o, [128, 86], I32)
o = _sb("gidx_dnmy", o, [128, DN_NSLOT], I32)
o = _sb("gidx_gu_f", o, [128, 32], F32)
o = _sb("gidx_dn_f", o, [128, 86], F32)
o = _sb("ind32", o, [128, 1024], BF16)
o = _sb("ind_d", o, [128, 172], BF16)
o = _sb("sums", o, [128, 64], F32)
o = _sb("sumsq", o, [128, 64], F32)
o = _sb("redt", o, [128, 4], F32)
o = _sb("rstd", o, [128, 1], F32)
o = _sb("lnb", o, [128, 1], F32)
o = _sb("rden", o, [1, 1024], BF16)
o = _sb("bcb", o, [128, 512], F32)
o = _sb("S_gu", o, [32, 1024], BF16)
o = _sb("S_dn", o, [86, 1024], BF16)
assert o <= 19456, o
_sb("kT", 19456, [128, 8192], BF16)
_sb("v", 35840, [128, 8192], BF16)
_sb("gu0", 19456, [128, 4096], BF16)
_sb("gu1", 27648, [128, 4096], BF16)
_sb("gu2", 35840, [128, 4096], BF16)
_sb("gu3", 44032, [128, 4096], BF16)
_sb("r3tt", 19456, [128, 4096], BF16)
_sb("outb", 27648, [128, 4096], F32)
_sb("ld32_0", 52224, [128, 4096], F32)
_sb("ld32_1", 68608, [128, 4096], F32)
_sb("castb_0", 84992, [128, 4096], BF16)
_sb("castb_1", 93184, [128, 4096], BF16)
_sb("tstage_0", 101376, [128, 4096], BF16)
_sb("tstage_1", 109568, [128, 4096], BF16)
_sb("drep", 117760, [128, 4096], U32)
_sb("dgath", 134144, [128, 4096], F32)
_sb("dnib", 150528, [128, 4096], U32)
_sb("dqout", 166912, [128, 4096], BF16)
_sb("xt_0", 52224, [128, 16384], BF16)
_sb("xt_1", 84992, [128, 16384], BF16)
_sb("wq_0", 175104, [128, 4096], BF16)
_sb("wq_1", 183296, [128, 4096], BF16)
_sb("vw", 191488, [128, 8192], BF16)
_sb("qstage_0", 207872, [128, 512], BF16)
_sb("qstage_1", 208896, [128, 512], BF16)
_sb("atst_0", 139264, [128, 4096], BF16)
_sb("atst_1", 147456, [128, 4096], BF16)
_sb("qt_0", 117760, [128, 4096], BF16)
_sb("qt_1", 125952, [128, 4096], BF16)
_sb("probs_0", 134144, [128, 512], BF16)
_sb("probs_1", 135168, [128, 512], BF16)
_sb("probs_2", 136192, [128, 512], BF16)
_sb("bcb2", 137216, [128, 512], F32)
_sb("at3_0", 54272, [128, 8192], BF16)
_sb("at3_1", 70656, [128, 8192], BF16)
_sb("wd3_0", 179200, [128, 4096], BF16)
_sb("wd3_1", 187392, [128, 4096], BF16)
_sb("dstage_0", 195584, [128, 512], BF16)
_sb("dstage_1", 197632, [128, 512], BF16)
_sb("x2T", 87040, [128, 32768], BF16)
_sb("r_tt", 152576, [128, 4096], BF16)
_sb("rstile_0", 160768, [128, 512], BF16)
_sb("rstile_1", 162816, [128, 512], BF16)
_sb("xres_0", 164864, [128, 512], F32)
_sb("xres_1", 166912, [128, 512], F32)
_sb("x2sb", 168960, [128, 4096], BF16)
_sb("sq_junk", 177152, [128, 512], F32)
_sb("znegg", 152576, [32, 22016], BF16)
_sb("znegd", 196608, [86, 4096], BF16)
_sb("zqz", 70656, [86, 256], U32)
_sb("zpl", 54272, [86, 2048], U32)
_sb("zsc", 62464, [86, 2048], F32)
_sb("hT4_0", 204800, [128, 1024], BF16)
_sb("hT4_1", 206848, [128, 1024], BF16)
_sb("g_sb", 208896, [128, 512], BF16)
_sb("u_sb", 209920, [128, 512], BF16)
_sb("wd5", 19456, [128, 44032], BF16)
_sb("hT5_0", 107520, [128, 11008], BF16)
_sb("hT5_1", 129536, [128, 11008], BF16)
_sb("x2t_0", 152576, [128, 512], BF16)
_sb("x2t_1", 153600, [128, 512], BF16)
_sb("r3st_0", 154624, [128, 512], BF16)
_sb("r3st_1", 155648, [128, 512], BF16)

SBUF_BASE = 16384
for name, (off, shape, dt) in SB.items():
    nb = {F32: 4, BF16: 2, U32: 4, I32: 4}[dt]
    assert SBUF_BASE + off + shape[1] * nb <= 229376, (name, off + shape[1] * nb)

SEM_NAMES = (
    ["gc", "gdq", "gdw", "cc", "dzn", "dr3",
     "sq", "spp", "sx2", "sr3", "sout", "dxT",
     "vp", "vdq", "ap", "tp", "vs", "as_"]
    + [f"{s}{p}" for s in ("dx", "dxt", "dwq", "dqt", "dwd", "drs", "dgu",
                           "dwdn", "dht", "dx2", "sxw", "sww", "sht",
                           "sat", "dat") for p in (0, 1)]
)


class Emitter:
    def __init__(self, eng_name, E, T, sems):
        self.eng = eng_name
        self.E = E
        self.T = T
        self.sems = sems
        self.cnt = {k: 0 for k in sems}

    def evsync(self, eng, fn, waits=(), inc=None):
        ssem = {"vector": "vs", "scalar": "as_"}[eng]
        self.cnt[ssem] += 1
        v = self.cnt[ssem]
        if inc is not None:
            self.cnt[inc[0]] += inc[1]
        if eng != self.eng:
            return
        E = self.E
        for s, val in waits:
            if val > 0:
                E.wait_ge(self.sems[s], val)
        ins = fn(E)
        ins.then_inc(self.sems[ssem], 1)
        E.wait_ge(self.sems[ssem], v)
        if inc is not None:
            E.sem_inc(self.sems[inc[0]], inc[1])

    def ev(self, eng, fn=None, waits=(), inc=None):
        if inc is not None:
            self.cnt[inc[0]] += inc[1]
        if eng != self.eng:
            return
        E = self.E
        for s, v in waits:
            if v > 0:
                E.wait_ge(self.sems[s], v)
        if fn is not None:
            ins = fn(E)
            if inc is not None and ins is not None:
                ins.then_inc(self.sems[inc[0]], inc[1])
        elif inc is not None:
            E.sem_inc(self.sems[inc[0]], inc[1])


def build_schedule(em: Emitter):
    T = em.T
    sb = T["sb"]
    ps = T["ps"]
    C = em.cnt

    const_loads = [
        ("c_f32", "in_cf32"), ("c_shift", "in_cshift"), ("c_id", "in_cid"),
        ("c_ones_col", "in_cones_col"), ("c_ones_row", "in_cones_row"),
        ("c_masks", "in_cmasks"), ("c_iota32", "in_ciota32"), ("c_iota86", "in_ciota86"),
        ("gidx_gu", "in_gidx_gu"), ("gidx_dn", "in_gidx_dn"), ("gidx_dnmy", "in_gidx_dnmy"),
    ]
    for name, key in const_loads:
        em.ev("gpsimd", lambda E, name=name, key=key: E.dma_start(
            out=sb[name][:, :], in_=T[key]), inc=("gc", 16))
    GC = C["gc"]
    em.ev("scalar", None, waits=[("gc", GC)])
    em.ev("vector", None, waits=[("gc", GC)])

    def transpose_pipeline(n_tiles, width, src_fn, store_fn, store_sem):
        cast_v = PIPE["cast_v"]
        tile_t = PIPE["tile_t"]
        evac_v = PIPE["evac_v"]
        store = PIPE["store"]
        ngr = width // 1024
        for i_ in range(n_tiles):
            i = PIPE["n"]
            PIPE["n"] += 1
            ldb = f"ld32_{i % 2}"
            cbb = f"castb_{i % 2}"
            stg = f"tstage_{i % 2}"
            w = [("vp", cast_v[i - 2])] if i - 2 in cast_v else []
            dxs = f"dx{i % 2}"
            em.ev("sync", lambda E, i_=i_, ldb=ldb: E.dma_start(
                out=sb[ldb][:, 0:width], in_=src_fn(i_)), waits=w, inc=(dxs, 16))
            dxn = C[dxs]
            w = [(dxs, dxn)]
            if i - 2 in tile_t:
                w.append(("tp", tile_t[i - 2]))
            em.ev("vector", lambda E, ldb=ldb, cbb=cbb: E.tensor_copy(
                sb[cbb][:, 0:width], sb[ldb][:, 0:width]), waits=w, inc=("vp", 1))
            cast_v[i] = C["vp"]
            for g in range(ngr):
                gg = PIPE["grp"]
                PIPE["grp"] += 1
                tb = f"T{gg % 2}"
                w = [("vp", cast_v[i]), ("gc", GC)]
                if gg - 2 in evac_v:
                    w.append(("vp", evac_v[gg - 2]))
                for k in range(8):
                    ht = 8 * g + k
                    em.ev("tensor", lambda E, cbb=cbb, ht=ht, k=k, tb=tb: E.transpose(
                        ps[tb][:, 128 * k:128 * (k + 1)],
                        sb[cbb][:, 128 * ht:128 * (ht + 1)], sb["c_id"][:, :]),
                        waits=w if k == 0 else [],
                        inc=("tp", 1) if k == 7 else None)
                w = [("tp", C["tp"])]
                if i - 2 in store:
                    w.append(store[i - 2])
                em.ev("vector", lambda E, tb=tb, stg=stg, g=g: E.tensor_copy(
                    sb[stg][:, 1024 * g:1024 * (g + 1)], ps[tb][:, :]),
                    waits=w, inc=("vp", 1))
                evac_v[gg] = C["vp"]
            tile_t[i] = C["tp"]
            ssem = f"{store_sem}{i % 2}"
            store_fn(i_, stg, evac_v[PIPE["grp"] - 1], ssem)
            store[i] = (ssem, C[ssem])

    PIPE = {"n": 0, "cast_v": {}, "tile_t": {}, "evac_v": {}, "store": {}, "grp": 0}
    xcast_v = {}
    xst = {}
    for i in range(8):
        ldb = f"ld32_{i % 2}"
        cbb = f"castb_{i % 2}"
        w = [("vp", xcast_v[i - 2])] if i - 2 in xcast_v else []
        dxs = f"dx{i % 2}"
        em.ev("sync", lambda E, i=i, ldb=ldb: E.dma_start(
            out=sb[ldb][:, :], in_=T["in_xseq"][128 * i:128 * (i + 1), :]), waits=w, inc=(dxs, 16))
        dxn = C[dxs]
        w = [(dxs, dxn)]
        if i - 2 in xst:
            w.append(xst[i - 2])
        em.ev("vector", lambda E, ldb=ldb, cbb=cbb: E.tensor_copy(
            sb[cbb][:, :], sb[ldb][:, :]), waits=w, inc=("vp", 1))
        xcast_v[i] = C["vp"]
        ssem = f"sxw{i % 2}"
        em.ev("scalar", lambda E, i=i, cbb=cbb: E.dma_start(
            out=T["xbf_in"][128 * i:128 * (i + 1), :], in_=sb[cbb][:, :]),
            waits=[("vp", xcast_v[i])], inc=(ssem, 16))
        xst[i] = (ssem, C[ssem])
    em.ev("gpsimd", lambda E: E.collective_compute(
        "AllGather", ALU.bypass, replica_groups=[[0, 1, 2, 3], [4, 5, 6, 7]],
        ins=[T["xbf_in"].opt()], outs=[T["xbf_all"].opt()]),
        waits=[("sxw0", C["sxw0"]), ("sxw1", C["sxw1"])], inc=("cc", 1))
    PIPE["cast_v"][-2] = xcast_v[6]
    PIPE["cast_v"][-1] = xcast_v[7]
    em.ev("vector", None, waits=[xst[6], xst[7]])

    def wq_store(i, stg, vv, ssem):
        em.ev("scalar", lambda E, i=i, stg=stg: E.dma_start(
            out=T["wqT_priv"][128 * i:128 * (i + 1), :], in_=sb[stg][:, :]),
            waits=[("vp", vv)], inc=(ssem, 16))

    transpose_pipeline(12, 4096, lambda i: T["in_wqkv"][128 * i:128 * (i + 1), :], wq_store, "sww")
    SWW_WQ = [("sww0", C["sww0"]), ("sww1", C["sww1"])]

    def wd_store(i, stg, vv, ssem):
        em.ev("scalar", lambda E, i=i, stg=stg: E.dma_start(
            out=T["wdT_priv"].rearrange("(at p) c -> p at c", at=8)[:, :, 128 * i:128 * (i + 1)],
            in_=sb[stg][:, 0:1024].rearrange("p (at w) -> p at w", at=8)),
            waits=[("vp", vv)], inc=(ssem, 16))

    transpose_pipeline(32, 1024, lambda i: T["in_wdense"][128 * i:128 * (i + 1), :], wd_store, "sww")
    SWW_ALL = [("sww0", C["sww0"]), ("sww1", C["sww1"])]

    em.evsync("vector", lambda E: E.tensor_copy(sb["gidx_gu_f"][:, :], sb["gidx_gu"][:, :]))
    em.evsync("vector", lambda E: E.tensor_copy(sb["gidx_dn_f"][:, :], sb["gidx_dn"][:, :]))
    for ht in range(32):
        em.ev("vector", lambda E, ht=ht: E.tensor_scalar(
            out=sb["ind32"][:, 32 * ht:32 * (ht + 1)], in0=sb["c_iota32"][:, :],
            scalar1=sb["gidx_gu_f"][:, ht:ht + 1], scalar2=None, op0=ALU.is_equal))
    em.ev("vector", None, inc=("vp", 1))
    V_IND32 = C["vp"]

    dq_vec_jobs = []
    n_dq = 0
    for kind in ("gu", "dn"):
        n_tiles = 32 if kind == "gu" else DN_NSLOT
        w_ = GU_FS if kind == "gu" else 4096
        for ti in range(n_tiles):
            gw = [("gc", GC)]
            if n_dq >= 1:
                gw.append(("vdq", n_dq))
            if kind == "gu":
                src = T["in_qwgu"][16 * ti:16 * (ti + 1), :].rearrange(
                    "r (o n) -> r o n", o=1).broadcast_to([16, 8, w_])
                tbl, idx = T["in_scgu"], sb["gidx_gu"][:, ti:ti + 1]
            else:
                src = T["in_qwdn"][16 * ti:16 * (ti + 1), :].rearrange(
                    "r (o n) -> r o n", o=1).broadcast_to([16, 8, w_])
                tbl, idx = T["in_scdn"], sb["gidx_dnmy"][:, ti:ti + 1]
            em.ev("gpsimd", lambda E, src=src, w_=w_: E.dma_start(
                out=sb["drep"][:, 0:w_], in_=src), waits=gw, inc=("gdq", 16))
            em.ev("gpsimd", lambda E, tbl=tbl, idx=idx, w_=w_: E.indirect_dma_start(
                out=sb["dgath"][:, 0:w_], out_offset=None, in_=tbl,
                in_offset=IndirectOffsetOnAxis(ap=idx, axis=0)), inc=("gdq", 16))
            gdqn = C["gdq"]
            vw_ = [("gdq", gdqn)]
            if n_dq >= 1:
                vw_.append(("gdw", 16 * n_dq))
            def dq_math(vw_=vw_, w_=w_):
                em.ev("vector", lambda E, w_=w_: E.tensor_scalar(
                    out=sb["dnib"][:, 0:w_], in0=sb["drep"][:, 0:w_],
                    scalar1=sb["c_shift"][:, :], scalar2=15,
                    op0=ALU.logical_shift_right, op1=ALU.bitwise_and), waits=vw_)
                em.ev("vector", lambda E, w_=w_: E.tensor_mul(
                    sb["dqout"][:, 0:w_], sb["dnib"][:, 0:w_], sb["dgath"][:, 0:w_]),
                    inc=("vdq", 1))
            dq_vec_jobs.append(dq_math)
            n_dq += 1
            if kind == "gu":
                em.ev("gpsimd", lambda E, ti=ti: E.dma_start(
                    out=T["gu_in"].rearrange("(s p) c -> p s c", s=GU_NSLOT)[:, :, 128 * ti:128 * (ti + 1)],
                    in_=sb["dqout"][:, 0:GU_FS].rearrange("p (s f) -> p s f", s=GU_NSLOT)),
                    waits=[("vdq", n_dq)], inc=("gdw", 16))
            else:
                em.ev("gpsimd", lambda E, ti=ti: E.dma_start(
                    out=T["dn_in"][128 * ti:128 * (ti + 1), :], in_=sb["dqout"][:, 0:4096]),
                    waits=[("vdq", n_dq)], inc=("gdw", 16))
    GDW_ALL = C["gdw"]
    em.ev("gpsimd", lambda E: E.collective_compute(
        "AllGather", ALU.bypass, replica_groups=[list(range(8))],
        ins=[T["gu_in"].opt()], outs=[T["gu_all"].opt()]),
        waits=[("gdw", GDW_ALL)], inc=("cc", 1))
    em.ev("gpsimd", lambda E: E.collective_compute(
        "AllGather", ALU.bypass, replica_groups=[list(range(8))],
        ins=[T["dn_in"].opt()], outs=[T["dn_all"].opt()]),
        inc=("cc", 1))

    for vb in range(2):
        em.ev("sync", lambda E, vb=vb: E.dma_start(
            out=sb["vw"].ap().rearrange("p (ht v) -> p ht v", v=256)[:, :, 128 * vb:128 * (vb + 1)],
            in_=T["wqT_priv"][128 * (10 + vb):128 * (11 + vb), :].rearrange("p (ht o) -> p ht o", ht=32)),
            waits=list(SWW_WQ), inc=(f"dwq{vb}", 16))
    VW_DONE = [("dwq0", C["dwq0"]), ("dwq1", C["dwq1"])]
    mm_done = {}
    ev_done = {}
    for tc in range(8):
        rkl = tc // 2
        colo = 512 * (tc % 2)
        xtb = f"xt_{tc % 2}"
        w = [("cc", 1)]
        if (tc - 2, 13) in mm_done:
            w.append(("tp", mm_done[(tc - 2, 13)]))
        dxts = f"dxt{tc % 2}"
        for ht in range(32):
            em.ev("scalar", lambda E, tc=tc, ht=ht, xtb=xtb: E.dma_start_transpose(
                sb[xtb][:, 512 * ht:512 * (ht + 1)],
                T["xbf_all"][512 * tc:512 * (tc + 1), 128 * ht:128 * (ht + 1)]),
                waits=w if ht == 0 else [], inc=(dxts, 16))
        dxtn = (dxts, C[dxts])
        for ot in range(10):
            wqb = f"wq_{ot % 2}"
            w = list(SWW_WQ)
            prev = (tc, ot - 2) if ot >= 2 else ((tc - 1, 8 + ot) if tc >= 1 else None)
            if prev in mm_done:
                w.append(("tp", mm_done[prev]))
            dwqs = f"dwq{ot % 2}"
            em.ev("sync", lambda E, ot=ot, wqb=wqb: E.dma_start(
                out=sb[wqb][:, :], in_=T["wqT_priv"][128 * ot:128 * (ot + 1), :]),
                waits=w, inc=(dwqs, 16))
            dwqn = (dwqs, C[dwqs])
            fb = f"F{ot % 2}"
            bw = [dxtn, dwqn]
            prev = (tc, ot - 2) if ot >= 2 else ((tc - 1, 8 + ot) if tc >= 1 else None)
            if prev in ev_done:
                bw.append(("vp", ev_done[prev]))
            for k in range(32):
                em.ev("tensor", lambda E, fb=fb, wqb=wqb, xtb=xtb, k=k: E.matmul(
                    ps[fb][:, :], sb[wqb][:, 128 * k:128 * (k + 1)],
                    sb[xtb][:, 512 * k:512 * (k + 1)], start=(k == 0), stop=(k == 31)),
                    waits=bw if k == 0 else [],
                    inc=("tp", 1) if k == 31 else None)
            mm_done[(tc, ot)] = C["tp"]
            if ot < 8:
                qsb = f"qstage_{ot % 2}"
                em.ev("vector", lambda E, fb=fb, qsb=qsb: E.tensor_copy(sb[qsb][:, :], ps[fb][:, :]),
                      waits=[("tp", mm_done[(tc, ot)]), ("sq", C["sq"])], inc=("vp", 1))
                ev_done[(tc, ot)] = C["vp"]
                em.ev("scalar", lambda E, qsb=qsb, ot=ot, tc=tc: E.dma_start(
                    out=T["qT_priv"][128 * ot:128 * (ot + 1), 512 * tc:512 * (tc + 1)],
                    in_=sb[qsb][:, :]),
                    waits=[("vp", ev_done[(tc, ot)])], inc=("sq", 16))
            else:
                u = ot - 8
                em.ev("vector", lambda E, fb=fb, u=u, tc=tc: E.tensor_copy(
                    sb["kT"][:, 4096 * u + 512 * tc:4096 * u + 512 * (tc + 1)], ps[fb][:, :]),
                    waits=[("tp", mm_done[(tc, ot)])], inc=("vp", 1))
                ev_done[(tc, ot)] = C["vp"]
        for vt in range(4):
            fb = f"F{2 + vt % 2}"
            bw = [dxtn] + VW_DONE
            prev = (tc, 10 + vt - 2) if vt >= 2 else ((tc - 1, 12 + vt) if tc >= 1 else None)
            if prev in ev_done:
                bw.append(("vp", ev_done[prev]))
            for k in range(32):
                em.ev("tensor", lambda E, fb=fb, xtb=xtb, vt=vt, k=k: E.matmul(
                    ps[fb][:, 0:256], sb[xtb][:, 512 * k + 128 * vt:512 * k + 128 * (vt + 1)],
                    sb["vw"][:, 256 * k:256 * (k + 1)], start=(k == 0), stop=(k == 31)),
                    waits=bw if k == 0 else [],
                    inc=("tp", 1) if k == 31 else None)
            mm_done[(tc, 10 + vt)] = C["tp"]
            ttb = 4 * tc + vt
            em.ev("vector", lambda E, fb=fb, ttb=ttb: E.tensor_copy(
                sb["v"][:, 256 * ttb:256 * (ttb + 1)], ps[fb][:, 0:256]),
                waits=[("tp", mm_done[(tc, 10 + vt)])], inc=("vp", 1))
            ev_done[(tc, 10 + vt)] = C["vp"]
        for _ in range(6):
            if dq_vec_jobs:
                dq_vec_jobs.pop(0)()
    while dq_vec_jobs:
        dq_vec_jobs.pop(0)()
    V_P1_END = C["vp"]
    SQ_ALL = C["sq"]

    strip_t = {}
    qt_loaded = {}
    expn = 0
    tail_tt = {}
    tail_rec = {}
    tail_cp = {}
    sc_pw = {}

    TL = []
    n = -1
    for u in range(2):
        for qh in range(4):
            sidx = 4 * u + qh
            for sqc in range(8):
                n += 1
                ntk = 4 * (sqc + 1)
                for tk in range(ntk):
                    TL.append(dict(n=n, u=u, sidx=sidx, sqc=sqc, tk=tk, ntk=ntk,
                                   par=n % 2, fa="F3" if n % 2 == 0 else "F4",
                                   dslc=(0, 1) if n % 2 == 0 else (32, 33)))

    def ensure_strip(sidx):
        if sidx in qt_loaded:
            return
        qtb = f"qt_{sidx % 2}"
        w = [("sq", SQ_ALL), ("gdw", GDW_ALL)]
        if sidx - 2 in strip_t:
            w.append(("tp", strip_t[sidx - 2]))
        dqts = f"dqt{sidx % 2}"
        em.ev("sync", lambda E, sidx=sidx, qtb=qtb: E.dma_start(
            out=sb[qtb][:, :], in_=T["qT_priv"][128 * sidx:128 * (sidx + 1), :]),
            waits=w, inc=(dqts, 16))
        qt_loaded[sidx] = (dqts, C[dqts])

    def emit_sc(gi):
        nonlocal expn
        t = TL[gi]
        ensure_strip(t["sidx"])
        fb = f"F{gi % 2}"
        pb = f"probs_{gi % 3}"
        qtb = f"qt_{t['sidx'] % 2}"
        sw = [qt_loaded[t["sidx"]], ("vp", V_P1_END)]
        if expn - 1 > 0:
            sw.append(("ap", expn - 1))
        em.ev("tensor", lambda E, fb=fb, t=t, qtb=qtb: E.matmul(
            ps[fb][:, :], sb["kT"][:, 4096 * t["u"] + 128 * t["tk"]:4096 * t["u"] + 128 * (t["tk"] + 1)],
            sb[qtb][:, 512 * t["sqc"]:512 * (t["sqc"] + 1)], start=True, stop=True),
            waits=sw, inc=("tp", 1))
        tsc = C["tp"]
        em.ev("scalar", lambda E, fb=fb, pb=pb: E.activation(
            out=sb[pb][:, :], in_=ps[fb][:, :], func=AF.Exp,
            bias=EXP_BIAS, scale=EXP_SCALE),
            waits=[("tp", tsc)], inc=("ap", 1))
        expn = C["ap"]
        if t["tk"] >= t["ntk"] - 4:
            dpos = t["tk"] - (t["ntk"] - 4)
            em.ev("vector", lambda E, pb=pb, dpos=dpos: E.tensor_mul(
                sb[pb][:, :], sb[pb][:, :], sb["c_masks"][:, 512 * dpos:512 * (dpos + 1)]),
                waits=[("ap", expn)], inc=("vp", 1))
            sc_pw[gi] = ("vp", C["vp"])
        else:
            sc_pw[gi] = ("ap", expn)

    def emit_avden(gi):
        t = TL[gi]
        n = t["n"]
        pb = f"probs_{gi % 3}"
        f_ = t["tk"] == 0
        l_ = t["tk"] == t["ntk"] - 1
        avw = [sc_pw.pop(gi)]
        if f_ and n - 2 in tail_tt:
            avw.append(("vp", tail_tt[n - 2]))
        em.ev("tensor", lambda E, t=t, pb=pb, f_=f_, l_=l_: E.matmul(
            ps[t["fa"]][:, :], sb["v"][:, 256 * t["tk"] + 128 * t["u"]:256 * t["tk"] + 128 * (t["u"] + 1)],
            sb[pb][:, :], start=f_, stop=l_),
            waits=avw, inc=("tp", 1))
        dw = []
        if f_ and n - 2 in tail_rec:
            dw.append(("vp", tail_rec[n - 2]))
        em.ev("tensor", lambda E, t=t, pb=pb, f_=f_, l_=l_: E.matmul(
            ps["F5"][t["dslc"][0]:t["dslc"][1], :], sb["c_ones_col"][:, :], sb[pb][:, :],
            start=f_, stop=l_,
            tile_position=(0, 32) if t["par"] == 1 else None),
            waits=dw, inc=("tp", 1))
        if not l_:
            return
        tden = C["tp"]
        par = t["par"]
        sidx = t["sidx"]
        sqc = t["sqc"]
        dslc = t["dslc"]
        fa = t["fa"]

        def _recip(E, dslc=dslc, par=par):
            with E.bass.allow_low_precision(reason="bf16 softmax denom"):
                return E.reciprocal(sb["rden"][0:1, 512 * par:512 * (par + 1)],
                                    ps["F5"][dslc[0]:dslc[1], :])
        em.evsync("vector", _recip, waits=[("tp", tden)], inc=("vp", 1))
        tail_rec[n] = C["vp"]
        bw = []
        if n - 1 in tail_cp:
            bw.append(("ap", tail_cp[n - 1]))
        em.ev("tensor", lambda E, par=par: E.matmul(
            ps["F2"][:, :], sb["c_ones_row"][:, :],
            sb["rden"][0:1, 512 * par:512 * (par + 1)], start=True, stop=True),
            waits=[("vp", tail_rec[n])] + bw, inc=("tp", 1))
        tbc = C["tp"]
        bcb = "bcb" if par == 0 else "bcb2"
        em.ev("scalar", lambda E, bcb=bcb: E.copy(sb[bcb][:, :], ps["F2"][:, :]),
              waits=[("tp", tbc)], inc=("ap", 1))
        tail_cp[n] = C["ap"]
        ats = f"atst_{sidx % 2}"
        sats = f"sat{sidx % 2}"
        ttw = [("ap", tail_cp[n])]
        if sqc == 0 and C[sats] > 0:
            ttw.append((sats, C[sats]))
        em.ev("vector", lambda E, ats=ats, sqc=sqc, fa=fa, bcb=bcb: E.tensor_mul(
            sb[ats][:, 512 * sqc:512 * (sqc + 1)], ps[fa][:, :], sb[bcb][:, :]),
            waits=ttw, inc=("vp", 1))
        tail_tt[n] = C["vp"]
        if sqc == 7:
            em.ev("scalar", lambda E, sidx=sidx, ats=ats: E.dma_start(
                out=T["attnT_priv"][128 * sidx:128 * (sidx + 1), :], in_=sb[ats][:, :]),
                waits=[("vp", tail_tt[n])], inc=(sats, 16))
            strip_t[sidx] = C["tp"]

    NTILES = len(TL)
    emit_sc(0)
    emit_sc(1)
    for gi in range(NTILES):
        if gi + 2 < NTILES:
            emit_sc(gi + 2)
        emit_avden(gi)
    V_P2_END = C["vp"]
    TP2_END = C["tp"]
    SAT_ALL = [("sat0", C["sat0"]), ("sat1", C["sat1"])]
    for nm in ("kT", "v", "attnT"):
        if nm in T.get("dbg", {}):
            em.ev("sync", lambda E, nm=nm: E.dma_start(out=T["dbg"][nm], in_=sb[nm][:, :]),
                  waits=[("vp", V_P2_END)], inc=("dr3", 16))
    if any(nm in T.get("dbg", {}) for nm in ("kT", "v", "attnT")):
        em.ev("sync", None, waits=[("dr3", C["dr3"])])

    tile_t3 = {}
    KEY3 = {}
    spp_g = {}
    nps = -1
    for g in range(4):
        datg = f"dat{g % 2}"
        aw = list(SAT_ALL)
        if g == 0:
            aw.append(("vp", V_P2_END))
        if g - 2 >= 0:
            aw.append(("vp", tile_t3[(g - 2, 7, 7)]))
        em.ev("sync", lambda E, g=g: E.dma_start(
            out=sb[f"at3_{g % 2}"].ap().rearrange("p (s c) -> p s c", s=8),
            in_=T["attnT_priv"].rearrange("(s p) c -> p s c", s=8)[:, :, 1024 * g:1024 * (g + 1)]),
            waits=aw, inc=(datg, 16))
        datn = (datg, C[datg])
        for hc in range(8):
            wdb = f"wd3_{hc % 2}"
            w = list(SWW_ALL)
            if g == 0 and hc == 0:
                w.append(("vp", V_P2_END))
            prev = (g, hc - 2) if hc >= 2 else ((g - 1, hc + 6) if g >= 1 else None)
            if prev is not None and (prev[0], prev[1], 7) in tile_t3:
                w.append(("vp", tile_t3[(prev[0], prev[1], 7)]))
            dwds = f"dwd{(8 * g + hc) % 2}"
            em.ev("sync", lambda E, hc=hc, wdb=wdb: E.dma_start(
                out=sb[wdb].ap().rearrange("p (at w) -> p at w", at=8),
                in_=T["wdT_priv"].rearrange("(at p) c -> p at c", at=8)[:, :, 512 * hc:512 * (hc + 1)]),
                waits=w, inc=(dwds, 16))
            dwdn_ = (dwds, C[dwds])
            for tl in range(8):
                nps += 1
                tt_b = 8 * g + tl
                fb = f"F{nps % 2}"
                bw = [dwdn_, datn]
                if nps >= 2:
                    bw.append(("vp", tile_t3[KEY3[nps - 2]]))
                KEY3[nps] = (g, hc, tl)
                for j in range(8):
                    em.ev("tensor", lambda E, fb=fb, j=j, tl=tl, wdb=wdb, g=g: E.matmul(
                        ps[fb][:, :], sb[f"at3_{g % 2}"][:, 1024 * j + 128 * tl:1024 * j + 128 * (tl + 1)],
                        sb[wdb][:, 512 * j:512 * (j + 1)], start=(j == 0), stop=(j == 7)),
                        waits=bw if j == 0 else [],
                        inc=("tp", 1) if j == 7 else None)
                tpn = C["tp"]
                dsb = f"dstage_{nps % 2}"
                em.ev("vector", lambda E, fb=fb, dsb=dsb: E.tensor_copy(sb[dsb][:, :], ps[fb][:, :]),
                      waits=[("tp", tpn), ("spp", C["spp"])], inc=("vp", 1))
                tile_t3[(g, hc, tl)] = C["vp"]
                em.ev("scalar", lambda E, dsb=dsb, tt_b=tt_b, hc=hc: E.dma_start(
                    out=T["partial_priv"][128 * tt_b:128 * (tt_b + 1), 512 * hc:512 * (hc + 1)],
                    in_=sb[dsb][:, :]),
                    waits=[("vp", tile_t3[(g, hc, tl)])], inc=("spp", 16))
        spp_g[g] = C["spp"]
        em.ev("gpsimd", lambda E, g=g: E.collective_compute(
            "ReduceScatter", ALU.add, replica_groups=[[0, 1, 2, 3], [4, 5, 6, 7]],
            ins=[T["partial_priv"][1024 * g:1024 * (g + 1), :].opt()],
            outs=[T["rs_out"][256 * g:256 * (g + 1), :].opt()]),
            waits=[("spp", spp_g[g])], inc=("cc", 1))
    TP3_END = C["tp"]

    ap_x2 = {}
    sx2_n = {}
    for tt in range(NT):
        g = tt // 2
        for hc in range(8):
            rsb = f"rstile_{hc % 2}"
            xrb = f"xres_{hc % 2}"
            w = [("cc", 4 + g)]
            if hc >= 2 or tt > 0:
                w.append(("vp", C["vp"]))
            drss = f"drs{hc % 2}"
            em.ev("sync", lambda E, tt=tt, hc=hc, rsb=rsb: E.dma_start(
                out=sb[rsb][:, :], in_=T["rs_out"][128 * tt:128 * (tt + 1), 512 * hc:512 * (hc + 1)]),
                waits=w, inc=(drss, 16))
            em.ev("sync", lambda E, tt=tt, hc=hc, xrb=xrb: E.dma_start(
                out=sb[xrb][:, :], in_=T["in_x"][128 * tt:128 * (tt + 1), 512 * hc:512 * (hc + 1)]),
                inc=(drss, 16))
            aw = [(drss, C[drss])]
            if hc == 0 and tt - 1 in ap_x2:
                aw.append(("ap", ap_x2[tt - 1]))
            em.ev("vector", lambda E, rsb=rsb, xrb=xrb, hc=hc: E.tensor_add(
                sb["r_tt"][:, 512 * hc:512 * (hc + 1)], sb[rsb][:, :], sb[xrb][:, :]), waits=aw)
            col = 8 * tt + hc
            em.evsync("vector", lambda E, hc=hc, col=col: E.reduce_sum(
                sb["sums"][:, col:col + 1], sb["r_tt"][:, 512 * hc:512 * (hc + 1)], axis=AXX),
                inc=("vp", 1))
            vn = C["vp"]
            em.evsync("scalar", lambda E, hc=hc, col=col: E.activation(
                out=sb["sq_junk"][:, :], in_=sb["r_tt"][:, 512 * hc:512 * (hc + 1)],
                func=AF.Square, accum_out=sb["sumsq"][:, col:col + 1]),
                waits=[("vp", vn)], inc=("ap", 1))
        apsq = C["ap"]
        em.evsync("vector", lambda E, tt=tt: E.reduce_sum(
            sb["redt"][:, 0:1], sb["sums"][:, 8 * tt:8 * (tt + 1)], axis=AXX),
            waits=[("ap", apsq)])
        em.evsync("vector", lambda E, tt=tt: E.reduce_sum(
            sb["redt"][:, 1:2], sb["sumsq"][:, 8 * tt:8 * (tt + 1)], axis=AXX))
        em.evsync("vector", lambda E: E.tensor_scalar_mul(sb["redt"][:, 2:3], sb["redt"][:, 0:1], INV_H))
        em.evsync("vector", lambda E: E.tensor_scalar_mul(sb["redt"][:, 1:2], sb["redt"][:, 1:2], INV_H))
        em.evsync("vector", lambda E: E.tensor_mul(sb["redt"][:, 0:1], sb["redt"][:, 2:3], sb["redt"][:, 2:3]))
        em.evsync("vector", lambda E: E.tensor_sub(sb["redt"][:, 1:2], sb["redt"][:, 1:2], sb["redt"][:, 0:1]),
              inc=("vp", 1))
        vvar = C["vp"]
        em.evsync("scalar", lambda E: E.activation(out=sb["redt"][:, 3:4], in_=sb["redt"][:, 1:2],
                                                   func=AF.Ln, bias=LN_EPS), waits=[("vp", vvar)])
        em.evsync("scalar", lambda E: E.activation(out=sb["rstd"][:, :], in_=sb["redt"][:, 3:4],
                                                   func=AF.Exp, scale=-0.5), inc=("ap", 1))
        aprs = C["ap"]
        em.evsync("vector", lambda E: E.tensor_scalar(
            out=sb["lnb"][:, :], in0=sb["redt"][:, 2:3], scalar1=sb["rstd"][:, :],
            scalar2=-1.0, op0=ALU.mult, op1=ALU.mult),
            waits=[("ap", aprs)], inc=("vp", 1))
        vlnb = C["vp"]
        aw = [("vp", vlnb)]
        if tt - 1 in sx2_n:
            aw.append(("sx2", sx2_n[tt - 1]))
        em.ev("scalar", lambda E: E.activation(
            out=sb["x2sb"][:, :], in_=sb["r_tt"][:, :], func=AF.Identity,
            scale=sb["rstd"][:, :], bias=sb["lnb"][:, :]), waits=aw, inc=("ap", 1))
        ap_x2[tt] = C["ap"]
        em.ev("scalar", lambda E, tt=tt: E.dma_start(
            out=T["x2_priv"][128 * tt:128 * (tt + 1), :], in_=sb["x2sb"][:, :]),
            waits=[("ap", ap_x2[tt])], inc=("sx2", 16))
        sx2_n[tt] = C["sx2"]
        for ht in range(32):
            em.ev("scalar", lambda E, ht=ht, tt=tt: E.dma_start_transpose(
                sb["x2T"][:, 1024 * ht + 128 * tt:1024 * ht + 128 * (tt + 1)],
                T["x2_priv"][128 * tt:128 * (tt + 1), 128 * ht:128 * (ht + 1)]),
                waits=[("sx2", sx2_n[tt])] if ht == 0 else [], inc=("dxT", 16))
    V_X2T = C["vp"]
    DXT_ALL = C["dxT"]
    SX2_ALL = C["sx2"]
    if "x2e" in T.get("dbg", {}):
        em.ev("sync", lambda E: E.dma_start(out=T["dbg"]["x2e"], in_=T["x2_priv"]),
              waits=[("sx2", SX2_ALL)], inc=("dr3", 16))
        em.ev("sync", None, waits=[("dr3", C["dr3"])])
    if "lnst" in T.get("dbg", {}):
        em.ev("sync", lambda E: E.dma_start(out=T["dbg"]["lnst"][:, 0:64], in_=sb["sums"][:, :]),
              waits=[("ap", C["ap"]), ("vp", C["vp"])], inc=("dr3", 16))
        em.ev("sync", lambda E: E.dma_start(out=T["dbg"]["lnst"][:, 64:128], in_=sb["sumsq"][:, :]),
              inc=("dr3", 16))
        em.ev("sync", lambda E: E.dma_start(out=T["dbg"]["lnst"][:, 128:132], in_=sb["redt"][:, :]),
              inc=("dr3", 16))
        def _d1(E):
            with E.bass.allow_non_contiguous_dma(reason="dbg"):
                return E.dma_start(out=T["dbg"]["lnst"][:, 132:133], in_=sb["rstd"][:, :])
        def _d2(E):
            with E.bass.allow_non_contiguous_dma(reason="dbg"):
                return E.dma_start(out=T["dbg"]["lnst"][:, 133:134], in_=sb["lnb"][:, :])
        em.ev("sync", _d1, inc=("dr3", 16))
        em.ev("sync", _d2, inc=("dr3", 16))
        em.ev("sync", None, waits=[("dr3", C["dr3"])])

    for tc2 in range(2):
        fb = "F4" if tc2 == 0 else "F5"
        for ht in range(32):
            em.ev("tensor", lambda E, fb=fb, ht=ht, tc2=tc2: E.matmul(
                ps[fb][0:32, :], sb["ind32"][:, 32 * ht:32 * (ht + 1)],
                sb["x2T"][:, 1024 * ht + 512 * tc2:1024 * ht + 512 * (tc2 + 1)],
                start=(ht == 0), stop=(ht == 31)),
                waits=[("dxT", DXT_ALL), ("vp", V_IND32)] if ht == 0 else [],
                inc=("tp", 1) if ht == 31 else None)
        tsg = C["tp"]
        em.ev("vector", lambda E, fb=fb, tc2=tc2: E.tensor_copy(
            sb["S_gu"][:, 512 * tc2:512 * (tc2 + 1)], ps[fb][0:32, :]),
            waits=[("tp", tsg)], inc=("vp", 1))
    V_SGU = C["vp"]
    ZG = [("ap", ap_x2[NT - 1]), ("sx2", C["sx2"]), ("tp", TP3_END)]
    for k in range(16):
        em.ev("sync", lambda E, k=k: E.dma_start(
            out=sb["zqz"][0:32, 0:172], in_=T["in_qzgu"][:, 172 * k:172 * (k + 1)]),
            waits=ZG + [("vp", C["vp"])], inc=("dzn", 16))
        em.ev("sync", lambda E, k=k: E.dma_start(
            out=sb["zsc"][0:32, 0:1376], in_=T["in_scgu_full"][:, 1376 * k:1376 * (k + 1)]),
            inc=("dzn", 16))
        dznn = C["dzn"]
        for j in range(8):
            fj = lambda E, j=j: E.tensor_scalar(
                out=sb["zpl"][0:32, 172 * j:172 * (j + 1)], in0=sb["zqz"][0:32, 0:172],
                scalar1=float(4 * j), scalar2=15,
                op0=ALU.logical_shift_right, op1=ALU.bitwise_and)
            if j == 7:
                em.evsync("vector", fj)
            else:
                em.ev("vector", fj, waits=(ZG + [("dzn", dznn)]) if (j == 0 and k == 0) else ([("dzn", dznn)] if j == 0 else []))
        em.evsync("vector", lambda E, k=k: E.scalar_tensor_tensor(
            out=sb["znegg"][0:32, 1376 * k:1376 * (k + 1)].rearrange("g (c j) -> g c j", j=8),
            in0=sb["zpl"][0:32, 0:1376].rearrange("g (j c) -> g c j", j=8),
            scalar=1.0, in1=sb["zsc"][0:32, 0:1376].rearrange("g (c j) -> g c j", j=8),
            op0=ALU.add, op1=ALU.mult))
        em.evsync("vector", lambda E, k=k: E.tensor_scalar_mul(
            sb["znegg"][0:32, 1376 * k:1376 * (k + 1)], sb["znegg"][0:32, 1376 * k:1376 * (k + 1)], -1.0),
            inc=("vp", 1))
    for half in range(2):
        em.ev("sync", lambda E, half=half: E.dma_start(
            out=sb["zqz"][0:86, 0:256], in_=T["in_qzdn"][:, 256 * half:256 * (half + 1)]),
            waits=ZG + [("vp", C["vp"])], inc=("dzn", 16))
        em.ev("sync", lambda E, half=half: E.dma_start(
            out=sb["zsc"][0:86, 0:2048], in_=T["in_scdn"][:, 2048 * half:2048 * (half + 1)]),
            inc=("dzn", 16))
        dznn = C["dzn"]
        for j in range(8):
            fj = lambda E, j=j: E.tensor_scalar(
                out=sb["zpl"][0:86, 256 * j:256 * (j + 1)], in0=sb["zqz"][0:86, 0:256],
                scalar1=float(4 * j), scalar2=15,
                op0=ALU.logical_shift_right, op1=ALU.bitwise_and)
            if j == 7:
                em.evsync("vector", fj)
            else:
                em.ev("vector", fj, waits=[("dzn", dznn)] if j == 0 else [])
        em.evsync("vector", lambda E, half=half: E.scalar_tensor_tensor(
            out=sb["znegd"][0:86, 2048 * half:2048 * (half + 1)].rearrange("g (c j) -> g c j", j=8),
            in0=sb["zpl"][0:86, 0:2048].rearrange("g (j c) -> g c j", j=8),
            scalar=1.0, in1=sb["zsc"][0:86, 0:2048].rearrange("g (c j) -> g c j", j=8),
            op0=ALU.add, op1=ALU.mult))
        em.evsync("vector", lambda E, half=half: E.tensor_scalar_mul(
            sb["znegd"][0:86, 2048 * half:2048 * (half + 1)],
            sb["znegd"][0:86, 2048 * half:2048 * (half + 1)], -1.0),
            inc=("vp", 1))
    V_ZNEG = C["vp"]

    blk_t = {}
    hT_v = {}
    sil_a = {}
    sht_n = {}
    ind_v = {}
    for i in range(86):
        gr, gs = _gu_block_rs(i)
        ur, us = _gu_block_rs(86 + i)
        grow = 128 * (GU_NSLOT * gr + gs)
        urow = 128 * (GU_NSLOT * ur + us)
        ga = f"gu{2 * (i % 2)}"
        ub = f"gu{2 * (i % 2) + 1}"
        w = [("cc", 2)]
        if i == 0:
            w.append(("tp", TP2_END))
            w.append(("sq", SQ_ALL))
        if i - 2 in blk_t:
            w.append(("tp", blk_t[i - 2]))
        dgus = f"dgu{i % 2}"
        em.ev("sync", lambda E, grow=grow, ga=ga: E.dma_start(
            out=sb[ga][:, :], in_=T["gu_all"][grow:grow + 128, :]), waits=w, inc=(dgus, 16))
        em.ev("sync", lambda E, urow=urow, ub=ub: E.dma_start(
            out=sb[ub][:, :], in_=T["gu_all"][urow:urow + 128, :]), inc=(dgus, 16))
        dgun = (dgus, C[dgus])
        iw = [("vp", V_ZNEG)]
        if i - 2 in blk_t:
            iw.append(("tp", blk_t[i - 2]))
        em.ev("vector", lambda E, i=i: E.tensor_scalar(
            out=sb["ind_d"][:, 86 * (i % 2):86 * (i % 2) + 86], in0=sb["c_iota86"][:, :],
            scalar1=sb["gidx_dn_f"][:, i:i + 1], scalar2=None, op0=ALU.is_equal),
            waits=iw, inc=("vp", 1))
        ind_v[i] = C["vp"]
        for tc2 in range(2):
            fg = f"F{i % 2}"
            fu = f"F{2 + i % 2}"
            bw = [dgun, ("vp", V_ZNEG), ("dxT", DXT_ALL)]
            if (i - 2, tc2) in hT_v:
                bw.append(("vp", hT_v[(i - 2, tc2)]))
            if (i - 2, tc2) in sil_a:
                bw.append(("ap", sil_a[(i - 2, tc2)]))
            for k in range(32):
                em.ev("tensor", lambda E, fg=fg, ga=ga, k=k, tc2=tc2: E.matmul(
                    ps[fg][:, :], sb[ga][:, 128 * k:128 * (k + 1)],
                    sb["x2T"][:, 1024 * k + 512 * tc2:1024 * k + 512 * (tc2 + 1)],
                    start=(k == 0), stop=False),
                    waits=bw if k == 0 else [])
            em.ev("tensor", lambda E, fg=fg, i=i, tc2=tc2: E.matmul(
                ps[fg][:, :], sb["znegg"][0:32, 128 * i:128 * (i + 1)],
                sb["S_gu"][:, 512 * tc2:512 * (tc2 + 1)], start=False, stop=True),
                waits=[("vp", V_SGU)], inc=("tp", 1))
            for k in range(32):
                em.ev("tensor", lambda E, fu=fu, ub=ub, k=k, tc2=tc2: E.matmul(
                    ps[fu][:, :], sb[ub][:, 128 * k:128 * (k + 1)],
                    sb["x2T"][:, 1024 * k + 512 * tc2:1024 * k + 512 * (tc2 + 1)],
                    start=(k == 0), stop=False))
            em.ev("tensor", lambda E, fu=fu, i=i, tc2=tc2: E.matmul(
                ps[fu][:, :], sb["znegg"][0:32, 128 * (86 + i):128 * (87 + i)],
                sb["S_gu"][:, 512 * tc2:512 * (tc2 + 1)], start=False, stop=True),
                inc=("tp", 1))
            tgu = C["tp"]
            sw = [("tp", tgu)]
            if i == 0 and tc2 == 0:
                sw.append(("sq", SQ_ALL))
            em.ev("scalar", lambda E, fg=fg: E.activation(
                out=sb["g_sb"][:, :], in_=ps[fg][:, :], func=AF.Silu),
                waits=sw, inc=("ap", 1))
            sil_a[(i, tc2)] = C["ap"]
            em.ev("vector", lambda E, fu=fu: E.tensor_copy(sb["u_sb"][:, :], ps[fu][:, :]),
                  waits=[("tp", tgu)] + ([("sq", SQ_ALL)] if i == 0 and tc2 == 0 else []))
            hb = f"hT4_{i % 2}"
            hw = [("ap", sil_a[(i, tc2)])]
            if i - 2 in sht_n:
                hw.append(sht_n[i - 2])
            em.ev("vector", lambda E, hb=hb, tc2=tc2: E.tensor_mul(
                sb[hb][:, 512 * tc2:512 * (tc2 + 1)], sb["g_sb"][:, :], sb["u_sb"][:, :]),
                waits=hw, inc=("vp", 1))
            hT_v[(i, tc2)] = C["vp"]
            fsd = "F4" if tc2 == 0 else "F5"
            em.ev("tensor", lambda E, fsd=fsd, i=i, hb=hb, tc2=tc2: E.matmul(
                ps[fsd][0:86, :], sb["ind_d"][:, 86 * (i % 2):86 * (i % 2) + 86],
                sb[hb][:, 512 * tc2:512 * (tc2 + 1)], start=(i == 0), stop=(i == 85)),
                waits=[("vp", hT_v[(i, tc2)]), ("vp", ind_v[i])] +
                      ([("vp", V_SGU)] if i == 0 else []),
                inc=("tp", 1))
        blk_t[i] = C["tp"]
        hb = f"hT4_{i % 2}"
        shts = f"sht{i % 2}"
        em.ev("scalar", lambda E, i=i, hb=hb: E.dma_start(
            out=T["hT_priv"].rearrange("(tt p) c -> p tt c", tt=8)[:, :, 128 * i:128 * (i + 1)],
            in_=sb[hb].ap().rearrange("p (tt t) -> p tt t", tt=8)),
            waits=[("vp", hT_v[(i, 1)])], inc=(shts, 16))
        sht_n[i] = (shts, C[shts])
    TP4_END = C["tp"]
    SHT_ALL = [("sht0", C["sht0"]), ("sht1", C["sht1"])]
    for tc2 in range(2):
        fsd = "F4" if tc2 == 0 else "F5"
        em.ev("vector", lambda E, fsd=fsd, tc2=tc2: E.tensor_copy(
            sb["S_dn"][:, 512 * tc2:512 * (tc2 + 1)], ps[fsd][0:86, :]),
            waits=[("tp", TP4_END)], inc=("vp", 1))
    V_SDN = C["vp"]

    ch5_t = {}
    tile_t5 = []
    ev5 = {}
    v_p5 = C["vp"]
    add_v = {}
    for hc in range(8):
        w = [("cc", 3), ("vp", V_SDN)] + list(SHT_ALL)
        if hc - 1 in ch5_t:
            w.append(("tp", ch5_t[hc - 1]))
        dwdns = f"dwdn{hc % 2}"
        em.ev("sync", lambda E, hc=hc: E.dma_start(
            out=sb["wd5"][:, 0:76 * 512].rearrange("p (g w) -> p g w", g=76),
            in_=T["dn_all"].rearrange("(g p) c -> p g c", g=88)[:, 0:76, 512 * hc:512 * (hc + 1)]),
            waits=w, inc=(dwdns, 16))
        em.ev("sync", lambda E, hc=hc: E.dma_start(
            out=sb["wd5"][:, 76 * 512:86 * 512].rearrange("p (g w) -> p g w", g=10),
            in_=T["dn_all"].rearrange("(g p) c -> p g c", g=88)[:, 77:87, 512 * hc:512 * (hc + 1)]),
            inc=(dwdns, 16))
        dwdnn = (dwdns, C[dwdns])
        for tt in range(NT):
            n = 8 * hc + tt
            h5 = f"hT5_{tt % 2}"
            w = list(SHT_ALL)
            if n >= 2:
                w.append(("tp", tile_t5[n - 2]))
            dhts = f"dht{n % 2}"
            em.ev("sync", lambda E, tt=tt, h5=h5: E.dma_start(
                out=sb[h5][:, :], in_=T["hT_priv"][128 * tt:128 * (tt + 1), :]),
                waits=w, inc=(dhts, 16))
            x2b = f"x2t_{tt % 2}"
            dx2s = f"dx2{n % 2}"
            xw = [("sx2", SX2_ALL), ("vp", V_SDN)]
            if n - 2 in add_v:
                xw.append(("vp", add_v[n - 2]))
            em.ev("sync", lambda E, tt=tt, hc=hc, x2b=x2b: E.dma_start(
                out=sb[x2b][:, :], in_=T["x2_priv"][128 * tt:128 * (tt + 1), 512 * hc:512 * (hc + 1)]),
                waits=xw, inc=(dx2s, 16))
            dhtn = (dhts, C[dhts])
            dx2n = (dx2s, C[dx2s])
            fb = f"F{n % 2}"
            bw = [dwdnn, dhtn]
            if n - 2 in ev5:
                bw.append(("vp", ev5[n - 2]))
            for g in range(86):
                em.ev("tensor", lambda E, fb=fb, g=g, h5=h5: E.matmul(
                    ps[fb][:, :], sb[h5][:, 128 * g:128 * (g + 1)],
                    sb["wd5"][:, 512 * g:512 * (g + 1)], start=(g == 0), stop=False),
                    waits=bw if g == 0 else [])
            em.ev("tensor", lambda E, fb=fb, tt=tt, hc=hc: E.matmul(
                ps[fb][:, :], sb["S_dn"][:, 128 * tt:128 * (tt + 1)],
                sb["znegd"][0:86, 512 * hc:512 * (hc + 1)], start=False, stop=True),
                inc=("tp", 1))
            tile_t5.append(C["tp"])
            r3b = f"r3st_{tt % 2}"
            em.ev("vector", lambda E, fb=fb, r3b=r3b, x2b=x2b: E.tensor_add(
                sb[r3b][:, :], ps[fb][:, :], sb[x2b][:, :]),
                waits=[("tp", tile_t5[-1]), dx2n, ("sr3", C["sr3"])])
            add_v[n] = C["vp"] + 1
            col = 8 * tt + hc
            em.evsync("vector", lambda E, r3b=r3b, col=col: E.reduce_sum(
                sb["sums"][:, col:col + 1], sb[r3b][:, :], axis=AXX), inc=("vp", 1))
            v_p5 = C["vp"]
            ev5[n] = C["vp"]
            em.evsync("scalar", lambda E, r3b=r3b, col=col: E.activation(
                out=sb["sq_junk"][:, :], in_=sb[r3b][:, :], func=AF.Square,
                accum_out=sb["sumsq"][:, col:col + 1]),
                waits=[("vp", v_p5)], inc=("ap", 1))
            em.ev("scalar", lambda E, r3b=r3b, tt=tt, hc=hc: E.dma_start(
                out=T["r3_priv"][128 * tt:128 * (tt + 1), 512 * hc:512 * (hc + 1)],
                in_=sb[r3b][:, :]),
                inc=("sr3", 16))
        ch5_t[hc] = C["tp"]
    SR3_ALL = C["sr3"]
    TP5_END = C["tp"]

    sout_n = {}
    apout_n = {}
    for tt in range(NT):
        w = [("sr3", SR3_ALL)]
        if tt == 0:
            w.append(("tp", TP5_END))
        if tt - 1 in apout_n:
            w.append(("ap", apout_n[tt - 1]))
        em.ev("sync", lambda E, tt=tt: E.dma_start(
            out=sb["r3tt"][:, :], in_=T["r3_priv"][128 * tt:128 * (tt + 1), :]),
            waits=w, inc=("dr3", 16))
        dr3n = C["dr3"]
        em.evsync("vector", lambda E, tt=tt: E.reduce_sum(
            sb["redt"][:, 0:1], sb["sums"][:, 8 * tt:8 * (tt + 1)], axis=AXX),
            waits=[("dr3", dr3n), ("ap", C["ap"])])
        em.evsync("vector", lambda E, tt=tt: E.reduce_sum(
            sb["redt"][:, 1:2], sb["sumsq"][:, 8 * tt:8 * (tt + 1)], axis=AXX))
        em.evsync("vector", lambda E: E.tensor_scalar_mul(sb["redt"][:, 2:3], sb["redt"][:, 0:1], INV_H))
        em.evsync("vector", lambda E: E.tensor_scalar_mul(sb["redt"][:, 1:2], sb["redt"][:, 1:2], INV_H))
        em.evsync("vector", lambda E: E.tensor_mul(sb["redt"][:, 0:1], sb["redt"][:, 2:3], sb["redt"][:, 2:3]))
        em.evsync("vector", lambda E: E.tensor_sub(sb["redt"][:, 1:2], sb["redt"][:, 1:2], sb["redt"][:, 0:1]),
              inc=("vp", 1))
        vvar = C["vp"]
        em.evsync("scalar", lambda E: E.activation(out=sb["redt"][:, 3:4], in_=sb["redt"][:, 1:2],
                                                   func=AF.Ln, bias=LN_EPS), waits=[("vp", vvar)])
        em.evsync("scalar", lambda E: E.activation(out=sb["rstd"][:, :], in_=sb["redt"][:, 3:4],
                                                   func=AF.Exp, scale=-0.5), inc=("ap", 1))
        aprs = C["ap"]
        em.evsync("vector", lambda E: E.tensor_scalar(
            out=sb["lnb"][:, :], in0=sb["redt"][:, 2:3], scalar1=sb["rstd"][:, :],
            scalar2=-1.0, op0=ALU.mult, op1=ALU.mult),
            waits=[("ap", aprs)], inc=("vp", 1))
        vlnb = C["vp"]
        aw = [("vp", vlnb)]
        if tt - 1 in sout_n:
            aw.append(("sout", sout_n[tt - 1]))
        em.ev("scalar", lambda E: E.activation(
            out=sb["outb"][:, :], in_=sb["r3tt"][:, :], func=AF.Identity,
            scale=sb["rstd"][:, :], bias=sb["lnb"][:, :]), waits=aw, inc=("ap", 1))
        apout = C["ap"]
        apout_n[tt] = apout
        em.ev("scalar", lambda E, tt=tt: E.dma_start(
            out=T["out"][128 * tt:128 * (tt + 1), :], in_=sb["outb"][:, :]),
            waits=[("ap", apout)], inc=("sout", 16))
        sout_n[tt] = C["sout"]
    em.ev("sync", None, waits=[("sout", C["sout"])])
    for nm, dst in T.get("dbg", {}).items():
        if nm == "at3":
            em.ev("sync", lambda E, dst=dst: E.dma_start(out=dst, in_=sb["at3_1"][:, :]),
                  inc=("dr3", 16))
            continue
        if nm == "wd3c":
            em.ev("sync", lambda E, dst=dst: E.dma_start(out=dst, in_=sb["wd3_1"][:, :]),
                  inc=("dr3", 16))
            continue
        if nm in ("kT", "v", "attnT", "lnst", "x2e"):
            continue
        srcteno = {"gu_in": T["gu_in"], "qT": T["qT_priv"], "pp": T["partial_priv"],
                   "wdT": T["wdT_priv"],
                   "rs": T["rs_out"], "x2": T["x2_priv"], "hT": T["hT_priv"],
                   "r3": T["r3_priv"], "dn_in": T["dn_in"], "atp": T["attnT_priv"]}[nm]
        em.ev("sync", lambda E, dst=dst, srcteno=srcteno: E.dma_start(out=dst, in_=srcteno),
              inc=("dr3", 16))
    if T.get("dbg"):
        em.ev("sync", None, waits=[("dr3", C["dr3"])])


def build_nc(debug=()):
    nc = bass.Bass("TRN2", target_bir_lowering=False)
    T = {}
    T["in_x"] = nc.dram_tensor("x_my", [TPC, H], F32, kind="ExternalInput").ap()
    T["in_xseq"] = nc.dram_tensor("x_seq", [TPC, H], F32, kind="ExternalInput").ap()
    T["in_wqkv"] = nc.dram_tensor("wqkv_my", [WR, H], F32, kind="ExternalInput").ap()
    T["in_wdense"] = nc.dram_tensor("wdense_my", [H, 1024], F32, kind="ExternalInput").ap()
    T["in_qwgu"] = nc.dram_tensor("qwgu_my", [512, GU_FS], U32, kind="ExternalInput").ap()
    T["in_scgu"] = nc.dram_tensor("scgu_my", [32, GU_FS], F32, kind="ExternalInput").ap()
    T["in_qwdn"] = nc.dram_tensor("qwdn_my", [16 * DN_NSLOT, H], U32, kind="ExternalInput").ap()
    T["in_scdn"] = nc.dram_tensor("scdn_full", [86, H], F32, kind="ExternalInput").ap()
    T["in_qzgu"] = nc.dram_tensor("qzgu_full", [32, 2752], U32, kind="ExternalInput").ap()
    T["in_scgu_full"] = nc.dram_tensor("scgu_full", [32, F2], F32, kind="ExternalInput").ap()
    T["in_qzdn"] = nc.dram_tensor("qzdn_full", [86, 512], U32, kind="ExternalInput").ap()
    T["in_cf32"] = nc.dram_tensor("c_f32", [128, 16], F32, kind="ExternalInput").ap()
    T["in_cshift"] = nc.dram_tensor("c_shift", [128, 1], U32, kind="ExternalInput").ap()
    T["in_cid"] = nc.dram_tensor("c_id", [128, 128], BF16, kind="ExternalInput").ap()
    T["in_cones_col"] = nc.dram_tensor("c_ones_col", [128, 1], BF16, kind="ExternalInput").ap()
    T["in_cones_row"] = nc.dram_tensor("c_ones_row", [1, 128], BF16, kind="ExternalInput").ap()
    T["in_cmasks"] = nc.dram_tensor("c_masks", [128, 2048], BF16, kind="ExternalInput").ap()
    T["in_ciota32"] = nc.dram_tensor("c_iota32", [128, 32], F32, kind="ExternalInput").ap()
    T["in_ciota86"] = nc.dram_tensor("c_iota86", [128, 86], F32, kind="ExternalInput").ap()
    T["in_gidx_gu"] = nc.dram_tensor("gidx_gu", [128, 32], I32, kind="ExternalInput").ap()
    T["in_gidx_dn"] = nc.dram_tensor("gidx_dn", [128, 86], I32, kind="ExternalInput").ap()
    T["in_gidx_dnmy"] = nc.dram_tensor("gidx_dnmy", [128, DN_NSLOT], I32, kind="ExternalInput").ap()
    T["out"] = nc.dram_tensor("out", [TPC, H], F32, kind="ExternalOutput").ap()
    T["xbf_in"] = nc.dram_tensor("xbf_in", [TPC, H], BF16, kind="Internal").ap()
    T["xbf_all"] = nc.dram_tensor("xbf_all", [TB, H], BF16, kind="Internal").ap()
    T["wqT_priv"] = nc.dram_tensor("wqT_priv", [12 * 128, 4096], BF16, kind="Internal").ap()
    T["wdT_priv"] = nc.dram_tensor("wdT_priv", [8 * 128, 4096], BF16, kind="Internal").ap()
    T["qT_priv"] = nc.dram_tensor("qT_priv", [1024, TB], BF16, kind="Internal").ap()
    T["partial_priv"] = nc.dram_tensor("partial_priv", [TB, H], BF16, kind="Internal").ap()
    T["rs_out"] = nc.dram_tensor("rs_out", [TPC, H], BF16, kind="Internal").ap()
    T["attnT_priv"] = nc.dram_tensor("attnT_priv", [1024, TB], BF16, kind="Internal").ap()
    T["gu_in"] = nc.dram_tensor("gu_in", [GU_NSLOT * 128, 4096], BF16, kind="Internal").ap()
    T["gu_all"] = nc.dram_tensor("gu_all", [8 * GU_NSLOT * 128, 4096], BF16, kind="Internal").ap()
    T["dn_in"] = nc.dram_tensor("dn_in", [DN_NSLOT * 128, 4096], BF16, kind="Internal").ap()
    T["dn_all"] = nc.dram_tensor("dn_all", [8 * DN_NSLOT * 128, 4096], BF16, kind="Internal").ap()
    T["x2_priv"] = nc.dram_tensor("x2_priv", [TPC, H], BF16, kind="Internal").ap()
    T["hT_priv"] = nc.dram_tensor("hT_priv", [8 * 128, INTER], BF16, kind="Internal").ap()
    T["r3_priv"] = nc.dram_tensor("r3_priv", [TPC, H], BF16, kind="Internal").ap()
    _dbg_shapes = {"gu_in": ([GU_NSLOT * 128, 4096], BF16), "qT": ([1024, TB], BF16),
                   "pp": ([TB, H], BF16), "rs": ([TPC, H], BF16), "x2": ([TPC, H], BF16),
                   "hT": ([8 * 128, INTER], BF16), "r3": ([TPC, H], BF16),
                   "xTin": ([128, 32768], BF16), "dn_in": ([DN_NSLOT * 128, 4096], BF16),
                   "kT": ([128, 8192], BF16), "v": ([128, 8192], BF16),
                   "atp": ([1024, TB], BF16), "at3": ([128, 8192], BF16),
                   "wd3c": ([128, 4096], BF16),
                   "attnT": ([128, 32768], BF16), "wdT": ([8 * 128, 4096], BF16),
                   "lnst": ([128, 134], F32), "x2e": ([TPC, H], BF16)}
    T["dbg"] = {nm: nc.dram_tensor("dbg_" + nm, *_dbg_shapes[nm], kind="ExternalOutput").ap()
                for nm in debug}

    sbt = {}
    for name, (off, shape, dt) in SB.items():
        sbt[name] = nc.alloc_sbuf_tensor_at(name, shape, dt, offset=SBUF_BASE + off)
    T["sb"] = sbt
    pst = {}
    psum_guards = []
    for i in range(6):
        g = nc.psum_tensor(f"F{i}", [128, 512], F32)
        pst[f"F{i}"] = g.__enter__()
        psum_guards.append(g)
    for i in range(2):
        g = nc.psum_tensor(f"T{i}", [128, 1024], BF16)
        pst[f"T{i}"] = g.__enter__()
        psum_guards.append(g)
    T["ps"] = pst

    from contextlib import ExitStack
    stack = ExitStack()
    sems = {}
    for s in SEM_NAMES:
        sems[s] = stack.enter_context(nc.semaphore(s))
    blk = stack.enter_context(nc.Block())

    for ci, v in enumerate([EXP_BIAS, EXP_SCALE, LN_EPS, -0.5, INV_H, -1.0, 1.0, 0.0]):
        nc.const_aps.register(F32, float(v), sbt["c_f32"][:, ci:ci + 1])

    def mk(eng_name):
        def body(E):
            em = Emitter(eng_name, E, T, sems)
            build_schedule(em)
        return body

    blk.sync(mk("sync"))
    blk.scalar(mk("scalar"))
    blk.vector(mk("vector"))
    blk.tensor(mk("tensor"))
    blk.gpsimd(mk("gpsimd"))
    stack.close()
    return nc


def _make_in_maps(inputs):
    x = np.ascontiguousarray(np.asarray(inputs["L0_attn_q"], dtype=np.float32)).reshape(B * S, H)
    wqkv = np.asarray(inputs["L0_attn_w_qkv"], dtype=np.float32)
    wdense = np.asarray(inputs["L0_attn_w_dense"], dtype=np.float32)
    qwgu = np.asarray(inputs["L0_ffn_qweight_gu"], dtype=np.uint32)
    qzgu = np.asarray(inputs["L0_ffn_qzeros_gu"], dtype=np.uint32)
    scgu = np.asarray(inputs["L0_ffn_scales_gu"], dtype=np.float32)
    gigu = np.asarray(inputs["L0_ffn_g_idx_gu"], dtype=np.int32)
    qwdn = np.asarray(inputs["L0_ffn_qweight_down"], dtype=np.uint32)
    qzdn = np.asarray(inputs["L0_ffn_qzeros_down"], dtype=np.uint32)
    scdn = np.asarray(inputs["L0_ffn_scales_down"], dtype=np.float32)
    gidn = np.asarray(inputs["L0_ffn_g_idx_down"], dtype=np.int32)

    cf32 = np.tile(np.array([[EXP_BIAS, EXP_SCALE, LN_EPS, -0.5, INV_H, -1.0, 1.0, 0.0,
                              0, 0, 0, 0, 0, 0, 0, 0]], np.float32), (128, 1))
    cshift = (4 * (np.arange(128) % 8)).astype(np.uint32).reshape(128, 1)
    cid = np.eye(128).astype(bfloat16)
    cones_col = np.ones((128, 1), dtype=bfloat16)
    cones_row = np.ones((1, 128), dtype=bfloat16)
    masks = np.zeros((128, 2048), dtype=bfloat16)
    p = np.arange(128)[:, None]
    j = np.arange(512)[None, :]
    for d in range(4):
        masks[:, 512 * d:512 * (d + 1)] = (j >= p + 128 * d).astype(bfloat16)
    ciota32 = np.tile(np.arange(32, dtype=np.float32)[None, :], (128, 1))
    ciota86 = np.tile(np.arange(86, dtype=np.float32)[None, :], (128, 1))
    gidx_gu_in = np.ascontiguousarray(gigu.reshape(32, 128).T).astype(np.int32)
    gidx_dn_in = np.ascontiguousarray(gidn.reshape(86, 128).T).astype(np.int32)

    in_maps = []
    for c in range(8):
        kh0 = 2 * (c % 4)
        q0 = 512 * kh0
        k0 = O_Q + 128 * kh0
        v0 = O_Q + O_K + 128 * kh0
        wq_my = np.ascontiguousarray(np.concatenate(
            [wqkv[q0:q0 + 1024], wqkv[k0:k0 + 256], wqkv[v0:v0 + 256]], axis=0))
        wd_my = np.ascontiguousarray(wdense[:, q0:q0 + 1024])
        blocks = list(range(GU_STARTS[c], GU_STARTS[c] + GU_COUNTS[c]))
        while len(blocks) < GU_NSLOT:
            blocks.append(blocks[0])
        qwgu_my = np.ascontiguousarray(np.concatenate(
            [qwgu[:, 128 * g:128 * (g + 1)] for g in blocks], axis=1))
        scgu_my = np.ascontiguousarray(np.concatenate(
            [scgu[:, 128 * g:128 * (g + 1)] for g in blocks], axis=1))
        dslots = list(range(DN_STARTS[c], DN_STARTS[c] + DN_COUNTS[c]))
        while len(dslots) < DN_NSLOT:
            dslots.append(dslots[0])
        qwdn_my = np.ascontiguousarray(np.concatenate(
            [qwdn[16 * g:16 * (g + 1), :] for g in dslots], axis=0))
        gidx_dnmy = np.ascontiguousarray(gidx_dn_in[:, dslots])
        tok = np.concatenate([np.arange(4096 * (c // 4) + 1024 * g + 256 * (c % 4),
                                        4096 * (c // 4) + 1024 * g + 256 * (c % 4) + 256)
                              for g in range(4)])
        in_maps.append({
            "x_my": np.ascontiguousarray(x[tok]),
            "x_seq": np.ascontiguousarray(x[TPC * c:TPC * (c + 1)]),
            "wqkv_my": wq_my,
            "wdense_my": wd_my,
            "qwgu_my": qwgu_my,
            "scgu_my": scgu_my,
            "qwdn_my": qwdn_my,
            "scdn_full": np.ascontiguousarray(scdn),
            "qzgu_full": np.ascontiguousarray(qzgu),
            "scgu_full": np.ascontiguousarray(scgu),
            "qzdn_full": np.ascontiguousarray(qzdn),
            "c_f32": cf32, "c_shift": cshift, "c_id": cid,
            "c_ones_col": cones_col, "c_ones_row": cones_row, "c_masks": masks,
            "c_iota32": ciota32, "c_iota86": ciota86,
            "gidx_gu": gidx_gu_in, "gidx_dn": gidx_dn_in, "gidx_dnmy": gidx_dnmy,
        })
    return in_maps


def kernel(**inputs):
    in_maps = _make_in_maps(inputs)
    nc = build_nc()
    res = run_bass_kernel_spmd(nc, in_maps, core_ids=list(range(8)))
    out = np.empty((B * S, H), np.float32)
    for c in range(8):
        tok = np.concatenate([np.arange(4096 * (c // 4) + 1024 * g + 256 * (c % 4),
                                        4096 * (c // 4) + 1024 * g + 256 * (c % 4) + 256)
                              for g in range(4)])
        out[tok] = np.asarray(res.results[c]["out"])
    return np.ascontiguousarray(out.reshape(B, S, H))


# version 14
# speedup vs baseline: 1.0571x; 1.0024x over previous
import numpy as np
import ml_dtypes
import concourse.bass as bass
import concourse.mybir as mybir
from concourse.bass_utils import run_bass_kernel_spmd
from concourse.bass_core_rust import IndirectOffsetOnAxis

bfloat16 = ml_dtypes.bfloat16
F32 = mybir.dt.float32
BF16 = mybir.dt.bfloat16
U32 = mybir.dt.uint32
I32 = mybir.dt.int32
AF = mybir.ActivationFunctionType
ALU = mybir.AluOpType
AXX = mybir.AxisListType.X

B, S, H = 2, 4096, 4096
O_Q, O_K = 4096, 1024
WR = 1536
INTER = 11008
F2 = 22016
NT = 8
TPC = 1024
TB = 4096
GU_NSLOT = 22
DN_NSLOT = 11
GU_FS = GU_NSLOT * 128
GU_STARTS = [0, 22, 44, 66, 88, 109, 130, 151]
GU_COUNTS = [22, 22, 22, 22, 21, 21, 21, 21]
DN_STARTS = [0, 11, 22, 33, 44, 55, 66, 76]
DN_COUNTS = [11, 11, 11, 11, 11, 11, 10, 10]
EXP_SCALE = float(1.0 / np.sqrt(128.0))
EXP_BIAS = -12.0
LN_EPS = 1e-5
INV_H = 1.0 / 4096.0


def _gu_block_rs(g):
    for r in range(8):
        if GU_STARTS[r] <= g < GU_STARTS[r] + GU_COUNTS[r]:
            return r, g - GU_STARTS[r]
    raise AssertionError(g)


SB = {}


def _sb(name, off, shape, dt):
    SB[name] = (off, shape, dt)
    nb = {F32: 4, BF16: 2, U32: 4, I32: 4}[dt]
    return off + ((shape[1] * nb + 31) // 32) * 32


o = 0
o = _sb("c_f32", o, [128, 16], F32)
o = _sb("c_shift", o, [128, 1], U32)
o = _sb("c_id", o, [128, 128], BF16)
o = _sb("c_ones_col", o, [128, 1], BF16)
o = _sb("c_ones_row", o, [1, 128], BF16)
o = _sb("c_masks", o, [128, 2048], BF16)
o = _sb("c_iota32", o, [128, 32], F32)
o = _sb("c_iota86", o, [128, 86], F32)
o = _sb("gidx_gu", o, [128, 32], I32)
o = _sb("gidx_dn", o, [128, 86], I32)
o = _sb("gidx_dnmy", o, [128, DN_NSLOT], I32)
o = _sb("gidx_gu_f", o, [128, 32], F32)
o = _sb("gidx_dn_f", o, [128, 86], F32)
o = _sb("ind32", o, [128, 1024], BF16)
o = _sb("ind_d", o, [128, 172], BF16)
o = _sb("sums", o, [128, 64], F32)
o = _sb("sumsq", o, [128, 64], F32)
o = _sb("redt", o, [128, 4], F32)
o = _sb("rstd", o, [128, 1], F32)
o = _sb("lnb", o, [128, 1], F32)
o = _sb("rden", o, [1, 1024], BF16)
o = _sb("bcb", o, [128, 512], F32)
o = _sb("S_gu", o, [32, 1024], BF16)
o = _sb("S_dn", o, [86, 1024], BF16)
assert o <= 19456, o
_sb("kT", 19456, [128, 8192], BF16)
_sb("v", 35840, [128, 8192], BF16)
_sb("gu0", 19456, [128, 4096], BF16)
_sb("gu1", 27648, [128, 4096], BF16)
_sb("gu2", 35840, [128, 4096], BF16)
_sb("gu3", 44032, [128, 4096], BF16)
_sb("r3tt", 19456, [128, 4096], BF16)
_sb("outb", 27648, [128, 4096], F32)
_sb("ld32_0", 52224, [128, 4096], F32)
_sb("ld32_1", 68608, [128, 4096], F32)
_sb("castb_0", 84992, [128, 4096], BF16)
_sb("castb_1", 93184, [128, 4096], BF16)
_sb("tstage_0", 101376, [128, 4096], BF16)
_sb("tstage_1", 109568, [128, 4096], BF16)
_sb("drep", 117760, [128, 4096], U32)
_sb("dgath", 134144, [128, 4096], F32)
_sb("dnib", 150528, [128, 4096], U32)
_sb("dqout", 166912, [128, 4096], BF16)
_sb("xt_0", 52224, [128, 16384], BF16)
_sb("xt_1", 84992, [128, 16384], BF16)
_sb("wq_0", 175104, [128, 4096], BF16)
_sb("wq_1", 183296, [128, 4096], BF16)
_sb("vw", 191488, [128, 8192], BF16)
_sb("qstage_0", 207872, [128, 512], BF16)
_sb("qstage_1", 208896, [128, 512], BF16)
_sb("atst_0", 139264, [128, 4096], BF16)
_sb("atst_1", 147456, [128, 4096], BF16)
_sb("qt_0", 117760, [128, 4096], BF16)
_sb("qt_1", 125952, [128, 4096], BF16)
_sb("probs_0", 134144, [128, 512], BF16)
_sb("probs_1", 135168, [128, 512], BF16)
_sb("probs_2", 136192, [128, 512], BF16)
_sb("bcb2", 137216, [128, 512], F32)
_sb("at3_0", 54272, [128, 8192], BF16)
_sb("at3_1", 70656, [128, 8192], BF16)
_sb("wd3_0", 179200, [128, 4096], BF16)
_sb("wd3_1", 187392, [128, 4096], BF16)
_sb("dstage_0", 195584, [128, 512], BF16)
_sb("dstage_1", 197632, [128, 512], BF16)
_sb("x2T", 87040, [128, 32768], BF16)
_sb("r_tt", 152576, [128, 4096], BF16)
_sb("rstile_0", 160768, [128, 512], BF16)
_sb("rstile_1", 162816, [128, 512], BF16)
_sb("xres_0", 164864, [128, 512], F32)
_sb("xres_1", 166912, [128, 512], F32)
_sb("x2sb", 168960, [128, 4096], BF16)
_sb("sq_junk", 177152, [128, 512], F32)
_sb("znegg", 152576, [32, 22016], BF16)
_sb("znegd", 196608, [86, 4096], BF16)
_sb("zqz", 70656, [86, 256], U32)
_sb("zpl", 54272, [86, 2048], U32)
_sb("zsc", 62464, [86, 2048], F32)
_sb("hT4_0", 204800, [128, 1024], BF16)
_sb("hT4_1", 206848, [128, 1024], BF16)
_sb("g_sb", 208896, [128, 512], BF16)
_sb("u_sb", 209920, [128, 512], BF16)
_sb("wd5", 19456, [128, 44032], BF16)
_sb("hT5_0", 107520, [128, 11008], BF16)
_sb("hT5_1", 129536, [128, 11008], BF16)
_sb("x2t_0", 152576, [128, 512], BF16)
_sb("x2t_1", 153600, [128, 512], BF16)
_sb("r3st_0", 154624, [128, 512], BF16)
_sb("r3st_1", 155648, [128, 512], BF16)

SBUF_BASE = 16384
for name, (off, shape, dt) in SB.items():
    nb = {F32: 4, BF16: 2, U32: 4, I32: 4}[dt]
    assert SBUF_BASE + off + shape[1] * nb <= 229376, (name, off + shape[1] * nb)

SEM_NAMES = (
    ["gc", "gdq", "gdw", "cc", "dzn", "dr3",
     "sq0", "sq1", "spp", "sx2", "sr3", "sout", "dxT",
     "vp", "vdq", "ap", "tp", "vs", "as_"]
    + [f"{s}{p}" for s in ("dx", "dxt", "dwq", "dqt", "dwd", "drs", "dgu",
                           "dwdn", "dht", "dx2", "sxw", "sww", "sht",
                           "sat", "dat") for p in (0, 1)]
)


class Emitter:
    def __init__(self, eng_name, E, T, sems):
        self.eng = eng_name
        self.E = E
        self.T = T
        self.sems = sems
        self.cnt = {k: 0 for k in sems}

    def evsync(self, eng, fn, waits=(), inc=None):
        ssem = {"vector": "vs", "scalar": "as_"}[eng]
        self.cnt[ssem] += 1
        v = self.cnt[ssem]
        if inc is not None:
            self.cnt[inc[0]] += inc[1]
        if eng != self.eng:
            return
        E = self.E
        for s, val in waits:
            if val > 0:
                E.wait_ge(self.sems[s], val)
        ins = fn(E)
        ins.then_inc(self.sems[ssem], 1)
        E.wait_ge(self.sems[ssem], v)
        if inc is not None:
            E.sem_inc(self.sems[inc[0]], inc[1])

    def ev(self, eng, fn=None, waits=(), inc=None):
        if inc is not None:
            self.cnt[inc[0]] += inc[1]
        if eng != self.eng:
            return
        E = self.E
        for s, v in waits:
            if v > 0:
                E.wait_ge(self.sems[s], v)
        if fn is not None:
            ins = fn(E)
            if inc is not None and ins is not None:
                ins.then_inc(self.sems[inc[0]], inc[1])
        elif inc is not None:
            E.sem_inc(self.sems[inc[0]], inc[1])


def build_schedule(em: Emitter):
    T = em.T
    sb = T["sb"]
    ps = T["ps"]
    C = em.cnt

    const_loads = [
        ("c_f32", "in_cf32"), ("c_shift", "in_cshift"), ("c_id", "in_cid"),
        ("c_ones_col", "in_cones_col"), ("c_ones_row", "in_cones_row"),
        ("c_masks", "in_cmasks"), ("c_iota32", "in_ciota32"), ("c_iota86", "in_ciota86"),
        ("gidx_gu", "in_gidx_gu"), ("gidx_dn", "in_gidx_dn"), ("gidx_dnmy", "in_gidx_dnmy"),
    ]
    for name, key in const_loads:
        em.ev("gpsimd", lambda E, name=name, key=key: E.dma_start(
            out=sb[name][:, :], in_=T[key]), inc=("gc", 16))
    GC = C["gc"]
    em.ev("scalar", None, waits=[("gc", GC)])
    em.ev("vector", None, waits=[("gc", GC)])

    def transpose_pipeline(n_tiles, width, src_fn, store_fn, store_sem):
        cast_v = PIPE["cast_v"]
        tile_t = PIPE["tile_t"]
        evac_v = PIPE["evac_v"]
        store = PIPE["store"]
        ngr = width // 1024
        for i_ in range(n_tiles):
            i = PIPE["n"]
            PIPE["n"] += 1
            ldb = f"ld32_{i % 2}"
            cbb = f"castb_{i % 2}"
            stg = f"tstage_{i % 2}"
            w = [("vp", cast_v[i - 2])] if i - 2 in cast_v else []
            dxs = f"dx{i % 2}"
            em.ev("sync", lambda E, i_=i_, ldb=ldb: E.dma_start(
                out=sb[ldb][:, 0:width], in_=src_fn(i_)), waits=w, inc=(dxs, 16))
            dxn = C[dxs]
            w = [(dxs, dxn)]
            if i - 2 in tile_t:
                w.append(("tp", tile_t[i - 2]))
            em.ev("vector", lambda E, ldb=ldb, cbb=cbb: E.tensor_copy(
                sb[cbb][:, 0:width], sb[ldb][:, 0:width]), waits=w, inc=("vp", 1))
            cast_v[i] = C["vp"]
            for g in range(ngr):
                gg = PIPE["grp"]
                PIPE["grp"] += 1
                tb = f"T{gg % 2}"
                w = [("vp", cast_v[i]), ("gc", GC)]
                if gg - 2 in evac_v:
                    w.append(("vp", evac_v[gg - 2]))
                for k in range(8):
                    ht = 8 * g + k
                    em.ev("tensor", lambda E, cbb=cbb, ht=ht, k=k, tb=tb: E.transpose(
                        ps[tb][:, 128 * k:128 * (k + 1)],
                        sb[cbb][:, 128 * ht:128 * (ht + 1)], sb["c_id"][:, :]),
                        waits=w if k == 0 else [],
                        inc=("tp", 1) if k == 7 else None)
                w = [("tp", C["tp"])]
                if i - 2 in store:
                    w.append(store[i - 2])
                em.ev("vector", lambda E, tb=tb, stg=stg, g=g: E.tensor_copy(
                    sb[stg][:, 1024 * g:1024 * (g + 1)], ps[tb][:, :]),
                    waits=w, inc=("vp", 1))
                evac_v[gg] = C["vp"]
            tile_t[i] = C["tp"]
            ssem = f"{store_sem}{i % 2}"
            store_fn(i_, stg, evac_v[PIPE["grp"] - 1], ssem)
            store[i] = (ssem, C[ssem])

    PIPE = {"n": 0, "cast_v": {}, "tile_t": {}, "evac_v": {}, "store": {}, "grp": 0}
    xcast_v = {}
    xst = {}
    for i in range(8):
        ldb = f"ld32_{i % 2}"
        cbb = f"castb_{i % 2}"
        w = [("vp", xcast_v[i - 2])] if i - 2 in xcast_v else []
        dxs = f"dx{i % 2}"
        em.ev("sync", lambda E, i=i, ldb=ldb: E.dma_start(
            out=sb[ldb][:, :], in_=T["in_xseq"][128 * i:128 * (i + 1), :]), waits=w, inc=(dxs, 16))
        dxn = C[dxs]
        w = [(dxs, dxn)]
        if i - 2 in xst:
            w.append(xst[i - 2])
        em.ev("vector", lambda E, ldb=ldb, cbb=cbb: E.tensor_copy(
            sb[cbb][:, :], sb[ldb][:, :]), waits=w, inc=("vp", 1))
        xcast_v[i] = C["vp"]
        ssem = f"sxw{i % 2}"
        em.ev("scalar", lambda E, i=i, cbb=cbb: E.dma_start(
            out=T["xbf_in"][128 * i:128 * (i + 1), :], in_=sb[cbb][:, :]),
            waits=[("vp", xcast_v[i])], inc=(ssem, 16))
        xst[i] = (ssem, C[ssem])
    em.ev("gpsimd", lambda E: E.collective_compute(
        "AllGather", ALU.bypass, replica_groups=[[0, 1, 2, 3], [4, 5, 6, 7]],
        ins=[T["xbf_in"].opt()], outs=[T["xbf_all"].opt()]),
        waits=[("sxw0", C["sxw0"]), ("sxw1", C["sxw1"])], inc=("cc", 1))
    PIPE["cast_v"][-2] = xcast_v[6]
    PIPE["cast_v"][-1] = xcast_v[7]
    em.ev("vector", None, waits=[xst[6], xst[7]])

    def wq_store(i, stg, vv, ssem):
        em.ev("scalar", lambda E, i=i, stg=stg: E.dma_start(
            out=T["wqT_priv"][128 * i:128 * (i + 1), :], in_=sb[stg][:, :]),
            waits=[("vp", vv)], inc=(ssem, 16))

    transpose_pipeline(12, 4096, lambda i: T["in_wqkv"][128 * i:128 * (i + 1), :], wq_store, "sww")
    SWW_WQ = [("sww0", C["sww0"]), ("sww1", C["sww1"])]

    def wd_store(i, stg, vv, ssem):
        em.ev("scalar", lambda E, i=i, stg=stg: E.dma_start(
            out=T["wdT_priv"].rearrange("(at p) c -> p at c", at=8)[:, :, 128 * i:128 * (i + 1)],
            in_=sb[stg][:, 0:1024].rearrange("p (at w) -> p at w", at=8)),
            waits=[("vp", vv)], inc=(ssem, 16))

    transpose_pipeline(32, 1024, lambda i: T["in_wdense"][128 * i:128 * (i + 1), :], wd_store, "sww")
    SWW_ALL = [("sww0", C["sww0"]), ("sww1", C["sww1"])]

    em.evsync("vector", lambda E: E.tensor_copy(sb["gidx_gu_f"][:, :], sb["gidx_gu"][:, :]))
    em.evsync("vector", lambda E: E.tensor_copy(sb["gidx_dn_f"][:, :], sb["gidx_dn"][:, :]))
    for ht in range(32):
        em.ev("vector", lambda E, ht=ht: E.tensor_scalar(
            out=sb["ind32"][:, 32 * ht:32 * (ht + 1)], in0=sb["c_iota32"][:, :],
            scalar1=sb["gidx_gu_f"][:, ht:ht + 1], scalar2=None, op0=ALU.is_equal))
    em.ev("vector", None, inc=("vp", 1))
    V_IND32 = C["vp"]

    dq_vec_jobs = []
    n_dq = 0
    for kind in ("gu", "dn"):
        n_tiles = 32 if kind == "gu" else DN_NSLOT
        w_ = GU_FS if kind == "gu" else 4096
        for ti in range(n_tiles):
            gw = [("gc", GC)]
            if n_dq >= 1:
                gw.append(("vdq", n_dq))
            if kind == "gu":
                src = T["in_qwgu"][16 * ti:16 * (ti + 1), :].rearrange(
                    "r (o n) -> r o n", o=1).broadcast_to([16, 8, w_])
                tbl, idx = T["in_scgu"], sb["gidx_gu"][:, ti:ti + 1]
            else:
                src = T["in_qwdn"][16 * ti:16 * (ti + 1), :].rearrange(
                    "r (o n) -> r o n", o=1).broadcast_to([16, 8, w_])
                tbl, idx = T["in_scdn"], sb["gidx_dnmy"][:, ti:ti + 1]
            em.ev("gpsimd", lambda E, src=src, w_=w_: E.dma_start(
                out=sb["drep"][:, 0:w_], in_=src), waits=gw, inc=("gdq", 16))
            em.ev("gpsimd", lambda E, tbl=tbl, idx=idx, w_=w_: E.indirect_dma_start(
                out=sb["dgath"][:, 0:w_], out_offset=None, in_=tbl,
                in_offset=IndirectOffsetOnAxis(ap=idx, axis=0)), inc=("gdq", 16))
            gdqn = C["gdq"]
            vw_ = [("gdq", gdqn)]
            if n_dq >= 1:
                vw_.append(("gdw", 16 * n_dq))
            def dq_math(vw_=vw_, w_=w_):
                em.ev("vector", lambda E, w_=w_: E.tensor_scalar(
                    out=sb["dnib"][:, 0:w_], in0=sb["drep"][:, 0:w_],
                    scalar1=sb["c_shift"][:, :], scalar2=15,
                    op0=ALU.logical_shift_right, op1=ALU.bitwise_and), waits=vw_)
                em.ev("vector", lambda E, w_=w_: E.tensor_mul(
                    sb["dqout"][:, 0:w_], sb["dnib"][:, 0:w_], sb["dgath"][:, 0:w_]),
                    inc=("vdq", 1))
            dq_vec_jobs.append(dq_math)
            n_dq += 1
            if kind == "gu":
                em.ev("gpsimd", lambda E, ti=ti: E.dma_start(
                    out=T["gu_in"].rearrange("(s p) c -> p s c", s=GU_NSLOT)[:, :, 128 * ti:128 * (ti + 1)],
                    in_=sb["dqout"][:, 0:GU_FS].rearrange("p (s f) -> p s f", s=GU_NSLOT)),
                    waits=[("vdq", n_dq)], inc=("gdw", 16))
            else:
                em.ev("gpsimd", lambda E, ti=ti: E.dma_start(
                    out=T["dn_in"][128 * ti:128 * (ti + 1), :], in_=sb["dqout"][:, 0:4096]),
                    waits=[("vdq", n_dq)], inc=("gdw", 16))
    GDW_ALL = C["gdw"]
    em.ev("gpsimd", lambda E: E.collective_compute(
        "AllGather", ALU.bypass, replica_groups=[list(range(8))],
        ins=[T["gu_in"].opt()], outs=[T["gu_all"].opt()]),
        waits=[("gdw", GDW_ALL)], inc=("cc", 1))
    em.ev("gpsimd", lambda E: E.collective_compute(
        "AllGather", ALU.bypass, replica_groups=[list(range(8))],
        ins=[T["dn_in"].opt()], outs=[T["dn_all"].opt()]),
        inc=("cc", 1))

    for vb in range(2):
        em.ev("sync", lambda E, vb=vb: E.dma_start(
            out=sb["vw"].ap().rearrange("p (ht v) -> p ht v", v=256)[:, :, 128 * vb:128 * (vb + 1)],
            in_=T["wqT_priv"][128 * (10 + vb):128 * (11 + vb), :].rearrange("p (ht o) -> p ht o", ht=32)),
            waits=list(SWW_WQ), inc=(f"dwq{vb}", 16))
    VW_DONE = [("dwq0", C["dwq0"]), ("dwq1", C["dwq1"])]
    mm_done = {}
    ev_done = {}
    for tc in range(8):
        rkl = tc // 2
        colo = 512 * (tc % 2)
        xtb = f"xt_{tc % 2}"
        w = [("cc", 1)]
        if (tc - 2, 13) in mm_done:
            w.append(("tp", mm_done[(tc - 2, 13)]))
        dxts = f"dxt{tc % 2}"
        for ht in range(32):
            em.ev("sync", lambda E, tc=tc, ht=ht, xtb=xtb: E.dma_start_transpose(
                sb[xtb][:, 512 * ht:512 * (ht + 1)],
                T["xbf_all"][512 * tc:512 * (tc + 1), 128 * ht:128 * (ht + 1)]),
                waits=w if ht == 0 else [], inc=(dxts, 16))
        dxtn = (dxts, C[dxts])
        for ot in range(10):
            wqb = f"wq_{ot % 2}"
            w = list(SWW_WQ)
            prev = (tc, ot - 2) if ot >= 2 else ((tc - 1, 8 + ot) if tc >= 1 else None)
            if prev in mm_done:
                w.append(("tp", mm_done[prev]))
            dwqs = f"dwq{ot % 2}"
            em.ev("sync", lambda E, ot=ot, wqb=wqb: E.dma_start(
                out=sb[wqb][:, :], in_=T["wqT_priv"][128 * ot:128 * (ot + 1), :]),
                waits=w, inc=(dwqs, 16))
            dwqn = (dwqs, C[dwqs])
            fb = f"F{ot % 2}"
            bw = [dxtn, dwqn]
            prev = (tc, ot - 2) if ot >= 2 else ((tc - 1, 8 + ot) if tc >= 1 else None)
            if prev in ev_done:
                bw.append(("vp", ev_done[prev]))
            for k in range(32):
                em.ev("tensor", lambda E, fb=fb, wqb=wqb, xtb=xtb, k=k: E.matmul(
                    ps[fb][:, :], sb[wqb][:, 128 * k:128 * (k + 1)],
                    sb[xtb][:, 512 * k:512 * (k + 1)], start=(k == 0), stop=(k == 31)),
                    waits=bw if k == 0 else [],
                    inc=("tp", 1) if k == 31 else None)
            mm_done[(tc, ot)] = C["tp"]
            if ot < 8:
                qsb = f"qstage_{ot % 2}"
                sqs = f"sq{ot % 2}"
                em.ev("vector", lambda E, fb=fb, qsb=qsb: E.tensor_copy(sb[qsb][:, :], ps[fb][:, :]),
                      waits=[("tp", mm_done[(tc, ot)]), (sqs, C[sqs])], inc=("vp", 1))
                ev_done[(tc, ot)] = C["vp"]
                em.ev("scalar", lambda E, qsb=qsb, ot=ot, tc=tc: E.dma_start(
                    out=T["qT_priv"][128 * ot:128 * (ot + 1), 512 * tc:512 * (tc + 1)],
                    in_=sb[qsb][:, :]),
                    waits=[("vp", ev_done[(tc, ot)])], inc=(sqs, 16))
            else:
                u = ot - 8
                em.ev("vector", lambda E, fb=fb, u=u, tc=tc: E.tensor_copy(
                    sb["kT"][:, 4096 * u + 512 * tc:4096 * u + 512 * (tc + 1)], ps[fb][:, :]),
                    waits=[("tp", mm_done[(tc, ot)])], inc=("vp", 1))
                ev_done[(tc, ot)] = C["vp"]
        for vt in range(4):
            fb = f"F{2 + vt % 2}"
            bw = [dxtn] + VW_DONE
            prev = (tc, 10 + vt - 2) if vt >= 2 else ((tc - 1, 12 + vt) if tc >= 1 else None)
            if prev in ev_done:
                bw.append(("vp", ev_done[prev]))
            for k in range(32):
                em.ev("tensor", lambda E, fb=fb, xtb=xtb, vt=vt, k=k: E.matmul(
                    ps[fb][:, 0:256], sb[xtb][:, 512 * k + 128 * vt:512 * k + 128 * (vt + 1)],
                    sb["vw"][:, 256 * k:256 * (k + 1)], start=(k == 0), stop=(k == 31)),
                    waits=bw if k == 0 else [],
                    inc=("tp", 1) if k == 31 else None)
            mm_done[(tc, 10 + vt)] = C["tp"]
            ttb = 4 * tc + vt
            em.ev("vector", lambda E, fb=fb, ttb=ttb: E.tensor_copy(
                sb["v"][:, 256 * ttb:256 * (ttb + 1)], ps[fb][:, 0:256]),
                waits=[("tp", mm_done[(tc, 10 + vt)])], inc=("vp", 1))
            ev_done[(tc, 10 + vt)] = C["vp"]
        for _ in range(6):
            if dq_vec_jobs:
                dq_vec_jobs.pop(0)()
    while dq_vec_jobs:
        dq_vec_jobs.pop(0)()
    V_P1_END = C["vp"]
    SQ_ALL = [("sq0", C["sq0"]), ("sq1", C["sq1"])]

    strip_t = {}
    qt_loaded = {}
    expn = 0
    tail_tt = {}
    tail_rec = {}
    tail_cp = {}
    sc_pw = {}

    TL = []
    n = -1
    for u in range(2):
        for qh in range(4):
            sidx = 4 * u + qh
            for sqc in range(8):
                n += 1
                ntk = 4 * (sqc + 1)
                for tk in range(ntk):
                    TL.append(dict(n=n, u=u, sidx=sidx, sqc=sqc, tk=tk, ntk=ntk,
                                   par=n % 2, fa="F3" if n % 2 == 0 else "F4",
                                   dslc=(0, 1) if n % 2 == 0 else (32, 33)))

    def ensure_strip(sidx):
        if sidx in qt_loaded:
            return
        qtb = f"qt_{sidx % 2}"
        w = list(SQ_ALL) + [("gdw", GDW_ALL)]
        if sidx - 2 in strip_t:
            w.append(("tp", strip_t[sidx - 2]))
        dqts = f"dqt{sidx % 2}"
        em.ev("sync", lambda E, sidx=sidx, qtb=qtb: E.dma_start(
            out=sb[qtb][:, :], in_=T["qT_priv"][128 * sidx:128 * (sidx + 1), :]),
            waits=w, inc=(dqts, 16))
        qt_loaded[sidx] = (dqts, C[dqts])

    def emit_sc(gi):
        nonlocal expn
        t = TL[gi]
        ensure_strip(t["sidx"])
        fb = f"F{gi % 2}"
        pb = f"probs_{gi % 3}"
        qtb = f"qt_{t['sidx'] % 2}"
        sw = [qt_loaded[t["sidx"]], ("vp", V_P1_END)]
        if expn - 1 > 0:
            sw.append(("ap", expn - 1))
        em.ev("tensor", lambda E, fb=fb, t=t, qtb=qtb: E.matmul(
            ps[fb][:, :], sb["kT"][:, 4096 * t["u"] + 128 * t["tk"]:4096 * t["u"] + 128 * (t["tk"] + 1)],
            sb[qtb][:, 512 * t["sqc"]:512 * (t["sqc"] + 1)], start=True, stop=True),
            waits=sw, inc=("tp", 1))
        tsc = C["tp"]
        em.ev("scalar", lambda E, fb=fb, pb=pb: E.activation(
            out=sb[pb][:, :], in_=ps[fb][:, :], func=AF.Exp,
            bias=EXP_BIAS, scale=EXP_SCALE),
            waits=[("tp", tsc)], inc=("ap", 1))
        expn = C["ap"]
        if t["tk"] >= t["ntk"] - 4:
            dpos = t["tk"] - (t["ntk"] - 4)
            em.ev("vector", lambda E, pb=pb, dpos=dpos: E.tensor_mul(
                sb[pb][:, :], sb[pb][:, :], sb["c_masks"][:, 512 * dpos:512 * (dpos + 1)]),
                waits=[("ap", expn)], inc=("vp", 1))
            sc_pw[gi] = ("vp", C["vp"])
        else:
            sc_pw[gi] = ("ap", expn)

    def emit_avden(gi):
        t = TL[gi]
        n = t["n"]
        pb = f"probs_{gi % 3}"
        f_ = t["tk"] == 0
        l_ = t["tk"] == t["ntk"] - 1
        avw = [sc_pw.pop(gi)]
        if f_ and n - 2 in tail_tt:
            avw.append(("vp", tail_tt[n - 2]))
        em.ev("tensor", lambda E, t=t, pb=pb, f_=f_, l_=l_: E.matmul(
            ps[t["fa"]][:, :], sb["v"][:, 256 * t["tk"] + 128 * t["u"]:256 * t["tk"] + 128 * (t["u"] + 1)],
            sb[pb][:, :], start=f_, stop=l_),
            waits=avw, inc=("tp", 1))
        dw = []
        if f_ and n - 2 in tail_rec:
            dw.append(("vp", tail_rec[n - 2]))
        em.ev("tensor", lambda E, t=t, pb=pb, f_=f_, l_=l_: E.matmul(
            ps["F5"][t["dslc"][0]:t["dslc"][1], :], sb["c_ones_col"][:, :], sb[pb][:, :],
            start=f_, stop=l_,
            tile_position=(0, 32) if t["par"] == 1 else None),
            waits=dw, inc=("tp", 1))
        if not l_:
            return
        tden = C["tp"]
        par = t["par"]
        sidx = t["sidx"]
        sqc = t["sqc"]
        dslc = t["dslc"]
        fa = t["fa"]

        def _recip(E, dslc=dslc, par=par):
            with E.bass.allow_low_precision(reason="bf16 softmax denom"):
                return E.reciprocal(sb["rden"][0:1, 512 * par:512 * (par + 1)],
                                    ps["F5"][dslc[0]:dslc[1], :])
        em.evsync("vector", _recip, waits=[("tp", tden)], inc=("vp", 1))
        tail_rec[n] = C["vp"]
        bw = []
        if n - 1 in tail_cp:
            bw.append(("ap", tail_cp[n - 1]))
        em.ev("tensor", lambda E, par=par: E.matmul(
            ps["F2"][:, :], sb["c_ones_row"][:, :],
            sb["rden"][0:1, 512 * par:512 * (par + 1)], start=True, stop=True),
            waits=[("vp", tail_rec[n])] + bw, inc=("tp", 1))
        tbc = C["tp"]
        bcb = "bcb" if par == 0 else "bcb2"
        em.ev("scalar", lambda E, bcb=bcb: E.copy(sb[bcb][:, :], ps["F2"][:, :]),
              waits=[("tp", tbc)], inc=("ap", 1))
        tail_cp[n] = C["ap"]
        ats = f"atst_{sidx % 2}"
        sats = f"sat{sidx % 2}"
        ttw = [("ap", tail_cp[n])]
        if sqc == 0 and C[sats] > 0:
            ttw.append((sats, C[sats]))
        em.ev("vector", lambda E, ats=ats, sqc=sqc, fa=fa, bcb=bcb: E.tensor_mul(
            sb[ats][:, 512 * sqc:512 * (sqc + 1)], ps[fa][:, :], sb[bcb][:, :]),
            waits=ttw, inc=("vp", 1))
        tail_tt[n] = C["vp"]
        if sqc == 7:
            em.ev("scalar", lambda E, sidx=sidx, ats=ats: E.dma_start(
                out=T["attnT_priv"][128 * sidx:128 * (sidx + 1), :], in_=sb[ats][:, :]),
                waits=[("vp", tail_tt[n])], inc=(sats, 16))
            strip_t[sidx] = C["tp"]

    NTILES = len(TL)
    emit_sc(0)
    emit_sc(1)
    for gi in range(NTILES):
        if gi + 2 < NTILES:
            emit_sc(gi + 2)
        emit_avden(gi)
    V_P2_END = C["vp"]
    TP2_END = C["tp"]
    SAT_ALL = [("sat0", C["sat0"]), ("sat1", C["sat1"])]
    for nm in ("kT", "v", "attnT"):
        if nm in T.get("dbg", {}):
            em.ev("sync", lambda E, nm=nm: E.dma_start(out=T["dbg"][nm], in_=sb[nm][:, :]),
                  waits=[("vp", V_P2_END)], inc=("dr3", 16))
    if any(nm in T.get("dbg", {}) for nm in ("kT", "v", "attnT")):
        em.ev("sync", None, waits=[("dr3", C["dr3"])])

    tile_t3 = {}
    KEY3 = {}
    spp_g = {}
    nps = -1
    for g in range(4):
        datg = f"dat{g % 2}"
        aw = list(SAT_ALL)
        if g == 0:
            aw.append(("vp", V_P2_END))
        if g - 2 >= 0:
            aw.append(("vp", tile_t3[(g - 2, 7, 7)]))
        em.ev("sync", lambda E, g=g: E.dma_start(
            out=sb[f"at3_{g % 2}"].ap().rearrange("p (s c) -> p s c", s=8),
            in_=T["attnT_priv"].rearrange("(s p) c -> p s c", s=8)[:, :, 1024 * g:1024 * (g + 1)]),
            waits=aw, inc=(datg, 16))
        datn = (datg, C[datg])
        for hc in range(8):
            wdb = f"wd3_{hc % 2}"
            w = list(SWW_ALL)
            if g == 0 and hc == 0:
                w.append(("vp", V_P2_END))
            prev = (g, hc - 2) if hc >= 2 else ((g - 1, hc + 6) if g >= 1 else None)
            if prev is not None and (prev[0], prev[1], 7) in tile_t3:
                w.append(("vp", tile_t3[(prev[0], prev[1], 7)]))
            dwds = f"dwd{(8 * g + hc) % 2}"
            em.ev("sync", lambda E, hc=hc, wdb=wdb: E.dma_start(
                out=sb[wdb].ap().rearrange("p (at w) -> p at w", at=8),
                in_=T["wdT_priv"].rearrange("(at p) c -> p at c", at=8)[:, :, 512 * hc:512 * (hc + 1)]),
                waits=w, inc=(dwds, 16))
            dwdn_ = (dwds, C[dwds])
            for tl in range(8):
                nps += 1
                tt_b = 8 * g + tl
                fb = f"F{nps % 2}"
                bw = [dwdn_, datn]
                if nps >= 2:
                    bw.append(("vp", tile_t3[KEY3[nps - 2]]))
                KEY3[nps] = (g, hc, tl)
                for j in range(8):
                    em.ev("tensor", lambda E, fb=fb, j=j, tl=tl, wdb=wdb, g=g: E.matmul(
                        ps[fb][:, :], sb[f"at3_{g % 2}"][:, 1024 * j + 128 * tl:1024 * j + 128 * (tl + 1)],
                        sb[wdb][:, 512 * j:512 * (j + 1)], start=(j == 0), stop=(j == 7)),
                        waits=bw if j == 0 else [],
                        inc=("tp", 1) if j == 7 else None)
                tpn = C["tp"]
                dsb = f"dstage_{nps % 2}"
                em.ev("vector", lambda E, fb=fb, dsb=dsb: E.tensor_copy(sb[dsb][:, :], ps[fb][:, :]),
                      waits=[("tp", tpn), ("spp", C["spp"])], inc=("vp", 1))
                tile_t3[(g, hc, tl)] = C["vp"]
                em.ev("scalar", lambda E, dsb=dsb, tt_b=tt_b, hc=hc: E.dma_start(
                    out=T["partial_priv"][128 * tt_b:128 * (tt_b + 1), 512 * hc:512 * (hc + 1)],
                    in_=sb[dsb][:, :]),
                    waits=[("vp", tile_t3[(g, hc, tl)])], inc=("spp", 16))
        spp_g[g] = C["spp"]
        em.ev("gpsimd", lambda E, g=g: E.collective_compute(
            "ReduceScatter", ALU.add, replica_groups=[[0, 1, 2, 3], [4, 5, 6, 7]],
            ins=[T["partial_priv"][1024 * g:1024 * (g + 1), :].opt()],
            outs=[T["rs_out"][256 * g:256 * (g + 1), :].opt()]),
            waits=[("spp", spp_g[g])], inc=("cc", 1))
    TP3_END = C["tp"]

    ap_x2 = {}
    sx2_n = {}
    for tt in range(NT):
        g = tt // 2
        for hc in range(8):
            rsb = f"rstile_{hc % 2}"
            xrb = f"xres_{hc % 2}"
            w = [("cc", 4 + g)]
            if hc >= 2 or tt > 0:
                w.append(("vp", C["vp"]))
            drss = f"drs{hc % 2}"
            em.ev("sync", lambda E, tt=tt, hc=hc, rsb=rsb: E.dma_start(
                out=sb[rsb][:, :], in_=T["rs_out"][128 * tt:128 * (tt + 1), 512 * hc:512 * (hc + 1)]),
                waits=w, inc=(drss, 16))
            em.ev("sync", lambda E, tt=tt, hc=hc, xrb=xrb: E.dma_start(
                out=sb[xrb][:, :], in_=T["in_x"][128 * tt:128 * (tt + 1), 512 * hc:512 * (hc + 1)]),
                inc=(drss, 16))
            aw = [(drss, C[drss])]
            if hc == 0 and tt - 1 in ap_x2:
                aw.append(("ap", ap_x2[tt - 1]))
            em.ev("vector", lambda E, rsb=rsb, xrb=xrb, hc=hc: E.tensor_add(
                sb["r_tt"][:, 512 * hc:512 * (hc + 1)], sb[rsb][:, :], sb[xrb][:, :]), waits=aw)
            col = 8 * tt + hc
            em.evsync("vector", lambda E, hc=hc, col=col: E.reduce_sum(
                sb["sums"][:, col:col + 1], sb["r_tt"][:, 512 * hc:512 * (hc + 1)], axis=AXX),
                inc=("vp", 1))
            vn = C["vp"]
            em.evsync("scalar", lambda E, hc=hc, col=col: E.activation(
                out=sb["sq_junk"][:, :], in_=sb["r_tt"][:, 512 * hc:512 * (hc + 1)],
                func=AF.Square, accum_out=sb["sumsq"][:, col:col + 1]),
                waits=[("vp", vn)], inc=("ap", 1))
        apsq = C["ap"]
        em.evsync("vector", lambda E, tt=tt: E.reduce_sum(
            sb["redt"][:, 0:1], sb["sums"][:, 8 * tt:8 * (tt + 1)], axis=AXX),
            waits=[("ap", apsq)])
        em.evsync("vector", lambda E, tt=tt: E.reduce_sum(
            sb["redt"][:, 1:2], sb["sumsq"][:, 8 * tt:8 * (tt + 1)], axis=AXX))
        em.evsync("vector", lambda E: E.tensor_scalar_mul(sb["redt"][:, 2:3], sb["redt"][:, 0:1], INV_H))
        em.evsync("vector", lambda E: E.tensor_scalar_mul(sb["redt"][:, 1:2], sb["redt"][:, 1:2], INV_H))
        em.evsync("vector", lambda E: E.tensor_mul(sb["redt"][:, 0:1], sb["redt"][:, 2:3], sb["redt"][:, 2:3]))
        em.evsync("vector", lambda E: E.tensor_sub(sb["redt"][:, 1:2], sb["redt"][:, 1:2], sb["redt"][:, 0:1]),
              inc=("vp", 1))
        vvar = C["vp"]
        em.evsync("scalar", lambda E: E.activation(out=sb["redt"][:, 3:4], in_=sb["redt"][:, 1:2],
                                                   func=AF.Ln, bias=LN_EPS), waits=[("vp", vvar)])
        em.evsync("scalar", lambda E: E.activation(out=sb["rstd"][:, :], in_=sb["redt"][:, 3:4],
                                                   func=AF.Exp, scale=-0.5), inc=("ap", 1))
        aprs = C["ap"]
        em.evsync("vector", lambda E: E.tensor_scalar(
            out=sb["lnb"][:, :], in0=sb["redt"][:, 2:3], scalar1=sb["rstd"][:, :],
            scalar2=-1.0, op0=ALU.mult, op1=ALU.mult),
            waits=[("ap", aprs)], inc=("vp", 1))
        vlnb = C["vp"]
        aw = [("vp", vlnb)]
        if tt - 1 in sx2_n:
            aw.append(("sx2", sx2_n[tt - 1]))
        em.ev("scalar", lambda E: E.activation(
            out=sb["x2sb"][:, :], in_=sb["r_tt"][:, :], func=AF.Identity,
            scale=sb["rstd"][:, :], bias=sb["lnb"][:, :]), waits=aw, inc=("ap", 1))
        ap_x2[tt] = C["ap"]
        em.ev("scalar", lambda E, tt=tt: E.dma_start(
            out=T["x2_priv"][128 * tt:128 * (tt + 1), :], in_=sb["x2sb"][:, :]),
            waits=[("ap", ap_x2[tt])], inc=("sx2", 16))
        sx2_n[tt] = C["sx2"]
        for g in range(4):
            tb = f"T{g % 2}"
            for k in range(8):
                ht = 8 * g + k
                em.ev("tensor", lambda E, ht=ht, k=k, tb=tb: E.transpose(
                    ps[tb][:, 128 * k:128 * (k + 1)], sb["x2sb"][:, 128 * ht:128 * (ht + 1)],
                    sb["c_id"][:, :]),
                    waits=[("ap", ap_x2[tt])] if k == 0 else [],
                    inc=("tp", 1) if k == 7 else None)
            tg = C["tp"]
            em.ev("vector", lambda E, tb=tb, g=g, tt=tt: E.tensor_copy(
                sb["x2T"].ap().rearrange("p (ht t) -> p ht t", ht=32)[:, 8 * g:8 * (g + 1), 128 * tt:128 * (tt + 1)],
                ps[tb].ap().rearrange("p (k t) -> p k t", k=8)),
                waits=[("tp", tg)], inc=("vp", 1))
    V_X2T = C["vp"]
    DXT_ALL = 0
    SX2_ALL = C["sx2"]
    if "x2e" in T.get("dbg", {}):
        em.ev("sync", lambda E: E.dma_start(out=T["dbg"]["x2e"], in_=T["x2_priv"]),
              waits=[("sx2", SX2_ALL)], inc=("dr3", 16))
        em.ev("sync", None, waits=[("dr3", C["dr3"])])
    if "lnst" in T.get("dbg", {}):
        em.ev("sync", lambda E: E.dma_start(out=T["dbg"]["lnst"][:, 0:64], in_=sb["sums"][:, :]),
              waits=[("ap", C["ap"]), ("vp", C["vp"])], inc=("dr3", 16))
        em.ev("sync", lambda E: E.dma_start(out=T["dbg"]["lnst"][:, 64:128], in_=sb["sumsq"][:, :]),
              inc=("dr3", 16))
        em.ev("sync", lambda E: E.dma_start(out=T["dbg"]["lnst"][:, 128:132], in_=sb["redt"][:, :]),
              inc=("dr3", 16))
        def _d1(E):
            with E.bass.allow_non_contiguous_dma(reason="dbg"):
                return E.dma_start(out=T["dbg"]["lnst"][:, 132:133], in_=sb["rstd"][:, :])
        def _d2(E):
            with E.bass.allow_non_contiguous_dma(reason="dbg"):
                return E.dma_start(out=T["dbg"]["lnst"][:, 133:134], in_=sb["lnb"][:, :])
        em.ev("sync", _d1, inc=("dr3", 16))
        em.ev("sync", _d2, inc=("dr3", 16))
        em.ev("sync", None, waits=[("dr3", C["dr3"])])

    for tc2 in range(2):
        fb = "F4" if tc2 == 0 else "F5"
        for ht in range(32):
            em.ev("tensor", lambda E, fb=fb, ht=ht, tc2=tc2: E.matmul(
                ps[fb][0:32, :], sb["ind32"][:, 32 * ht:32 * (ht + 1)],
                sb["x2T"][:, 1024 * ht + 512 * tc2:1024 * ht + 512 * (tc2 + 1)],
                start=(ht == 0), stop=(ht == 31)),
                waits=[("vp", V_X2T), ("vp", V_IND32)] if ht == 0 else [],
                inc=("tp", 1) if ht == 31 else None)
        tsg = C["tp"]
        em.ev("vector", lambda E, fb=fb, tc2=tc2: E.tensor_copy(
            sb["S_gu"][:, 512 * tc2:512 * (tc2 + 1)], ps[fb][0:32, :]),
            waits=[("tp", tsg)], inc=("vp", 1))
    V_SGU = C["vp"]
    ZG = [("ap", ap_x2[NT - 1]), ("sx2", C["sx2"]), ("tp", TP3_END)]
    for k in range(16):
        em.ev("sync", lambda E, k=k: E.dma_start(
            out=sb["zqz"][0:32, 0:172], in_=T["in_qzgu"][:, 172 * k:172 * (k + 1)]),
            waits=ZG + [("vp", C["vp"])], inc=("dzn", 16))
        em.ev("sync", lambda E, k=k: E.dma_start(
            out=sb["zsc"][0:32, 0:1376], in_=T["in_scgu_full"][:, 1376 * k:1376 * (k + 1)]),
            inc=("dzn", 16))
        dznn = C["dzn"]
        for j in range(8):
            fj = lambda E, j=j: E.tensor_scalar(
                out=sb["zpl"][0:32, 172 * j:172 * (j + 1)], in0=sb["zqz"][0:32, 0:172],
                scalar1=float(4 * j), scalar2=15,
                op0=ALU.logical_shift_right, op1=ALU.bitwise_and)
            if j == 7:
                em.evsync("vector", fj)
            else:
                em.ev("vector", fj, waits=(ZG + [("dzn", dznn)]) if (j == 0 and k == 0) else ([("dzn", dznn)] if j == 0 else []))
        em.evsync("vector", lambda E, k=k: E.scalar_tensor_tensor(
            out=sb["znegg"][0:32, 1376 * k:1376 * (k + 1)].rearrange("g (c j) -> g c j", j=8),
            in0=sb["zpl"][0:32, 0:1376].rearrange("g (j c) -> g c j", j=8),
            scalar=1.0, in1=sb["zsc"][0:32, 0:1376].rearrange("g (c j) -> g c j", j=8),
            op0=ALU.add, op1=ALU.mult))
        em.evsync("vector", lambda E, k=k: E.tensor_scalar_mul(
            sb["znegg"][0:32, 1376 * k:1376 * (k + 1)], sb["znegg"][0:32, 1376 * k:1376 * (k + 1)], -1.0),
            inc=("vp", 1))
    for half in range(2):
        em.ev("sync", lambda E, half=half: E.dma_start(
            out=sb["zqz"][0:86, 0:256], in_=T["in_qzdn"][:, 256 * half:256 * (half + 1)]),
            waits=ZG + [("vp", C["vp"])], inc=("dzn", 16))
        em.ev("sync", lambda E, half=half: E.dma_start(
            out=sb["zsc"][0:86, 0:2048], in_=T["in_scdn"][:, 2048 * half:2048 * (half + 1)]),
            inc=("dzn", 16))
        dznn = C["dzn"]
        for j in range(8):
            fj = lambda E, j=j: E.tensor_scalar(
                out=sb["zpl"][0:86, 256 * j:256 * (j + 1)], in0=sb["zqz"][0:86, 0:256],
                scalar1=float(4 * j), scalar2=15,
                op0=ALU.logical_shift_right, op1=ALU.bitwise_and)
            if j == 7:
                em.evsync("vector", fj)
            else:
                em.ev("vector", fj, waits=[("dzn", dznn)] if j == 0 else [])
        em.evsync("vector", lambda E, half=half: E.scalar_tensor_tensor(
            out=sb["znegd"][0:86, 2048 * half:2048 * (half + 1)].rearrange("g (c j) -> g c j", j=8),
            in0=sb["zpl"][0:86, 0:2048].rearrange("g (j c) -> g c j", j=8),
            scalar=1.0, in1=sb["zsc"][0:86, 0:2048].rearrange("g (c j) -> g c j", j=8),
            op0=ALU.add, op1=ALU.mult))
        em.evsync("vector", lambda E, half=half: E.tensor_scalar_mul(
            sb["znegd"][0:86, 2048 * half:2048 * (half + 1)],
            sb["znegd"][0:86, 2048 * half:2048 * (half + 1)], -1.0),
            inc=("vp", 1))
    V_ZNEG = C["vp"]

    blk_t = {}
    hT_v = {}
    sil_a = {}
    sht_n = {}
    ind_v = {}
    for i in range(86):
        gr, gs = _gu_block_rs(i)
        ur, us = _gu_block_rs(86 + i)
        grow = 128 * (GU_NSLOT * gr + gs)
        urow = 128 * (GU_NSLOT * ur + us)
        ga = f"gu{2 * (i % 2)}"
        ub = f"gu{2 * (i % 2) + 1}"
        w = [("cc", 2)]
        if i == 0:
            w.append(("tp", TP2_END))
            w.extend(SQ_ALL)
        if i - 2 in blk_t:
            w.append(("tp", blk_t[i - 2]))
        dgus = f"dgu{i % 2}"
        em.ev("sync", lambda E, grow=grow, ga=ga: E.dma_start(
            out=sb[ga][:, :], in_=T["gu_all"][grow:grow + 128, :]), waits=w, inc=(dgus, 16))
        em.ev("sync", lambda E, urow=urow, ub=ub: E.dma_start(
            out=sb[ub][:, :], in_=T["gu_all"][urow:urow + 128, :]), inc=(dgus, 16))
        dgun = (dgus, C[dgus])
        iw = [("vp", V_ZNEG)]
        if i - 2 in blk_t:
            iw.append(("tp", blk_t[i - 2]))
        em.ev("vector", lambda E, i=i: E.tensor_scalar(
            out=sb["ind_d"][:, 86 * (i % 2):86 * (i % 2) + 86], in0=sb["c_iota86"][:, :],
            scalar1=sb["gidx_dn_f"][:, i:i + 1], scalar2=None, op0=ALU.is_equal),
            waits=iw, inc=("vp", 1))
        ind_v[i] = C["vp"]
        for tc2 in range(2):
            fg = f"F{i % 2}"
            fu = f"F{2 + i % 2}"
            bw = [dgun, ("vp", V_ZNEG)]
            if (i - 2, tc2) in hT_v:
                bw.append(("vp", hT_v[(i - 2, tc2)]))
            if (i - 2, tc2) in sil_a:
                bw.append(("ap", sil_a[(i - 2, tc2)]))
            for k in range(32):
                em.ev("tensor", lambda E, fg=fg, ga=ga, k=k, tc2=tc2: E.matmul(
                    ps[fg][:, :], sb[ga][:, 128 * k:128 * (k + 1)],
                    sb["x2T"][:, 1024 * k + 512 * tc2:1024 * k + 512 * (tc2 + 1)],
                    start=(k == 0), stop=False),
                    waits=bw if k == 0 else [])
            em.ev("tensor", lambda E, fg=fg, i=i, tc2=tc2: E.matmul(
                ps[fg][:, :], sb["znegg"][0:32, 128 * i:128 * (i + 1)],
                sb["S_gu"][:, 512 * tc2:512 * (tc2 + 1)], start=False, stop=True),
                waits=[("vp", V_SGU)], inc=("tp", 1))
            for k in range(32):
                em.ev("tensor", lambda E, fu=fu, ub=ub, k=k, tc2=tc2: E.matmul(
                    ps[fu][:, :], sb[ub][:, 128 * k:128 * (k + 1)],
                    sb["x2T"][:, 1024 * k + 512 * tc2:1024 * k + 512 * (tc2 + 1)],
                    start=(k == 0), stop=False))
            em.ev("tensor", lambda E, fu=fu, i=i, tc2=tc2: E.matmul(
                ps[fu][:, :], sb["znegg"][0:32, 128 * (86 + i):128 * (87 + i)],
                sb["S_gu"][:, 512 * tc2:512 * (tc2 + 1)], start=False, stop=True),
                inc=("tp", 1))
            tgu = C["tp"]
            sw = [("tp", tgu)]
            if i == 0 and tc2 == 0:
                sw.extend(SQ_ALL)
            em.ev("scalar", lambda E, fg=fg: E.activation(
                out=sb["g_sb"][:, :], in_=ps[fg][:, :], func=AF.Silu),
                waits=sw, inc=("ap", 1))
            sil_a[(i, tc2)] = C["ap"]
            em.ev("vector", lambda E, fu=fu: E.tensor_copy(sb["u_sb"][:, :], ps[fu][:, :]),
                  waits=[("tp", tgu)] + (SQ_ALL if i == 0 and tc2 == 0 else []))
            hb = f"hT4_{i % 2}"
            hw = [("ap", sil_a[(i, tc2)])]
            if i - 2 in sht_n:
                hw.append(sht_n[i - 2])
            em.ev("vector", lambda E, hb=hb, tc2=tc2: E.tensor_mul(
                sb[hb][:, 512 * tc2:512 * (tc2 + 1)], sb["g_sb"][:, :], sb["u_sb"][:, :]),
                waits=hw, inc=("vp", 1))
            hT_v[(i, tc2)] = C["vp"]
            fsd = "F4" if tc2 == 0 else "F5"
            em.ev("tensor", lambda E, fsd=fsd, i=i, hb=hb, tc2=tc2: E.matmul(
                ps[fsd][0:86, :], sb["ind_d"][:, 86 * (i % 2):86 * (i % 2) + 86],
                sb[hb][:, 512 * tc2:512 * (tc2 + 1)], start=(i == 0), stop=(i == 85)),
                waits=[("vp", hT_v[(i, tc2)]), ("vp", ind_v[i])] +
                      ([("vp", V_SGU)] if i == 0 else []),
                inc=("tp", 1))
        blk_t[i] = C["tp"]
        hb = f"hT4_{i % 2}"
        shts = f"sht{i % 2}"
        em.ev("scalar", lambda E, i=i, hb=hb: E.dma_start(
            out=T["hT_priv"].rearrange("(tt p) c -> p tt c", tt=8)[:, :, 128 * i:128 * (i + 1)],
            in_=sb[hb].ap().rearrange("p (tt t) -> p tt t", tt=8)),
            waits=[("vp", hT_v[(i, 1)])], inc=(shts, 16))
        sht_n[i] = (shts, C[shts])
    TP4_END = C["tp"]
    SHT_ALL = [("sht0", C["sht0"]), ("sht1", C["sht1"])]
    for tc2 in range(2):
        fsd = "F4" if tc2 == 0 else "F5"
        em.ev("vector", lambda E, fsd=fsd, tc2=tc2: E.tensor_copy(
            sb["S_dn"][:, 512 * tc2:512 * (tc2 + 1)], ps[fsd][0:86, :]),
            waits=[("tp", TP4_END)], inc=("vp", 1))
    V_SDN = C["vp"]

    ch5_t = {}
    tile_t5 = []
    ev5 = {}
    v_p5 = C["vp"]
    add_v = {}
    for hc in range(8):
        w = [("cc", 3), ("vp", V_SDN)] + list(SHT_ALL)
        if hc - 1 in ch5_t:
            w.append(("tp", ch5_t[hc - 1]))
        dwdns = f"dwdn{hc % 2}"
        em.ev("sync", lambda E, hc=hc: E.dma_start(
            out=sb["wd5"][:, 0:76 * 512].rearrange("p (g w) -> p g w", g=76),
            in_=T["dn_all"].rearrange("(g p) c -> p g c", g=88)[:, 0:76, 512 * hc:512 * (hc + 1)]),
            waits=w, inc=(dwdns, 16))
        em.ev("sync", lambda E, hc=hc: E.dma_start(
            out=sb["wd5"][:, 76 * 512:86 * 512].rearrange("p (g w) -> p g w", g=10),
            in_=T["dn_all"].rearrange("(g p) c -> p g c", g=88)[:, 77:87, 512 * hc:512 * (hc + 1)]),
            inc=(dwdns, 16))
        dwdnn = (dwdns, C[dwdns])
        for tt in range(NT):
            n = 8 * hc + tt
            h5 = f"hT5_{tt % 2}"
            w = list(SHT_ALL)
            if n >= 2:
                w.append(("tp", tile_t5[n - 2]))
            dhts = f"dht{n % 2}"
            em.ev("sync", lambda E, tt=tt, h5=h5: E.dma_start(
                out=sb[h5][:, :], in_=T["hT_priv"][128 * tt:128 * (tt + 1), :]),
                waits=w, inc=(dhts, 16))
            x2b = f"x2t_{tt % 2}"
            dx2s = f"dx2{n % 2}"
            xw = [("sx2", SX2_ALL), ("vp", V_SDN)]
            if n - 2 in add_v:
                xw.append(("vp", add_v[n - 2]))
            em.ev("sync", lambda E, tt=tt, hc=hc, x2b=x2b: E.dma_start(
                out=sb[x2b][:, :], in_=T["x2_priv"][128 * tt:128 * (tt + 1), 512 * hc:512 * (hc + 1)]),
                waits=xw, inc=(dx2s, 16))
            dhtn = (dhts, C[dhts])
            dx2n = (dx2s, C[dx2s])
            fb = f"F{n % 2}"
            bw = [dwdnn, dhtn]
            if n - 2 in ev5:
                bw.append(("vp", ev5[n - 2]))
            for g in range(86):
                em.ev("tensor", lambda E, fb=fb, g=g, h5=h5: E.matmul(
                    ps[fb][:, :], sb[h5][:, 128 * g:128 * (g + 1)],
                    sb["wd5"][:, 512 * g:512 * (g + 1)], start=(g == 0), stop=False),
                    waits=bw if g == 0 else [])
            em.ev("tensor", lambda E, fb=fb, tt=tt, hc=hc: E.matmul(
                ps[fb][:, :], sb["S_dn"][:, 128 * tt:128 * (tt + 1)],
                sb["znegd"][0:86, 512 * hc:512 * (hc + 1)], start=False, stop=True),
                inc=("tp", 1))
            tile_t5.append(C["tp"])
            r3b = f"r3st_{tt % 2}"
            em.ev("vector", lambda E, fb=fb, r3b=r3b, x2b=x2b: E.tensor_add(
                sb[r3b][:, :], ps[fb][:, :], sb[x2b][:, :]),
                waits=[("tp", tile_t5[-1]), dx2n, ("sr3", C["sr3"])])
            add_v[n] = C["vp"] + 1
            col = 8 * tt + hc
            em.evsync("vector", lambda E, r3b=r3b, col=col: E.reduce_sum(
                sb["sums"][:, col:col + 1], sb[r3b][:, :], axis=AXX), inc=("vp", 1))
            v_p5 = C["vp"]
            ev5[n] = C["vp"]
            em.evsync("scalar", lambda E, r3b=r3b, col=col: E.activation(
                out=sb["sq_junk"][:, :], in_=sb[r3b][:, :], func=AF.Square,
                accum_out=sb["sumsq"][:, col:col + 1]),
                waits=[("vp", v_p5)], inc=("ap", 1))
            em.ev("scalar", lambda E, r3b=r3b, tt=tt, hc=hc: E.dma_start(
                out=T["r3_priv"][128 * tt:128 * (tt + 1), 512 * hc:512 * (hc + 1)],
                in_=sb[r3b][:, :]),
                inc=("sr3", 16))
        ch5_t[hc] = C["tp"]
    SR3_ALL = C["sr3"]
    TP5_END = C["tp"]

    sout_n = {}
    apout_n = {}
    for tt in range(NT):
        w = [("sr3", SR3_ALL)]
        if tt == 0:
            w.append(("tp", TP5_END))
        if tt - 1 in apout_n:
            w.append(("ap", apout_n[tt - 1]))
        em.ev("sync", lambda E, tt=tt: E.dma_start(
            out=sb["r3tt"][:, :], in_=T["r3_priv"][128 * tt:128 * (tt + 1), :]),
            waits=w, inc=("dr3", 16))
        dr3n = C["dr3"]
        em.evsync("vector", lambda E, tt=tt: E.reduce_sum(
            sb["redt"][:, 0:1], sb["sums"][:, 8 * tt:8 * (tt + 1)], axis=AXX),
            waits=[("dr3", dr3n), ("ap", C["ap"])])
        em.evsync("vector", lambda E, tt=tt: E.reduce_sum(
            sb["redt"][:, 1:2], sb["sumsq"][:, 8 * tt:8 * (tt + 1)], axis=AXX))
        em.evsync("vector", lambda E: E.tensor_scalar_mul(sb["redt"][:, 2:3], sb["redt"][:, 0:1], INV_H))
        em.evsync("vector", lambda E: E.tensor_scalar_mul(sb["redt"][:, 1:2], sb["redt"][:, 1:2], INV_H))
        em.evsync("vector", lambda E: E.tensor_mul(sb["redt"][:, 0:1], sb["redt"][:, 2:3], sb["redt"][:, 2:3]))
        em.evsync("vector", lambda E: E.tensor_sub(sb["redt"][:, 1:2], sb["redt"][:, 1:2], sb["redt"][:, 0:1]),
              inc=("vp", 1))
        vvar = C["vp"]
        em.evsync("scalar", lambda E: E.activation(out=sb["redt"][:, 3:4], in_=sb["redt"][:, 1:2],
                                                   func=AF.Ln, bias=LN_EPS), waits=[("vp", vvar)])
        em.evsync("scalar", lambda E: E.activation(out=sb["rstd"][:, :], in_=sb["redt"][:, 3:4],
                                                   func=AF.Exp, scale=-0.5), inc=("ap", 1))
        aprs = C["ap"]
        em.evsync("vector", lambda E: E.tensor_scalar(
            out=sb["lnb"][:, :], in0=sb["redt"][:, 2:3], scalar1=sb["rstd"][:, :],
            scalar2=-1.0, op0=ALU.mult, op1=ALU.mult),
            waits=[("ap", aprs)], inc=("vp", 1))
        vlnb = C["vp"]
        aw = [("vp", vlnb)]
        if tt - 1 in sout_n:
            aw.append(("sout", sout_n[tt - 1]))
        em.ev("scalar", lambda E: E.activation(
            out=sb["outb"][:, :], in_=sb["r3tt"][:, :], func=AF.Identity,
            scale=sb["rstd"][:, :], bias=sb["lnb"][:, :]), waits=aw, inc=("ap", 1))
        apout = C["ap"]
        apout_n[tt] = apout
        em.ev("scalar", lambda E, tt=tt: E.dma_start(
            out=T["out"][128 * tt:128 * (tt + 1), :], in_=sb["outb"][:, :]),
            waits=[("ap", apout)], inc=("sout", 16))
        sout_n[tt] = C["sout"]
    em.ev("sync", None, waits=[("sout", C["sout"])])
    for nm, dst in T.get("dbg", {}).items():
        if nm == "at3":
            em.ev("sync", lambda E, dst=dst: E.dma_start(out=dst, in_=sb["at3_1"][:, :]),
                  inc=("dr3", 16))
            continue
        if nm == "wd3c":
            em.ev("sync", lambda E, dst=dst: E.dma_start(out=dst, in_=sb["wd3_1"][:, :]),
                  inc=("dr3", 16))
            continue
        if nm in ("kT", "v", "attnT", "lnst", "x2e"):
            continue
        srcteno = {"gu_in": T["gu_in"], "qT": T["qT_priv"], "pp": T["partial_priv"],
                   "wdT": T["wdT_priv"],
                   "rs": T["rs_out"], "x2": T["x2_priv"], "hT": T["hT_priv"],
                   "r3": T["r3_priv"], "dn_in": T["dn_in"], "atp": T["attnT_priv"]}[nm]
        em.ev("sync", lambda E, dst=dst, srcteno=srcteno: E.dma_start(out=dst, in_=srcteno),
              inc=("dr3", 16))
    if T.get("dbg"):
        em.ev("sync", None, waits=[("dr3", C["dr3"])])


def build_nc(debug=()):
    nc = bass.Bass("TRN2", target_bir_lowering=False)
    T = {}
    T["in_x"] = nc.dram_tensor("x_my", [TPC, H], F32, kind="ExternalInput").ap()
    T["in_xseq"] = nc.dram_tensor("x_seq", [TPC, H], F32, kind="ExternalInput").ap()
    T["in_wqkv"] = nc.dram_tensor("wqkv_my", [WR, H], F32, kind="ExternalInput").ap()
    T["in_wdense"] = nc.dram_tensor("wdense_my", [H, 1024], F32, kind="ExternalInput").ap()
    T["in_qwgu"] = nc.dram_tensor("qwgu_my", [512, GU_FS], U32, kind="ExternalInput").ap()
    T["in_scgu"] = nc.dram_tensor("scgu_my", [32, GU_FS], F32, kind="ExternalInput").ap()
    T["in_qwdn"] = nc.dram_tensor("qwdn_my", [16 * DN_NSLOT, H], U32, kind="ExternalInput").ap()
    T["in_scdn"] = nc.dram_tensor("scdn_full", [86, H], F32, kind="ExternalInput").ap()
    T["in_qzgu"] = nc.dram_tensor("qzgu_full", [32, 2752], U32, kind="ExternalInput").ap()
    T["in_scgu_full"] = nc.dram_tensor("scgu_full", [32, F2], F32, kind="ExternalInput").ap()
    T["in_qzdn"] = nc.dram_tensor("qzdn_full", [86, 512], U32, kind="ExternalInput").ap()
    T["in_cf32"] = nc.dram_tensor("c_f32", [128, 16], F32, kind="ExternalInput").ap()
    T["in_cshift"] = nc.dram_tensor("c_shift", [128, 1], U32, kind="ExternalInput").ap()
    T["in_cid"] = nc.dram_tensor("c_id", [128, 128], BF16, kind="ExternalInput").ap()
    T["in_cones_col"] = nc.dram_tensor("c_ones_col", [128, 1], BF16, kind="ExternalInput").ap()
    T["in_cones_row"] = nc.dram_tensor("c_ones_row", [1, 128], BF16, kind="ExternalInput").ap()
    T["in_cmasks"] = nc.dram_tensor("c_masks", [128, 2048], BF16, kind="ExternalInput").ap()
    T["in_ciota32"] = nc.dram_tensor("c_iota32", [128, 32], F32, kind="ExternalInput").ap()
    T["in_ciota86"] = nc.dram_tensor("c_iota86", [128, 86], F32, kind="ExternalInput").ap()
    T["in_gidx_gu"] = nc.dram_tensor("gidx_gu", [128, 32], I32, kind="ExternalInput").ap()
    T["in_gidx_dn"] = nc.dram_tensor("gidx_dn", [128, 86], I32, kind="ExternalInput").ap()
    T["in_gidx_dnmy"] = nc.dram_tensor("gidx_dnmy", [128, DN_NSLOT], I32, kind="ExternalInput").ap()
    T["out"] = nc.dram_tensor("out", [TPC, H], F32, kind="ExternalOutput").ap()
    T["xbf_in"] = nc.dram_tensor("xbf_in", [TPC, H], BF16, kind="Internal").ap()
    T["xbf_all"] = nc.dram_tensor("xbf_all", [TB, H], BF16, kind="Internal").ap()
    T["wqT_priv"] = nc.dram_tensor("wqT_priv", [12 * 128, 4096], BF16, kind="Internal").ap()
    T["wdT_priv"] = nc.dram_tensor("wdT_priv", [8 * 128, 4096], BF16, kind="Internal").ap()
    T["qT_priv"] = nc.dram_tensor("qT_priv", [1024, TB], BF16, kind="Internal").ap()
    T["partial_priv"] = nc.dram_tensor("partial_priv", [TB, H], BF16, kind="Internal").ap()
    T["rs_out"] = nc.dram_tensor("rs_out", [TPC, H], BF16, kind="Internal").ap()
    T["attnT_priv"] = nc.dram_tensor("attnT_priv", [1024, TB], BF16, kind="Internal").ap()
    T["gu_in"] = nc.dram_tensor("gu_in", [GU_NSLOT * 128, 4096], BF16, kind="Internal").ap()
    T["gu_all"] = nc.dram_tensor("gu_all", [8 * GU_NSLOT * 128, 4096], BF16, kind="Internal").ap()
    T["dn_in"] = nc.dram_tensor("dn_in", [DN_NSLOT * 128, 4096], BF16, kind="Internal").ap()
    T["dn_all"] = nc.dram_tensor("dn_all", [8 * DN_NSLOT * 128, 4096], BF16, kind="Internal").ap()
    T["x2_priv"] = nc.dram_tensor("x2_priv", [TPC, H], BF16, kind="Internal").ap()
    T["hT_priv"] = nc.dram_tensor("hT_priv", [8 * 128, INTER], BF16, kind="Internal").ap()
    T["r3_priv"] = nc.dram_tensor("r3_priv", [TPC, H], BF16, kind="Internal").ap()
    _dbg_shapes = {"gu_in": ([GU_NSLOT * 128, 4096], BF16), "qT": ([1024, TB], BF16),
                   "pp": ([TB, H], BF16), "rs": ([TPC, H], BF16), "x2": ([TPC, H], BF16),
                   "hT": ([8 * 128, INTER], BF16), "r3": ([TPC, H], BF16),
                   "xTin": ([128, 32768], BF16), "dn_in": ([DN_NSLOT * 128, 4096], BF16),
                   "kT": ([128, 8192], BF16), "v": ([128, 8192], BF16),
                   "atp": ([1024, TB], BF16), "at3": ([128, 8192], BF16),
                   "wd3c": ([128, 4096], BF16),
                   "attnT": ([128, 32768], BF16), "wdT": ([8 * 128, 4096], BF16),
                   "lnst": ([128, 134], F32), "x2e": ([TPC, H], BF16)}
    T["dbg"] = {nm: nc.dram_tensor("dbg_" + nm, *_dbg_shapes[nm], kind="ExternalOutput").ap()
                for nm in debug}

    sbt = {}
    for name, (off, shape, dt) in SB.items():
        sbt[name] = nc.alloc_sbuf_tensor_at(name, shape, dt, offset=SBUF_BASE + off)
    T["sb"] = sbt
    pst = {}
    psum_guards = []
    for i in range(6):
        g = nc.psum_tensor(f"F{i}", [128, 512], F32)
        pst[f"F{i}"] = g.__enter__()
        psum_guards.append(g)
    for i in range(2):
        g = nc.psum_tensor(f"T{i}", [128, 1024], BF16)
        pst[f"T{i}"] = g.__enter__()
        psum_guards.append(g)
    T["ps"] = pst

    from contextlib import ExitStack
    stack = ExitStack()
    sems = {}
    for s in SEM_NAMES:
        sems[s] = stack.enter_context(nc.semaphore(s))
    blk = stack.enter_context(nc.Block())

    for ci, v in enumerate([EXP_BIAS, EXP_SCALE, LN_EPS, -0.5, INV_H, -1.0, 1.0, 0.0]):
        nc.const_aps.register(F32, float(v), sbt["c_f32"][:, ci:ci + 1])

    def mk(eng_name):
        def body(E):
            em = Emitter(eng_name, E, T, sems)
            build_schedule(em)
        return body

    blk.sync(mk("sync"))
    blk.scalar(mk("scalar"))
    blk.vector(mk("vector"))
    blk.tensor(mk("tensor"))
    blk.gpsimd(mk("gpsimd"))
    stack.close()
    return nc


def _make_in_maps(inputs):
    x = np.ascontiguousarray(np.asarray(inputs["L0_attn_q"], dtype=np.float32)).reshape(B * S, H)
    wqkv = np.asarray(inputs["L0_attn_w_qkv"], dtype=np.float32)
    wdense = np.asarray(inputs["L0_attn_w_dense"], dtype=np.float32)
    qwgu = np.asarray(inputs["L0_ffn_qweight_gu"], dtype=np.uint32)
    qzgu = np.asarray(inputs["L0_ffn_qzeros_gu"], dtype=np.uint32)
    scgu = np.asarray(inputs["L0_ffn_scales_gu"], dtype=np.float32)
    gigu = np.asarray(inputs["L0_ffn_g_idx_gu"], dtype=np.int32)
    qwdn = np.asarray(inputs["L0_ffn_qweight_down"], dtype=np.uint32)
    qzdn = np.asarray(inputs["L0_ffn_qzeros_down"], dtype=np.uint32)
    scdn = np.asarray(inputs["L0_ffn_scales_down"], dtype=np.float32)
    gidn = np.asarray(inputs["L0_ffn_g_idx_down"], dtype=np.int32)

    cf32 = np.tile(np.array([[EXP_BIAS, EXP_SCALE, LN_EPS, -0.5, INV_H, -1.0, 1.0, 0.0,
                              0, 0, 0, 0, 0, 0, 0, 0]], np.float32), (128, 1))
    cshift = (4 * (np.arange(128) % 8)).astype(np.uint32).reshape(128, 1)
    cid = np.eye(128).astype(bfloat16)
    cones_col = np.ones((128, 1), dtype=bfloat16)
    cones_row = np.ones((1, 128), dtype=bfloat16)
    masks = np.zeros((128, 2048), dtype=bfloat16)
    p = np.arange(128)[:, None]
    j = np.arange(512)[None, :]
    for d in range(4):
        masks[:, 512 * d:512 * (d + 1)] = (j >= p + 128 * d).astype(bfloat16)
    ciota32 = np.tile(np.arange(32, dtype=np.float32)[None, :], (128, 1))
    ciota86 = np.tile(np.arange(86, dtype=np.float32)[None, :], (128, 1))
    gidx_gu_in = np.ascontiguousarray(gigu.reshape(32, 128).T).astype(np.int32)
    gidx_dn_in = np.ascontiguousarray(gidn.reshape(86, 128).T).astype(np.int32)

    in_maps = []
    for c in range(8):
        kh0 = 2 * (c % 4)
        q0 = 512 * kh0
        k0 = O_Q + 128 * kh0
        v0 = O_Q + O_K + 128 * kh0
        wq_my = np.ascontiguousarray(np.concatenate(
            [wqkv[q0:q0 + 1024], wqkv[k0:k0 + 256], wqkv[v0:v0 + 256]], axis=0))
        wd_my = np.ascontiguousarray(wdense[:, q0:q0 + 1024])
        blocks = list(range(GU_STARTS[c], GU_STARTS[c] + GU_COUNTS[c]))
        while len(blocks) < GU_NSLOT:
            blocks.append(blocks[0])
        qwgu_my = np.ascontiguousarray(np.concatenate(
            [qwgu[:, 128 * g:128 * (g + 1)] for g in blocks], axis=1))
        scgu_my = np.ascontiguousarray(np.concatenate(
            [scgu[:, 128 * g:128 * (g + 1)] for g in blocks], axis=1))
        dslots = list(range(DN_STARTS[c], DN_STARTS[c] + DN_COUNTS[c]))
        while len(dslots) < DN_NSLOT:
            dslots.append(dslots[0])
        qwdn_my = np.ascontiguousarray(np.concatenate(
            [qwdn[16 * g:16 * (g + 1), :] for g in dslots], axis=0))
        gidx_dnmy = np.ascontiguousarray(gidx_dn_in[:, dslots])
        tok = np.concatenate([np.arange(4096 * (c // 4) + 1024 * g + 256 * (c % 4),
                                        4096 * (c // 4) + 1024 * g + 256 * (c % 4) + 256)
                              for g in range(4)])
        in_maps.append({
            "x_my": np.ascontiguousarray(x[tok]),
            "x_seq": np.ascontiguousarray(x[TPC * c:TPC * (c + 1)]),
            "wqkv_my": wq_my,
            "wdense_my": wd_my,
            "qwgu_my": qwgu_my,
            "scgu_my": scgu_my,
            "qwdn_my": qwdn_my,
            "scdn_full": np.ascontiguousarray(scdn),
            "qzgu_full": np.ascontiguousarray(qzgu),
            "scgu_full": np.ascontiguousarray(scgu),
            "qzdn_full": np.ascontiguousarray(qzdn),
            "c_f32": cf32, "c_shift": cshift, "c_id": cid,
            "c_ones_col": cones_col, "c_ones_row": cones_row, "c_masks": masks,
            "c_iota32": ciota32, "c_iota86": ciota86,
            "gidx_gu": gidx_gu_in, "gidx_dn": gidx_dn_in, "gidx_dnmy": gidx_dnmy,
        })
    return in_maps


def kernel(**inputs):
    in_maps = _make_in_maps(inputs)
    nc = build_nc()
    res = run_bass_kernel_spmd(nc, in_maps, core_ids=list(range(8)))
    out = np.empty((B * S, H), np.float32)
    for c in range(8):
        tok = np.concatenate([np.arange(4096 * (c // 4) + 1024 * g + 256 * (c % 4),
                                        4096 * (c // 4) + 1024 * g + 256 * (c % 4) + 256)
                              for g in range(4)])
        out[tok] = np.asarray(res.results[c]["out"])
    return np.ascontiguousarray(out.reshape(B, S, H))
